# version 10
# baseline (speedup 1.0000x reference)
import contextlib
import numpy as np
from concourse import library_config
import concourse.bass as bass
import concourse.mybir as mybir
from bass_rust import ReduceOp
from concourse.bass_utils import run_bass_kernel_spmd

F32 = mybir.dt.float32
BF16 = mybir.dt.bfloat16
I16 = mybir.dt.int16
U16 = mybir.dt.uint16
ALU = mybir.AluOpType
AF = mybir.ActivationFunctionType
AX = mybir.AxisListType

RC = 12582912.0
QMAX = 32767.0
NCORE = 8
D = 1024
DCH = 8
H = 16
DH = 64
I3 = 3072
ICH = 24
EPS1 = 1e-5
EPS2 = 1e-3
EPSQ = 1e-6


class Sem:
    def __init__(self, nc, name):
        self.h = nc.alloc_semaphore(name)
        self.v = 0


class Builder:
    def __init__(self, T=2048, TT=512, debug=False):
        assert T % TT == 0 and TT % 128 == 0
        self.T = T
        self.TT = TT
        self.NT = T // TT
        self.TSUB = TT // 128
        self.TCH = T // 128
        self.debug = debug
        self.nc = bass.Bass("TRN2", target_bir_lowering=False)
        self.sems = {}
        self.es = {}

    def sem(self, name):
        if name not in self.sems:
            self.sems[name] = Sem(self.nc, f"s_{name}")
        return self.sems[name]

    def scope(self, name):
        self.es[name] = contextlib.ExitStack()

    def salloc(self, scope, name, shape, dt=F32):
        return self.es[scope].enter_context(self.nc.sbuf_tensor(name, shape, dt))

    def close_scope(self, name):
        self.es[name].close()

    def bank(self, i, w=None):
        w = w if w is not None else self.TT
        return self.psP[i // 2][:, (i % 2) * 512:(i % 2) * 512 + w]

    def wait_bank(self, e, i):
        e.wait_ge(self.s_bk[i].h, self.bk_uses[i])
        self.bk_uses[i] += 1

    def free_bank(self, e, i):
        e.sem_inc(self.s_bk[i].h, 1)
        self.s_bk[i].v += 1

    @staticmethod
    def fo_groups(gsz):
        gs, fo = [], 0
        while fo < DCH:
            gs.append(list(range(fo, min(fo + gsz, DCH))))
            fo += gsz
        return gs

    def declare_io(self):
        nc, T = self.nc, self.T
        g = lambda n, s, dt=F32, kind="ExternalInput": nc.dram_tensor(n, s, dt, kind=kind)
        self.x_in = g("x", [T, D])
        for w in ["Wq", "Wk", "Wv", "Wq2", "Wk2", "Wv2", "Wo"]:
            setattr(self, w, g(w, [D, D]))
        for b in ["bq", "bk", "bv", "bq2", "bk2", "bv2", "bo", "gamma", "beta"]:
            setattr(self, b, g(b, [D]))
        self.w_gate = g("w_gate", [I3, D])
        self.w_up = g("w_up", [I3, D])
        self.w_down = g("w_down", [D, I3])
        self.ident_in = g("ident", [128, 128])
        self.selE_in = g("selE", [DCH, H, 128])
        self.out = g("out", [T, D], F32, kind="ExternalOutput")

        dt_ = lambda n, s, dt=F32: nc.dram_tensor(n, s, dt)
        self.h_u_up = dt_("h_u_up", [ICH, 128, T])
        self.h_u_gate = dt_("h_u_gate", [ICH, 128, T])
        self.h_uq = dt_("h_uq", [ICH, 128, T], I16)
        self.h_p = dt_("h_p", [ICH, 128, T])
        self.h_po = dt_("h_po", [ICH, 128, T])
        self.h_s = dt_("h_s", [self.TCH, 128, D])
        self.h_qeh = dt_("h_qeh", [self.NT, 128, self.TSUB * D], BF16)
        self.h_qel = dt_("h_qel", [self.NT, 128, self.TSUB * D], BF16)
        self.h_xnh = dt_("h_xnh", [DCH, 128, T], BF16)
        self.h_xnl = dt_("h_xnl", [DCH, 128, T], BF16)
        self.h_vh = dt_("h_vh", [DCH, 128, T], BF16)
        self.h_vl = dt_("h_vl", [DCH, 128, T], BF16)
        for w in ["wq", "wk", "wv", "wq2", "wk2", "wv2", "wo"]:
            setattr(self, f"h_{w}h", dt_(f"h_{w}h", [DCH, 128, D], BF16))
            setattr(self, f"h_{w}l", dt_(f"h_{w}l", [DCH, 128, D], BF16))
        self.h_wuph = dt_("h_wuph", [I3, D], BF16)
        self.h_wupl = dt_("h_wupl", [I3, D], BF16)
        self.h_wgh = dt_("h_wgh", [I3, D], BF16)
        self.h_wgl = dt_("h_wgl", [I3, D], BF16)
        self.h_wdnh = dt_("h_wdnh", [D, I3], BF16)
        self.h_wdnl = dt_("h_wdnl", [D, I3], BF16)
        self.cc_in = dt_("cc_in", [1, 8])
        self.cc_out = dt_("cc_out", [1, 8])
        self.cc_Sin = dt_("cc_Sin", [DH, H * DH])
        self.cc_Sout = dt_("cc_Sout", [DH, H * DH])

        if self.debug:
            dbg = lambda n, s, dt=F32: nc.dram_tensor(n, s, dt, kind="ExternalOutput")
            self.d_xnh = dbg("d_xnh", [128, DCH * T], BF16)
            self.d_xnl = dbg("d_xnl", [128, DCH * T], BF16)
            self.d_scal = dbg("d_scal", [128, 8])
            self.d_S = dbg("d_S", [DH, H * DH])
            self.d_rq = dbg("d_rq", [H, T])
            self.d_s = dbg("d_s", [self.TCH, 128, D])
            self.d_u = dbg("d_u", [ICH, 128, T])
            self.d_uq = dbg("d_uq", [ICH, 128, T], I16)
            self.d_p = dbg("d_p", [ICH, 128, T])
            self.d_po = dbg("d_po", [ICH, 128, T])
            self.d_vh = dbg("d_vh", [128, DCH * T], BF16)
            self.d_vl = dbg("d_vl", [128, DCH * T], BF16)

    def alloc_static(self):
        nc, T = self.nc, self.T
        S = lambda n, s, dt=F32: nc.alloc_sbuf_tensor(n, s, dt)
        self.ident = S("c_ident", [128, 128])
        self.selE = S("c_selE", [H, DCH * 128])
        self.biases = {}
        for b in ["bq", "bk", "bv", "bq2", "bk2", "bv2", "bo"]:
            self.biases[b] = S(f"c_{b}", [128, DCH])
        self.scal = {n: S(f"sc_{n}", [128, 1]) for n in
                     ["m_up", "m_gate", "m_p", "m_po",
                      "sx", "inv_sx", "su", "inv_su", "sg", "inv_sg",
                      "sg2", "inv_sg2", "so", "inv_so", "c3", "tmp0"]}
        self.sevens = S("c_sevens", [128, 1])
        self.c_eps1 = S("c_eps1", [128, 1])
        self.c_eps2 = S("c_eps2", [128, 1])
        self.ones_row = S("c_ones_row", [1, 128])
        self.rq = S("rq", [H, T])
        self.qks_tm = S("qks_tm", [128, self.TCH * H])
        self.S_sb = S("S_sb", [DH, H * DH])
        self.Sp_f = S("Sp_f", [128, DCH * DH])
        self.SpH = S("SpH", [128, DCH * DH], BF16)
        self.SpL = S("SpL", [128, DCH * DH], BF16)
        self.amax_a = S("amax_a", [128, 1])
        self.amax_b = S("amax_b", [128, 1])
        self.ar_stage = S("ar_stage", [128, 8])
        self.psP = [nc.alloc_psum_tensor(f"psP{i}", [128, 1024], F32) for i in range(4)]
        self.s_bk = [self.sem(f"bk{i}") for i in range(8)]
        self.bk_uses = [0] * 8

    def build(self):
        self.declare_io()
        self.alloc_static()
        nc = self.nc
        with nc.Block() as blk:
            self.blk = blk
            self.ph_consts()
            self.scope("e0")
            self.scope("ln1")
            self.ph_ln1()
            self.ph_sx_scales()
            self.close_scope("ln1")
            self.ph_xi()
            self.ph_prep_and_sweep1()
            self.close_scope("e0")
            self.scope("att")
            self.ph_attn()
            self.close_scope("att")
            self.scope("a68")
            self.ph_A6A8()
            self.close_scope("a68")
            self.scope("s23")
            self.ph_sweep2()
            self.ph_sweep3_and_wdprep()
            self.close_scope("s23")
            self.scope("s4")
            self.ph_sweep4()
            self.close_scope("s4")
            self.ph_final_wait()
        return nc

    def ph_consts(self):
        blk = self.blk
        s_c = self.sem("consts")
        s_g = self.sem("ar_g")

        def _sync(e):
            e.dma_start(out=self.ident[:], in_=self.ident_in[:, :]).then_inc(s_c.h, 16)
            e.dma_start(out=self.selE[:].rearrange("h (c p) -> h c p", c=DCH),
                        in_=self.selE_in[:, :, :].transpose([1, 0, 2])
                        ).then_inc(s_c.h, 16)
            for b in ["bq", "bk", "bv", "bq2", "bk2", "bv2", "bo"]:
                src = getattr(self, b)[:].rearrange("(c p) -> p c", p=128)
                e.dma_start(out=self.biases[b][:], in_=src,
                            allow_slow_non_contiguous=True).then_inc(s_c.h, 16)
            s_c.v = 16 * 9
        blk.sync(_sync)

        def _g(e):
            e.memset(self.sevens[:], 7.0)
            e.memset(self.amax_a[:], 0.0)
            e.memset(self.c_eps1[:], EPS1)
            e.memset(self.c_eps2[:], EPS2)
            e.memset(self.ones_row[:], 1.0)
            e.drain()
            e.sem_inc(s_g.h, 1)
            s_g.v += 1
        blk.gpsimd(_g)

    def ph_ln1(self):
        nc, blk = self.nc, self.blk
        TCH = self.TCH
        s_c = self.sem("consts")
        s_g0 = self.sem("ar_g")
        s_ld = [self.sem(f"l1ld{i}") for i in range(2)]
        s_ldf = [self.sem(f"l1ldf{i}") for i in range(2)]
        s_xm = self.sem("l1xm")
        s_sq = self.sem("l1sq")
        s_xn = self.sem("l1xn")
        s_tr = self.sem("l1tr")
        s_hi = self.sem("l1hi")
        s_lo = self.sem("l1lo")
        s_xst = [self.sem("xstA"), self.sem("xstB")]

        A = lambda n, s, dt=F32: self.salloc("ln1", n, s, dt)
        xt = [A(f"l1xt{i}", [128, D]) for i in range(2)]
        xm = [A(f"l1xm{i}", [128, D]) for i in range(2)]
        xn_tm = [A(f"l1xn{i}", [128, D]) for i in range(2)]
        xnhS = [A(f"l1xnh{i}", [128, D], BF16) for i in range(2)]
        xnlS = [A(f"l1xnl{i}", [128, D], BF16) for i in range(2)]
        sq = A("l1sqb", [128, D])
        st1 = A("l1st1", [128, 1])
        st2 = [A(f"l1st2{i}", [128, 1]) for i in range(2)]
        red = A("l1red", [128, 1])

        def _sync(e):
            for i in range(TCH):
                sl = i % 2
                if i >= 2:
                    e.wait_ge(s_ldf[sl].h, i // 2)
                e.dma_start(out=xt[sl][:], in_=self.x_in[i * 128:(i + 1) * 128, :]
                            ).then_inc(s_ld[sl].h, 16)
                s_ld[sl].v += 16
        blk.sync(_sync)

        def _v(e):
            e.wait_ge(s_g0.h, 1)
            e.wait_ge(s_c.h, s_c.v)
            for i in range(TCH):
                sl = i % 2
                e.wait_ge(s_ld[sl].h, 16 * (i // 2 + 1))
                e.tensor_reduce(st1[:], xt[sl][:], axis=AX.X, op=ALU.add)
                e.drain()
                e.tensor_scalar(st1[:], st1[:], 1.0 / D, None, ALU.mult)
                e.drain()
                if i >= 2:
                    e.wait_ge(s_sq.h, i - 1)
                e.tensor_scalar(xm[sl][:], xt[sl][:], st1[:], None, ALU.subtract)
                e.drain()
                e.sem_inc(s_ldf[sl].h, 1)
                s_ldf[sl].v += 1
                e.sem_inc(s_xm.h, 1)
                s_xm.v += 1
                e.wait_ge(s_sq.h, i + 1)
                e.reciprocal(red[:], st2[sl][:])
                e.drain()
                if i >= 2:
                    e.wait_ge(s_tr.h, i - 1)
                e.tensor_scalar(xn_tm[sl][:], xm[sl][:], red[:], None, ALU.mult)
                e.drain()
                e.tensor_reduce(st1[:], xn_tm[sl][:], axis=AX.X, op=ALU.max, apply_absolute_value=True)
                e.drain()
                e.tensor_tensor(self.amax_a[:], self.amax_a[:], st1[:], ALU.max)
                e.drain()
                e.sem_inc(s_xn.h, 1)
                s_xn.v += 1
                e.wait_ge(s_hi.h, i + 1)
                bj = i % 2
                e.tensor_tensor(xnlS[bj][:], self.psP[bj][:, 0:D], xnhS[bj][:],
                                ALU.subtract)
                e.drain()
                self.free_bank(e, 2 * bj)
                self.free_bank(e, 2 * bj + 1)
                e.sem_inc(s_lo.h, 1)
                s_lo.v += 1
        blk.vector(_v)

        def _a(e):
            e.wait_ge(s_g0.h, 1)
            for i in range(TCH):
                sl = i % 2
                e.wait_ge(s_xm.h, i + 1)
                if i >= 2:
                    e.wait_ge(s_xn.h, i - 1)
                e.activation(sq[:], xm[sl][:], AF.Square, accum_out=st2[sl][:])
                e.drain()
                e.activation(st2[sl][:], st2[sl][:], AF.Sqrt, bias=self.c_eps1[:], scale=1.0 / D)
                e.drain()
                e.sem_inc(s_sq.h, 1)
                s_sq.v += 1
                e.wait_ge(s_tr.h, i + 1)
                bj = i % 2
                if i >= 2:
                    e.wait_ge(s_xst[bj].h, 32 * (i // 2))
                e.activation(xnhS[bj][:], self.psP[bj][:, 0:D], AF.Copy)
                e.drain()
                e.sem_inc(s_hi.h, 1)
                s_hi.v += 1
                e.wait_ge(s_lo.h, i + 1)
                dsth = self.h_xnh[:, :, i * 128:(i + 1) * 128].transpose([1, 0, 2])
                dstl = self.h_xnl[:, :, i * 128:(i + 1) * 128].transpose([1, 0, 2])
                e.dma_start(out=dsth, in_=xnhS[bj][:].rearrange("p (c t) -> p c t", c=DCH)
                            ).then_inc(s_xst[bj].h, 16)
                e.dma_start(out=dstl, in_=xnlS[bj][:].rearrange("p (c t) -> p c t", c=DCH)
                            ).then_inc(s_xst[bj].h, 16)
                s_xst[bj].v += 32
        blk.scalar(_a)

        def _t(e):
            e.wait_ge(s_c.h, s_c.v)
            for i in range(TCH):
                sl = i % 2
                bj = i % 2
                e.wait_ge(s_xn.h, i + 1)
                self.wait_bank(e, 2 * bj)
                self.wait_bank(e, 2 * bj + 1)
                for c in range(DCH):
                    instr = e.transpose(self.psP[bj][:, c * 128:(c + 1) * 128],
                                        xn_tm[sl][:, c * 128:(c + 1) * 128], self.ident[:])
                instr.then_inc(s_tr.h, 1)
                s_tr.v += 1
        blk.tensor(_t)

    def ar_scalar(self, name, srcs):
        blk = self.blk
        n_src = len(srcs)
        s_d = self.sem("ar_d")
        s_g = self.sem("ar_g")
        s_v = self.sem("l1xm")
        s_t = self.sem("l1tr")
        s_done = self.sem("ar_done")

        def _v0(e):
            e.drain()
            e.sem_inc(s_v.h, 1)
            s_v.v += 1
        blk.vector(_v0)

        def _t0(e):
            e.wait_ge(s_v.h, s_v.v)
            self.wait_bank(e, 4)
            for j, srcap in enumerate(srcs):
                instr = e.transpose(self.psP[2][0:1, j * 128:(j + 1) * 128],
                                    srcap, self.ident[:])
            instr.then_inc(s_t.h, 1)
            s_t.v += 1
        blk.tensor(_t0)

        def _v1(e):
            e.wait_ge(s_t.h, s_t.v)
            for j in range(n_src):
                e.tensor_reduce(self.ar_stage[0:1, j:j + 1],
                                self.psP[2][0:1, j * 128:(j + 1) * 128],
                                axis=AX.X, op=ALU.max)
            e.drain()
            self.free_bank(e, 4)
            e.sem_inc(s_v.h, 1)
            s_v.v += 1
        blk.vector(_v1)

        def _sy(e):
            e.wait_ge(s_v.h, s_v.v)
            e.dma_start(out=self.cc_in[0:1, 0:n_src],
                        in_=self.ar_stage[0:1, 0:n_src]).then_inc(s_d.h, 16)
            s_d.v += 16
        blk.sync(_sy)

        def _g2(e):
            e.wait_ge(s_d.h, s_d.v)
            cc = e.collective_compute(
                "AllReduce", ALU.max,
                replica_groups=[list(range(NCORE))],
                ins=[self.cc_in[0:1, 0:n_src].opt()],
                outs=[self.cc_out[0:1, 0:n_src].opt()])
            cc.then_inc(s_g.h, 1)
            s_g.v += 1
        blk.gpsimd(_g2)

        def _sy2(e):
            e.wait_ge(s_g.h, s_g.v)
            e.dma_start(out=self.ar_stage[0:1, 0:n_src],
                        in_=self.cc_out[0:1, 0:n_src]).then_inc(s_d.h, 16)
            s_d.v += 16
        blk.sync(_sy2)

        def _t1(e):
            e.wait_ge(s_d.h, s_d.v)
            self.wait_bank(e, 4)
            mm = e.matmul(self.psP[2][:, 512:512 + n_src], self.ones_row[:],
                          self.ar_stage[0:1, 0:n_src], start=True, stop=True)
            mm.then_inc(s_t.h, 1)
            s_t.v += 1
        blk.tensor(_t1)

        def _v2(e):
            e.wait_ge(s_t.h, s_t.v)
            e.tensor_copy(self.ar_stage[:, 0:n_src], self.psP[2][:, 512:512 + n_src])
            e.drain()
            self.free_bank(e, 4)
            e.sem_inc(s_done.h, 1)
            s_done.v += 1
        blk.vector(_v2)
        return s_done

    def ph_sx_scales(self):
        blk = self.blk
        s_xn = self.sem("l1xn")

        s_done = self.ar_scalar("sxar", [self.amax_a[:]])
        s_sc = self.sem("scd")

        def _v(e):
            e.wait_ge(s_done.h, s_done.v)
            e.tensor_scalar(self.scal["sx"][:], self.ar_stage[:, 0:1], 1.0 / QMAX,
                            1e-12, ALU.mult, ALU.max)
            e.drain()
            e.reciprocal(self.scal["inv_sx"][:], self.scal["sx"][:])
            e.drain()
            e.sem_inc(s_sc.h, 1)
            s_sc.v += 1
        blk.vector(_v)

    def ph_xi(self):
        T, blk = self.T, self.blk
        s_sc = self.sem("scd")
        s_lo = self.sem("l1lo")
        s_xst = [self.sem("xstA"), self.sem("xstB")]
        s_xi = self.sem("l1xm")
        s_ld = [self.sem(f"l1ld{i}") for i in range(2)]
        s_ldf = [self.sem(f"l1ldf{i}") for i in range(2)]
        b_ld = [s.v for s in s_ld]
        b_ldf = [s.v for s in s_ldf]
        self.xiH = self.salloc("e0", "xiH", [128, DCH * T], BF16)
        self.xiL = self.salloc("e0", "xiL", [128, DCH * T], BF16)
        xnc_h = [self.salloc("e0", f"xinh{i}", [128, T], BF16) for i in range(2)]
        xnc_l = [self.salloc("e0", f"xinl{i}", [128, T], BF16) for i in range(2)]
        xi_f = self.salloc("e0", "xi_f", [128, T])

        def _sync(e):
            e.wait_ge(s_xst[0].h, s_xst[0].v)
            e.wait_ge(s_xst[1].h, s_xst[1].v)
            for c in range(DCH):
                sl = c % 2
                if c >= 2:
                    e.wait_ge(s_ldf[sl].h, b_ldf[sl] + c // 2)
                e.dma_start(out=xnc_h[sl][:], in_=self.h_xnh[c]).then_inc(s_ld[sl].h, 16)
                e.dma_start(out=xnc_l[sl][:], in_=self.h_xnl[c]).then_inc(s_ld[sl].h, 16)
                s_ld[sl].v += 32
        blk.sync(_sync)

        def _v(e):
            e.wait_ge(s_sc.h, 1)
            for c in range(DCH):
                sl = c % 2
                cs = slice(c * T, (c + 1) * T)
                e.wait_ge(s_ld[sl].h, b_ld[sl] + 32 * (c // 2 + 1))
                e.tensor_tensor(xi_f[:], xnc_h[sl][:], xnc_l[sl][:], ALU.add)
                e.drain()
                e.sem_inc(s_ldf[sl].h, 1)
                s_ldf[sl].v += 1
                e.tensor_scalar(xi_f[:], xi_f[:], self.scal["inv_sx"][:], RC,
                                ALU.mult, ALU.add)
                e.drain()
                e.tensor_scalar(xi_f[:], xi_f[:], RC, None, ALU.subtract)
                e.drain()
                e.tensor_copy(self.xiH[:, cs], xi_f[:])
                e.drain()
                e.tensor_tensor(self.xiL[:, cs], xi_f[:], self.xiH[:, cs], ALU.subtract)
                e.drain()
                e.sem_inc(s_xi.h, 1)
                s_xi.v += 1
        blk.vector(_v)

    def ph_prep_and_sweep1(self):
        nc, blk, T, TT, NT = self.nc, self.blk, self.T, self.TT, self.NT
        s_xi = self.sem("l1xm")
        s_lo = self.sem("l1lo")

        s_pld = [self.sem(f"ppld{i}") for i in range(2)]
        s_pldf = [self.sem(f"ppldf{i}") for i in range(2)]
        s_pg = self.sem("ppg")
        s_pst = [self.sem("ppst0"), self.sem("ppst1")]
        A = lambda n, s, dt=F32: self.salloc("e0", n, s, dt)
        win = [A(f"ppin{i}", [128, D]) for i in range(2)]
        bmax = A("ppbm", [128, 32])
        sblk = [A(f"pps{i}", [128, 32]) for i in range(2)]
        rblk = [A(f"ppr{i}", [128, 32]) for i in range(2)]
        tq = A("ppt", [128, D])
        wh = [A(f"pph{i}", [128, D], BF16) for i in range(2)]
        wl = [A(f"ppl{i}", [128, D], BF16) for i in range(2)]

        up_jobs = [(self.w_up[ic * 128:(ic + 1) * 128, :],
                    self.h_wuph[ic * 128:(ic + 1) * 128, :],
                    self.h_wupl[ic * 128:(ic + 1) * 128, :], True) for ic in range(ICH)]
        gate_jobs = [(self.w_gate[ic * 128:(ic + 1) * 128, :],
                      self.h_wgh[ic * 128:(ic + 1) * 128, :],
                      self.h_wgl[ic * 128:(ic + 1) * 128, :], True) for ic in range(ICH)]
        aw_jobs = []
        nbase = len(up_jobs) + len(gate_jobs)
        for wname, W in [("wq", self.Wq), ("wq2", self.Wq2), ("wk", self.Wk),
                         ("wk2", self.Wk2), ("wv", self.Wv), ("wv2", self.Wv2),
                         ("wo", self.Wo)]:
            hh = getattr(self, f"h_{wname}h")
            hl = getattr(self, f"h_{wname}l")
            for c in range(DCH):
                aw_jobs.append((W[c * 128:(c + 1) * 128, :], hh[c], hl[c], False))
        self.prep_counts = {"up": len(up_jobs), "gate": nbase,
                            "all": nbase + len(aw_jobs)}

        NQ = 8
        NOC = ICH // NQ
        seq = []
        q_need = {}
        aw_pos = {}

        def _push(jobs):
            p0 = len(seq)
            seq.extend(jobs)
            return list(range(p0, len(seq)))

        hjs = list(range(6, 2 * NQ))
        per = [len(aw_jobs) // len(hjs)] * len(hjs)
        for i in range(len(aw_jobs) - sum(per)):
            per[-1 - i] += 1
        q_need[0] = _push(up_jobs[0:NOC])
        q_need[1] = _push(up_jobs[NOC:2 * NOC])
        awi = 0
        for hj in range(2 * NQ):
            q = hj + 2
            if q < 2 * NQ:
                jl = up_jobs if q < NQ else gate_jobs
                qf = q if q < NQ else q - NQ
                q_need[q] = _push(jl[qf * NOC:(qf + 1) * NOC])
            if hj in hjs:
                n = per[hjs.index(hj)]
                aw_pos[hj] = list(range(awi, awi + n))
                awi += n
            else:
                aw_pos[hj] = []

        s_ald = [self.sem("wdld0"), self.sem("wdld1")]
        s_aldf = self.sem("wdldf0")
        s_ag = self.sem("wdg")
        s_ast = [self.sem("wdst0"), self.sem("wdst1")]
        awin = [A(f"awin{i}", [128, D]) for i in range(2)]
        awh = [A(f"awh{i}", [128, D], BF16) for i in range(2)]
        awl = [A(f"awl{i}", [128, D], BF16) for i in range(2)]

        def emit_aw_load(e, a):
            sl = a % 2
            if a >= 2:
                e.wait_ge(s_aldf.h, a - 1)
            e.dma_start(out=awin[sl][:], in_=aw_jobs[a][0]).then_inc(s_ald[sl].h, 16)
            s_ald[sl].v += 16

        def emit_aw_gp(e, a):
            sl = a % 2
            e.wait_ge(s_ald[sl].h, 16 * (a // 2 + 1))
            if a >= 2:
                e.wait_ge(s_ast[sl].h, 32 * (a // 2))
            e.tensor_copy(awh[sl][:], awin[sl][:])
            e.drain()
            e.tensor_tensor(awl[sl][:], awin[sl][:], awh[sl][:], ALU.subtract)
            e.drain()
            e.sem_inc(s_aldf.h, 1)
            s_aldf.v += 1
            e.sem_inc(s_ag.h, 1)
            s_ag.v += 1

        def emit_aw_store(e, a):
            sl = a % 2
            e.wait_ge(s_ag.h, a + 1)
            e.dma_start(out=aw_jobs[a][1], in_=awh[sl][:]).then_inc(s_ast[sl].h, 16)
            e.dma_start(out=aw_jobs[a][2], in_=awl[sl][:]).then_inc(s_ast[sl].h, 16)
            s_ast[sl].v += 32

        def emit_prep_sync_job(e, j):
            src, dh, dl, lp = seq[j]
            sl = j % 2
            if j >= 2:
                e.wait_ge(s_pldf[sl].h, j // 2)
            e.dma_start(out=win[sl][:], in_=src).then_inc(s_pld[sl].h, 16)
            s_pld[sl].v += 16
            e.wait_ge(s_pg.h, j + 1)
            e.dma_start(out=dh, in_=wh[sl][:]).then_inc(s_pst[sl].h, 16)
            e.dma_start(out=dl, in_=wl[sl][:]).then_inc(s_pst[sl].h, 16)
            s_pst[sl].v += 32

        def emit_prep_dve(e, j):
            src, dh, dl, lp = seq[j]
            sl = j % 2
            e.wait_ge(s_pld[sl].h, 16 * (j // 2 + 1))
            if j >= 2:
                e.wait_ge(s_pst[sl].h, 32 * (j // 2))
            if lp:
                e.tensor_reduce(bmax[:], win[sl][:].rearrange("p (a b) -> p a b", b=32),
                                axis=AX.X, op=ALU.max, apply_absolute_value=True)
                e.drain()
                e.tensor_scalar(sblk[sl][:], bmax[:], 1.0 / 7.0, 1e-12,
                                ALU.mult, ALU.max)
                e.drain()
                e.reciprocal(rblk[sl][:], sblk[sl][:])
                e.drain()
                e.tensor_tensor(tq[:], win[sl][:],
                                rblk[sl][:].rearrange("p (a b) -> p a b", b=1)
                                .broadcast_to([128, 32, 32]), ALU.mult)
                e.drain()
                e.sem_inc(s_pldf[sl].h, 1)
                s_pldf[sl].v += 1
                e.tensor_scalar(tq[:], tq[:], RC, RC, ALU.add, ALU.subtract)
                e.drain()
                e.tensor_tensor(tq[:], tq[:],
                                sblk[sl][:].rearrange("p (a b) -> p a b", b=1)
                                .broadcast_to([128, 32, 32]), ALU.mult)
                e.drain()
                src_dq = tq
            else:
                src_dq = win[sl]
            e.tensor_copy(wh[sl][:], src_dq[:])
            e.drain()
            e.tensor_tensor(wl[sl][:], src_dq[:], wh[sl][:], ALU.subtract)
            e.drain()
            if not lp:
                e.sem_inc(s_pldf[sl].h, 1)
                s_pldf[sl].v += 1
            e.sem_inc(s_pg.h, 1)
            s_pg.v += 1

        NWS = 3
        s_w1 = [self.sem(f"s1w{i}") for i in range(NWS)]
        s_w1f = [self.sem(f"s1wf{i}") for i in range(NWS)]
        s_mm = self.sem("s1mm")
        s_ev = self.sem("s1ev")
        s_st = [self.sem("s1st0"), self.sem("s1st1")]
        HOW = I3 // NQ
        wT_h = [A(f"s1th{i}", [128, DCH * HOW], BF16) for i in range(NWS)]
        wT_l = [A(f"s1tl{i}", [128, DCH * HOW], BF16) for i in range(NWS)]
        ub = [A(f"s1u{i}", [128, TT]) for i in range(2)]
        quarters = [(nm, qf) for nm in ["up", "gate"] for qf in range(NQ)]
        self.s1_total = len(quarters) * NOC * NT

        def emit_s1_wload(e, hj, kcs):
            nm, qf = quarters[hj]
            sl = hj % NWS
            m = max(q_need[hj])
            e.wait_ge(s_pst[0].h, 32 * (m // 2 + 1))
            e.wait_ge(s_pst[1].h, 32 * ((m + 1) // 2))
            if hj >= NWS:
                e.wait_ge(s_w1f[sl].h, hj // NWS)
            hsrc = self.h_wuph if nm == "up" else self.h_wgh
            lsrc = self.h_wupl if nm == "up" else self.h_wgl
            if hj == 0:
                e.wait_ge(s_lo.h, self.TCH)
            for kc in kcs:
                e.dma_start(
                    out=wT_h[sl][:, kc * HOW:(kc + 1) * HOW],
                    in_=hsrc[qf * HOW:(qf + 1) * HOW, kc * 128:(kc + 1) * 128],
                    transpose=True).then_inc(s_w1[sl].h, 16)
                e.dma_start(
                    out=wT_l[sl][:, kc * HOW:(kc + 1) * HOW],
                    in_=lsrc[qf * HOW:(qf + 1) * HOW, kc * 128:(kc + 1) * 128],
                    transpose=True).then_inc(s_w1[sl].h, 16)
                s_w1[sl].v += 32

        KC_SY = list(range(DCH))

        def _sync(e):
            for q in (0, 1):
                for j in q_need[q]:
                    emit_prep_sync_job(e, j)
            emit_s1_wload(e, 0, KC_SY)
            emit_s1_wload(e, 1, KC_SY)
            for hj in range(2 * NQ):
                if hj + 2 < 2 * NQ:
                    for j in q_need[hj + 2]:
                        emit_prep_sync_job(e, j)
                    emit_s1_wload(e, hj + 2, KC_SY)
                if hj >= 1:
                    for a in aw_pos[hj - 1]:
                        emit_aw_store(e, a)
            for a in aw_pos[2 * NQ - 1]:
                emit_aw_store(e, a)
        blk.sync(_sync)

        def _g(e):
            for hj in range(2 * NQ):
                for a in aw_pos[hj]:
                    emit_aw_gp(e, a)
        blk.gpsimd(_g)

        def emit_s1_evacs(e, hj0, hj1, nmm):
            for hj in range(hj0, hj1):
                nm, qf = quarters[hj]
                acc = self.scal["m_up"] if nm == "up" else self.scal["m_gate"]
                for oc in range(NOC):
                    for tt in range(NT):
                        ubuf = ub[nmm % 2]
                        bk = nmm % 2
                        par = nmm % 2
                        nmm += 1
                        e.wait_ge(s_mm.h, nmm)
                        if nmm > 2:
                            e.wait_ge(s_st[par].h, 16 * ((nmm - 1) // 2))
                        e.tensor_scalar(ubuf[:], self.bank(bk), self.scal["sx"][:],
                                        None, ALU.mult)
                        e.drain()
                        self.free_bank(e, bk)
                        e.tensor_reduce(self.amax_b[:], ubuf[:], axis=AX.X,
                                        op=ALU.max, apply_absolute_value=True)
                        e.drain()
                        if qf == 0 and oc == 0 and tt == 0:
                            e.tensor_copy(acc[:], self.amax_b[:])
                        else:
                            e.tensor_tensor(acc[:], acc[:], self.amax_b[:], ALU.max)
                        e.drain()
                        e.sem_inc(s_ev.h, 1)
                        s_ev.v += 1
                e.sem_inc(s_w1f[hj % NWS].h, 1)
                s_w1f[hj % NWS].v += 1
            return nmm

        def _v(e):
            for q in (0, 1):
                for j in q_need[q]:
                    emit_prep_dve(e, j)
            nmm = 0
            for hj in range(2 * NQ):
                if hj + 2 < 2 * NQ:
                    for j in q_need[hj + 2]:
                        emit_prep_dve(e, j)
                nmm = emit_s1_evacs(e, hj, hj + 1, nmm)
        blk.vector(_v)

        def _t(e):
            e.wait_ge(s_xi.h, s_xi.v)
            nmm = 0
            for hj, (nm, qf) in enumerate(quarters):
                sl = hj % NWS
                e.wait_ge(s_w1[sl].h, 32 * DCH * (hj // NWS + 1))
                for oc in range(NOC):
                    for tt in range(NT):
                        bk = nmm % 2
                        nmm += 1
                        self.wait_bank(e, bk)
                        first = True
                        for kc in range(DCH):
                            lw_h = wT_h[sl][:, kc * HOW + oc * 128: kc * HOW + (oc + 1) * 128]
                            lw_l = wT_l[sl][:, kc * HOW + oc * 128: kc * HOW + (oc + 1) * 128]
                            rx_h = self.xiH[:, kc * T + tt * TT: kc * T + (tt + 1) * TT]
                            rx_l = self.xiL[:, kc * T + tt * TT: kc * T + (tt + 1) * TT]
                            e.matmul(self.bank(bk), lw_h, rx_h, start=first, stop=False)
                            first = False
                            e.matmul(self.bank(bk), lw_h, rx_l, start=False, stop=False)
                            mm = e.matmul(self.bank(bk), lw_l, rx_h, start=False,
                                          stop=(kc == DCH - 1))
                        mm.then_inc(s_mm.h, 1)
                        s_mm.v += 1
        blk.tensor(_t)

        def _a(e):
            nmm = 0
            for hj, (nm, qf) in enumerate(quarters):
                dst = self.h_u_up if nm == "up" else self.h_u_gate
                for oc in range(NOC):
                    ic = qf * NOC + oc
                    for tt in range(NT):
                        nmm += 1
                        e.wait_ge(s_ev.h, nmm)
                        par = (nmm - 1) % 2
                        e.dma_start(out=dst[ic, :, tt * TT:(tt + 1) * TT],
                                    in_=ub[par][:]).then_inc(s_st[par].h, 16)
                        s_st[par].v += 16
                for a in aw_pos[hj]:
                    emit_aw_load(e, a)
        blk.scalar(_a)

        s_done = self.ar_scalar("ugar", [self.scal["m_up"][:], self.scal["m_gate"][:]])
        s_sc2 = self.sem("scd")

        def _v2(e):
            e.wait_ge(s_done.h, s_done.v)
            for j, nm in enumerate(["su", "sg"]):
                e.tensor_scalar(self.scal[nm][:], self.ar_stage[:, j:j + 1], 1.0 / QMAX,
                                1e-12, ALU.mult, ALU.max)
                e.drain()
                e.reciprocal(self.scal["inv_" + nm][:], self.scal[nm][:])
                e.drain()
            e.sem_inc(s_sc2.h, 1)
            s_sc2.v += 1
        blk.vector(_v2)

        if self.debug:
            s_dbg = self.sem("dbg")

            def _dbg(e):
                e.wait_ge(s_st[0].h, s_st[0].v)
                e.wait_ge(s_st[1].h, s_st[1].v)
                for ic in range(ICH):
                    e.dma_start(out=self.d_u[ic], in_=self.h_u_up[ic]
                                ).then_inc(s_dbg.h, 16)
                    s_dbg.v += 16
            blk.scalar(_dbg)

    def attn_iter(self):
        for pi, ph in enumerate(self.attnP):
            pname, gsz = ph[0], ph[7]
            for tt in range(self.NT):
                for st in (0, 1):
                    for grp in self.fo_groups(gsz):
                        yield (pi, pname, tt, st, grp)

    def ph_attn(self):
        nc, blk, T, TT, NT, TSUB = self.nc, self.blk, self.T, self.TT, self.NT, self.TSUB
        s_lo = self.sem("l1lo")
        s_pst = [self.sem("ppst0"), self.sem("ppst1")]
        s_s1st = [self.sem("s1st0"), self.sem("s1st1")]

        s_ws = [self.sem(f"ws{i}") for i in range(2)]
        s_wsf = [self.sem(f"wsf{i}") for i in range(2)]
        self.s_ws, self.s_wsf = s_ws, s_wsf
        self.ws_n_ld = 0
        self.ws_n_use = 0
        self.ws_tot = {}
        self.ws_sltot = [0, 0]
        A = lambda n, s, dt=F32: self.salloc("att", n, s, dt)
        self.wsH = [A(f"wsh{i}", [128, D], BF16) for i in range(2)]
        self.wsL = [A(f"wsl{i}", [128, D], BF16) for i in range(2)]
        self.xsH = [A(f"xsh{i}", [128, self.TT], BF16) for i in range(2)]
        self.xsL = [A(f"xsl{i}", [128, self.TT], BF16) for i in range(2)]

        t1H = A("att1h", [128, DCH * TT], BF16)
        t1L = A("att1l", [128, DCH * TT], BF16)
        qe_fm = A("atqefm", [128, DCH * TT])
        elu_a = A("atelua", [128, TT])
        elu_b = A("atelub", [128, TT])
        tmH = A("attmh", [128, TSUB * D], BF16)
        tmL = A("attml", [128, TSUB * D], BF16)
        qeH = A("atqeh2", [128, TSUB * D], BF16)
        qeL = A("atqel2", [128, TSUB * D], BF16)
        pk1 = A("atpk1", [128, D])
        pk2 = A("atpk2", [128, D])
        qksr = A("atqksr", [128, H])
        vstH = [A(f"atvsth{i}", [128, TT], BF16) for i in range(2)]
        vstL = [A(f"atvstl{i}", [128, TT], BF16) for i in range(2)]
        self.b_qest = [32 * ((self.NT + 1) // 2), 32 * (self.NT // 2)]

        s_mmg = self.sem("atmmg")
        s_ev1 = self.sem("atev1")
        s_qe = self.sem("atqe")
        s_exp = self.sem("atexp")
        s_elu_z = self.sem("atz")
        s_tr = self.sem("attr")
        s_hi = self.sem("athi")
        s_lo2 = self.sem("atlo")
        s_qest = [self.sem("atqestA"), self.sem("atqestB")]
        s_qeld = self.sem("atqeld")
        s_qefree = self.sem("atqefree")
        s_qks = self.sem("atqks")

        self.attnP = [
            ("q", self.h_wqh, self.h_wql, "bq", self.h_wq2h, self.h_wq2l, "bq2", 8),
            ("k", self.h_wkh, self.h_wkl, "bk", self.h_wk2h, self.h_wk2l, "bk2", 6),
            ("v", self.h_wvh, self.h_wvl, "bv", self.h_wv2h, self.h_wv2l, "bv2", 8)]
        psS = self.psP[3][0:DH, :]

        s_ast = [self.sem("wdst0"), self.sem("wdst1")]

        def _sync(e):
            e.wait_ge(s_pst[0].h, s_pst[0].v)
            e.wait_ge(s_pst[1].h, s_pst[1].v)
            e.wait_ge(s_ast[0].h, s_ast[0].v)
            e.wait_ge(s_ast[1].h, s_ast[1].v)
            e.wait_ge(s_s1st[0].h, 16 * ((self.s1_total + 1) // 2))
            e.wait_ge(s_s1st[1].h, 16 * (self.s1_total // 2))
            if self.debug:
                sdbg = self.sem("dbg")
                e.wait_ge(sdbg.h, sdbg.v)
            for (pi, pname, tt, st, grp) in self.attn_iter():
                ph = self.attnP[pi]
                hsrc = ph[1] if st == 0 else ph[4]
                lsrc = ph[2] if st == 0 else ph[5]
                if pname == "k" and st == 0 and grp[0] == 0:
                    e.wait_ge(s_qest[tt % 2].h, 32 * (tt // 2 + 1))
                    if tt >= 1:
                        e.wait_ge(s_qefree.h, 2 * tt)
                    e.dma_start(out=qeH[:], in_=self.h_qeh[tt]).then_inc(s_qeld.h, 16)
                    e.dma_start(out=qeL[:], in_=self.h_qel[tt]).then_inc(s_qeld.h, 16)
                    s_qeld.v += 32
                for kc in range(DCH):
                    n = self.ws_n_ld
                    self.ws_n_ld += 1
                    sl = n % 2
                    if n >= 2:
                        e.wait_ge(s_wsf[sl].h, n // 2)
                    e.dma_start(out=self.wsH[sl][:], in_=hsrc[kc]).then_inc(s_ws[sl].h, 16)
                    e.dma_start(out=self.wsL[sl][:], in_=lsrc[kc]).then_inc(s_ws[sl].h, 16)
                    self.ws_sltot[sl] += 32
                    if st == 0:
                        cs = slice(tt * self.TT, (tt + 1) * self.TT)
                        e.dma_start(out=self.xsH[sl][:], in_=self.h_xnh[kc][:, cs]
                                    ).then_inc(s_ws[sl].h, 16)
                        e.dma_start(out=self.xsL[sl][:], in_=self.h_xnl[kc][:, cs]
                                    ).then_inc(s_ws[sl].h, 16)
                        self.ws_sltot[sl] += 32
                    self.ws_tot[n] = self.ws_sltot[sl]
                    s_ws[sl].v = self.ws_sltot[sl]
        blk.sync(_sync)

        def _t(e):
            e.wait_ge(s_lo.h, self.TCH)

            def take(kc_last):
                n = self.ws_n_use
                self.ws_n_use += 1
                sl = n % 2
                e.wait_ge(s_ws[sl].h, self.ws_tot[n])
                return sl

            cur = [None]

            def mm_group(grp, rhsH, rhsL, col_of):
                for fi, fo in enumerate(grp):
                    self.wait_bank(e, fi)
                last_sl = None
                for kc in range(DCH):
                    sl = take(kc == DCH - 1)
                    last_sl = sl
                    for fi, fo in enumerate(grp):
                        lw_h = self.wsH[sl][:, fo * 128:(fo + 1) * 128]
                        lw_l = self.wsL[sl][:, fo * 128:(fo + 1) * 128]
                        if rhsH is None:
                            rx_h = self.xsH[sl][:]
                            rx_l = self.xsL[sl][:]
                        else:
                            rx_h = rhsH[:, col_of(kc)]
                            rx_l = rhsL[:, col_of(kc)]
                        e.matmul(self.bank(fi), lw_h, rx_h, start=(kc == 0), stop=False)
                        e.matmul(self.bank(fi), lw_h, rx_l, start=False, stop=False)
                        mm = e.matmul(self.bank(fi), lw_l, rx_h, start=False,
                                      stop=(kc == DCH - 1))
                    if kc < DCH - 1:
                        mm.then_inc(s_wsf[sl].h, 1)
                        s_wsf[sl].v += 1
                mm.then_inc(s_mmg.h, 1)
                s_mmg.v += 1
                e.drain()
                e.sem_inc(s_wsf[last_sl].h, 1)
                s_wsf[last_sl].v += 1

            for pi, ph in enumerate(self.attnP):
                pname, gsz = ph[0], ph[7]
                for tt in range(NT):
                    for grp in self.fo_groups(gsz):
                        mm_group(grp, None, None, None)
                    e.wait_ge(s_ev1.h, pi * NT + tt + 1)
                    for grp in self.fo_groups(gsz):
                        mm_group(grp, t1H, t1L,
                                 lambda kc: slice(kc * TT, (kc + 1) * TT))
                    if pname in ("q", "k"):
                        if pname == "k":
                            e.wait_ge(s_qks.h, tt)
                        e.wait_ge(s_qe.h, pi * NT + tt + 1)
                        for ts_ in range(TSUB):
                            bj = ts_ % 2
                            self.wait_bank(e, 2 * bj)
                            self.wait_bank(e, 2 * bj + 1)
                            for c in range(DCH):
                                instr = e.transpose(
                                    self.psP[bj][:, c * 128:(c + 1) * 128],
                                    qe_fm[:, c * TT + ts_ * 128: c * TT + (ts_ + 1) * 128],
                                    self.ident[:])
                            instr.then_inc(s_tr.h, 1)
                            s_tr.v += 1
                    if pname == "k":
                        if tt == 0:
                            self.wait_bank(e, 6)
                            self.wait_bank(e, 7)
                        e.wait_ge(s_lo2.h, (NT + tt) * TSUB + TSUB)
                        e.wait_ge(s_qeld.h, 32 * (tt + 1))
                        for ts_ in range(TSUB):
                            for h in range(H):
                                qh_ = qeH[:, ts_ * D + h * DH: ts_ * D + (h + 1) * DH]
                                ql_ = qeL[:, ts_ * D + h * DH: ts_ * D + (h + 1) * DH]
                                kh_ = tmH[:, ts_ * D + h * DH: ts_ * D + (h + 1) * DH]
                                kl_ = tmL[:, ts_ * D + h * DH: ts_ * D + (h + 1) * DH]
                                po = psS[:, h * DH:(h + 1) * DH]
                                firstS = (tt == 0 and ts_ == 0 and h % 8 == 0)
                                e.matmul(po, kh_, qh_, start=firstS, stop=False,
                                         skip_group_check=True)
                                e.matmul(po, kh_, ql_, start=False, stop=False,
                                         skip_group_check=True)
                                mm = e.matmul(po, kl_, qh_, start=False,
                                              stop=(tt == NT - 1 and ts_ == TSUB - 1),
                                              skip_group_check=True)
                        mm.then_inc(s_qefree.h, 1)
                        s_qefree.v += 1
            e.wait_ge(s_qks.h, NT + 1)
            per = 4
            self.n_rq = 0
            for base in range(0, self.TCH, per):
                self.wait_bank(e, 0)
                for j in range(per):
                    i = base + j
                    if i >= self.TCH:
                        break
                    instr = e.transpose(self.psP[0][0:H, j * 128:(j + 1) * 128],
                                        self.qks_tm[:, i * H:(i + 1) * H], self.ident[:])
                instr.then_inc(s_tr.h, 1)
                s_tr.v += 1
                self.n_rq += 1
        blk.tensor(_t)

        def _v(e):
            ngrp = 0
            for pi, ph in enumerate(self.attnP):
                pname, gsz = ph[0], ph[7]
                bias1 = self.biases[ph[3]]
                bias2 = self.biases[ph[6]]
                for tt in range(NT):
                    for grp in self.fo_groups(gsz):
                        ngrp += 1
                        e.wait_ge(s_mmg.h, ngrp)
                        for fi, fo in enumerate(grp):
                            dst_h = t1H[:, fo * TT:(fo + 1) * TT]
                            dst_l = t1L[:, fo * TT:(fo + 1) * TT]
                            e.tensor_scalar(dst_h, self.bank(fi), bias1[:, fo:fo + 1],
                                            None, ALU.add)
                            e.drain()
                            e.scalar_tensor_tensor(dst_l, self.bank(fi),
                                                   bias1[:, fo:fo + 1], dst_h,
                                                   ALU.add, ALU.subtract)
                            e.drain()
                            self.free_bank(e, fi)
                    e.sem_inc(s_ev1.h, 1)
                    s_ev1.v += 1
                    for grp in self.fo_groups(gsz):
                        ngrp += 1
                        e.wait_ge(s_mmg.h, ngrp)
                        for fi, fo in enumerate(grp):
                            if pname in ("q", "k"):
                                e.tensor_scalar(elu_a[:], self.bank(fi),
                                                bias2[:, fo:fo + 1], 0.0, ALU.add, ALU.min)
                                e.drain()
                                e.sem_inc(s_elu_z.h, 1)
                                s_elu_z.v += 1
                                e.tensor_scalar(elu_b[:], self.bank(fi),
                                                bias2[:, fo:fo + 1], 0.0, ALU.add, ALU.max)
                                e.drain()
                                self.free_bank(e, fi)
                                e.wait_ge(s_exp.h, s_elu_z.v)
                                e.tensor_tensor(qe_fm[:, fo * TT:(fo + 1) * TT],
                                                elu_b[:], elu_a[:], ALU.add)
                                e.drain()
                            else:
                                idx = tt * DCH + fo
                                par = idx % 2
                                if idx >= 2:
                                    e.wait_ge(s_qest[par].h,
                                              self.b_qest[par] + 32 * (idx // 2))
                                dst_h = vstH[par][:]
                                dst_l = vstL[par][:]
                                e.tensor_scalar(dst_h, self.bank(fi), bias2[:, fo:fo + 1],
                                                None, ALU.add)
                                e.drain()
                                e.scalar_tensor_tensor(dst_l, self.bank(fi),
                                                       bias2[:, fo:fo + 1], dst_h,
                                                       ALU.add, ALU.subtract)
                                e.drain()
                                self.free_bank(e, fi)
                                e.sem_inc(s_elu_z.h, 1)
                                s_elu_z.v += 1
                    e.sem_inc(s_qe.h, 1)
                    s_qe.v += 1
                    if pname in ("q", "k"):
                        for ts_ in range(TSUB):
                            bj = ts_ % 2
                            idx = (pi * NT + tt) * TSUB + ts_ + 1
                            e.wait_ge(s_hi.h, idx)
                            e.tensor_tensor(tmL[:, ts_ * D:(ts_ + 1) * D],
                                            self.psP[bj][:, 0:D],
                                            tmH[:, ts_ * D:(ts_ + 1) * D], ALU.subtract)
                            e.drain()
                            self.free_bank(e, 2 * bj)
                            self.free_bank(e, 2 * bj + 1)
                            e.sem_inc(s_lo2.h, 1)
                            s_lo2.v += 1
                    if pname == "k":
                        e.wait_ge(s_qeld.h, 32 * (tt + 1))
                        for ts_ in range(TSUB):
                            qh_ = qeH[:, ts_ * D:(ts_ + 1) * D]
                            ql_ = qeL[:, ts_ * D:(ts_ + 1) * D]
                            kh_ = tmH[:, ts_ * D:(ts_ + 1) * D]
                            kl_ = tmL[:, ts_ * D:(ts_ + 1) * D]
                            e.tensor_tensor(pk1[:], qh_, kh_, ALU.mult)
                            e.tensor_tensor(pk2[:], qh_, kl_, ALU.mult)
                            e.drain()
                            e.tensor_tensor(pk1[:], pk1[:], pk2[:], ALU.add)
                            e.drain()
                            e.tensor_tensor(pk2[:], ql_, kh_, ALU.mult)
                            e.drain()
                            e.tensor_tensor(pk1[:], pk1[:], pk2[:], ALU.add)
                            e.drain()
                            e.tensor_reduce(qksr[:],
                                            pk1[:].rearrange("p (h d) -> p h d", h=H),
                                            axis=AX.X, op=ALU.add)
                            e.drain()
                            i_sub = tt * TSUB + ts_
                            e.tensor_scalar(self.qks_tm[:, i_sub * H:(i_sub + 1) * H],
                                            qksr[:], EPSQ, None, ALU.add)
                            e.drain()
                        e.sem_inc(s_qefree.h, 1)
                        s_qefree.v += 1
                        e.sem_inc(s_qks.h, 1)
                        s_qks.v += 1
                if pname == "k":
                    e.wait_ge(s_qefree.h, 2 * NT)
                    e.tensor_copy(self.S_sb[:], psS[:])
                    e.drain()
                    self.free_bank(e, 6)
                    self.free_bank(e, 7)
                    e.reciprocal(self.qks_tm[:], self.qks_tm[:])
                    e.drain()
                    e.sem_inc(s_qks.h, 1)
                    s_qks.v += 1
            for r in range(self.TCH // 4 + (1 if self.TCH % 4 else 0)):
                e.wait_ge(s_tr.h, 2 * NT * TSUB + r + 1)
                base = r * 4
                n_here = min(4, self.TCH - base)
                e.tensor_copy(self.rq[:, base * 128:(base + n_here) * 128],
                              self.psP[0][0:H, 0:n_here * 128])
                e.drain()
                self.free_bank(e, 0)
        blk.vector(_v)

        def _a(e):
            nz = 0
            for pi, ph in enumerate(self.attnP):
                pname, gsz = ph[0], ph[7]
                if pname == "v":
                    for tt in range(NT):
                        for fo in range(DCH):
                            idx = tt * DCH + fo
                            par = idx % 2
                            nz += 1
                            e.wait_ge(s_elu_z.h, nz)
                            cs = slice(tt * TT, (tt + 1) * TT)
                            e.dma_start(out=self.h_vh[fo][:, cs], in_=vstH[par][:]
                                        ).then_inc(s_qest[par].h, 16)
                            e.dma_start(out=self.h_vl[fo][:, cs], in_=vstL[par][:]
                                        ).then_inc(s_qest[par].h, 16)
                            s_qest[par].v += 32
                    continue
                for tt in range(NT):
                    for grp in self.fo_groups(gsz):
                        for fi, fo in enumerate(grp):
                            nz += 1
                            e.wait_ge(s_elu_z.h, nz)
                            e.activation(elu_a[:], elu_a[:], AF.Exp)
                            e.drain()
                            e.sem_inc(s_exp.h, 1)
                            s_exp.v += 1
                    for ts_ in range(TSUB):
                        bj = ts_ % 2
                        idx = (pi * NT + tt) * TSUB + ts_ + 1
                        if ts_ == 0:
                            if pname == "q" and tt >= 1:
                                pt = (tt - 1) % 2
                                e.wait_ge(s_qest[pt].h, 32 * ((tt - 1) // 2 + 1))
                            if pname == "k" and tt == 0:
                                e.wait_ge(s_qest[0].h, s_qest[0].v)
                                e.wait_ge(s_qest[1].h, s_qest[1].v)
                        e.wait_ge(s_tr.h, idx)
                        e.activation(tmH[:, ts_ * D:(ts_ + 1) * D],
                                     self.psP[bj][:, 0:D], AF.Copy)
                        e.drain()
                        e.sem_inc(s_hi.h, 1)
                        s_hi.v += 1
                    if pname == "q":
                        e.wait_ge(s_lo2.h, tt * TSUB + TSUB)
                        e.dma_start(out=self.h_qeh[tt], in_=tmH[:]
                                    ).then_inc(s_qest[tt % 2].h, 16)
                        e.dma_start(out=self.h_qel[tt], in_=tmL[:]
                                    ).then_inc(s_qest[tt % 2].h, 16)
                        s_qest[tt % 2].v += 32
        blk.scalar(_a)

        s_sar = self.sem("atsar")
        s_sld = self.sem("atsld")
        s_spl = self.sem("atspl")

        def _sy2(e):
            e.wait_ge(s_qks.h, NT + 1)
            e.dma_start(out=self.cc_Sin[:, :], in_=self.S_sb[:]).then_inc(s_sld.h, 16)
            s_sld.v += 16
        blk.sync(_sy2)

        def _g2(e):
            e.wait_ge(s_sld.h, 16)
            cc = e.collective_compute(
                "AllReduce", ALU.add,
                replica_groups=[[0, 1], [2, 3], [4, 5], [6, 7]],
                ins=[self.cc_Sin[:, :].opt()], outs=[self.cc_Sout[:, :].opt()])
            cc.then_inc(s_sar.h, 1)
            s_sar.v += 1
        blk.gpsimd(_g2)

        def _sy3(e):
            e.wait_ge(s_sar.h, 1)
            for c in range(DCH):
                e.dma_start(out=self.Sp_f[0:DH, c * DH:(c + 1) * DH],
                            in_=self.cc_Sout[:, (2 * c) * DH:(2 * c + 1) * DH]
                            ).then_inc(s_sld.h, 16)
                e.dma_start(out=self.Sp_f[DH:128, c * DH:(c + 1) * DH],
                            in_=self.cc_Sout[:, (2 * c + 1) * DH:(2 * c + 2) * DH]
                            ).then_inc(s_sld.h, 16)
                s_sld.v += 32
        blk.sync(_sy3)

        def _v3(e):
            e.wait_ge(s_sld.h, s_sld.v)
            e.tensor_copy(self.SpH[:], self.Sp_f[:])
            e.drain()
            e.tensor_tensor(self.SpL[:], self.Sp_f[:], self.SpH[:], ALU.subtract)
            e.drain()
            e.sem_inc(s_spl.h, 1)
            s_spl.v += 1
        blk.vector(_v3)

        if self.debug:
            s_d = self.sem("dbg")

            def _dbg(e):
                e.wait_ge(s_spl.h, 1)
                for sq_ in (self.sem("atqestA"), self.sem("atqestB")):
                    e.wait_ge(sq_.h, sq_.v)
                e.dma_start(out=self.d_S[:], in_=self.S_sb[:]).then_inc(s_d.h, 16)
                e.dma_start(out=self.d_rq[:], in_=self.rq[:]).then_inc(s_d.h, 16)
                e.dma_start(out=self.d_xnh[:].rearrange("p (c t) -> c p t", c=DCH),
                            in_=self.h_xnh[:, :, :]).then_inc(s_d.h, 16)
                e.dma_start(out=self.d_xnl[:].rearrange("p (c t) -> c p t", c=DCH),
                            in_=self.h_xnl[:, :, :]).then_inc(s_d.h, 16)
                e.dma_start(out=self.d_vh[:].rearrange("p (c t) -> c p t", c=DCH),
                            in_=self.h_vh[:, :, :]).then_inc(s_d.h, 16)
                e.dma_start(out=self.d_vl[:].rearrange("p (c t) -> c p t", c=DCH),
                            in_=self.h_vl[:, :, :]).then_inc(s_d.h, 16)
                s_d.v += 96
            blk.scalar(_dbg)

    def ph_A6A8(self):
        nc, blk, T, TT, NT, TSUB = self.nc, self.blk, self.T, self.TT, self.NT, self.TSUB
        s_spl = self.sem("atspl")

        s_mm6 = self.sem("a8mm6")
        s_rbc = self.sem("a8rbc")
        s_ev6 = self.sem("a8ev6")
        s_mmg8 = self.sem("a8mmg")
        s_oatt = self.sem("a8oatt")
        s_y = self.sem("a8y")
        s_ytr = self.sem("a8ytr")
        s_yev = self.sem("a8yev")
        s_ym = self.sem("a8ym")
        s_sq2 = self.sem("a8sq2")
        s_sv = self.sem("a8sv")
        s_sd = [self.sem("a8sd0"), self.sem("a8sd1")]

        A = lambda n, s, dt=F32: self.salloc("a68", n, s, dt)
        self.gamma_bc = A("gamma_bc", [128, D])
        self.beta_bc = A("beta_bc", [128, D])
        woRH = A("a8worh", [128, DCH * D], BF16)
        woRL = A("a8worl", [128, DCH * D], BF16)
        s_wo = self.sem("a8wo")
        vqH = [A(f"a8vqh{i}", [128, TT], BF16) for i in range(2)]
        vqL = [A(f"a8vql{i}", [128, TT], BF16) for i in range(2)]
        xnsbH = [A(f"a8xnh{i}", [128, DCH * 128], BF16) for i in range(2)]
        xnsbL = [A(f"a8xnl{i}", [128, DCH * 128], BF16) for i in range(2)]
        s_vld = [self.sem("ppld0"), self.sem("ppld1")]
        b_vld = [s.v for s in s_vld]
        s_qefree = self.sem("atqefree")
        b_qefree = s_qefree.v
        s_xl = [self.sem(f"l1ld{i}") for i in range(2)]
        s_xlf = [self.sem(f"l1ldf{i}") for i in range(2)]
        b_xl = [s.v for s in s_xl]
        b_xlf = [s.v for s in s_xlf]
        s_qest = [self.sem("atqestA"), self.sem("atqestB")]
        s_c = self.sem("consts")
        outH = A("a8outh", [128, DCH * TT], BF16)
        outL = A("a8outl", [128, DCH * TT], BF16)
        out_f = A("a8outf", [128, TT])
        rqbc = A("a8rqbc", [128, TT])
        oatt = A("a8oatt", [128, DCH * TT])
        y_fm = A("a8yfm", [128, DCH * 128])
        y_tm = [A(f"a8ytm{i}", [128, D]) for i in range(2)]
        ym = [A(f"a8ym{i}", [128, D]) for i in range(2)]
        sqb = A("a8sqb", [128, D])
        s_tmv = [A(f"a8stm{i}", [128, D]) for i in range(2)]
        st1 = A("a8st1", [128, 1])
        st2 = [A(f"a8st2{i}", [128, 1]) for i in range(2)]

        def _sync(e):
            e.wait_ge(s_qest[0].h, s_qest[0].v)
            e.wait_ge(s_qest[1].h, s_qest[1].v)
            e.wait_ge(s_spl.h, 1)
            for kc in range(DCH):
                e.dma_start(out=woRH[:, kc * D:(kc + 1) * D], in_=self.h_woh[kc]
                            ).then_inc(s_wo.h, 16)
                e.dma_start(out=woRL[:, kc * D:(kc + 1) * D], in_=self.h_wol[kc]
                            ).then_inc(s_wo.h, 16)
                s_wo.v += 32
            e.dma_start(out=self.gamma_bc[:],
                        in_=self.gamma[:].partition_broadcast(128)).then_inc(s_c.h, 16)
            e.dma_start(out=self.beta_bc[:],
                        in_=self.beta[:].partition_broadcast(128)).then_inc(s_c.h, 16)
            s_c.v += 32
            nvq = 0
            for tt in range(NT):
                for c in range(DCH):
                    idxs = tt * DCH + c
                    e.wait_ge(s_qest[idxs % 2].h,
                              self.b_qest[idxs % 2] + 32 * (idxs // 2 + 1))
                    sl = nvq % 2
                    if nvq >= 2:
                        e.wait_ge(s_qefree.h, b_qefree + nvq - 1)
                    nvq += 1
                    cs = slice(tt * TT, (tt + 1) * TT)
                    e.dma_start(out=vqH[sl][:], in_=self.h_vh[c][:, cs]
                                ).then_inc(s_vld[sl].h, 16)
                    e.dma_start(out=vqL[sl][:], in_=self.h_vl[c][:, cs]
                                ).then_inc(s_vld[sl].h, 16)
                    s_vld[sl].v += 32
                for ts_ in range(TSUB):
                    i = tt * TSUB + ts_
                    sl = i % 2
                    if i >= 2:
                        e.wait_ge(s_xlf[sl].h, b_xlf[sl] + i // 2)
                    colv = slice(i * 128, (i + 1) * 128)
                    e.dma_start(out=xnsbH[sl][:].rearrange("p (c t) -> p c t", c=DCH),
                                in_=self.h_xnh[:, :, colv].transpose([1, 0, 2])
                                ).then_inc(s_xl[sl].h, 16)
                    e.dma_start(out=xnsbL[sl][:].rearrange("p (c t) -> p c t", c=DCH),
                                in_=self.h_xnl[:, :, colv].transpose([1, 0, 2])
                                ).then_inc(s_xl[sl].h, 16)
                    s_xl[sl].v += 32
        blk.sync(_sync)

        def _t(e):
            e.wait_ge(s_spl.h, 1)
            for tt in range(NT):
                for c in range(DCH):
                    bk = c % 2
                    nvq = tt * DCH + c
                    vsl = nvq % 2
                    e.wait_ge(s_vld[vsl].h, b_vld[vsl] + 32 * (nvq // 2 + 1))
                    self.wait_bank(e, bk)
                    for half in range(2):
                        p0, p1 = half * DH, half * DH + DH
                        sh = self.SpH[p0:p1, c * DH:(c + 1) * DH]
                        sl_ = self.SpL[p0:p1, c * DH:(c + 1) * DH]
                        vh = vqH[vsl][p0:p1, :]
                        vl = vqL[vsl][p0:p1, :]
                        po = self.bank(bk)[p0:p1, :]
                        e.matmul(po, sh, vh, start=True, stop=False)
                        e.matmul(po, sh, vl, start=False, stop=False)
                        mm = e.matmul(po, sl_, vh, start=False, stop=True)
                    mm.then_inc(s_qefree.h, 1)
                    s_qefree.v += 1
                    mm2_note = None
                    e.drain()
                    e.sem_inc(s_mm6.h, 1)
                    s_mm6.v += 1
                    self.wait_bank(e, 5)
                    mm = e.matmul(self.bank(5), self.selE[:, c * 128:(c + 1) * 128],
                                  self.rq[:, tt * TT:(tt + 1) * TT], start=True, stop=True)
                    mm.then_inc(s_rbc.h, 1)
                    s_rbc.v += 1
                e.wait_ge(s_ev6.h, tt * DCH + DCH)
                if tt == 0:
                    e.wait_ge(s_wo.h, s_wo.v)
                for fo_g in self.fo_groups(2):
                    for fi, fo in enumerate(fo_g):
                        self.wait_bank(e, 2 + fi)
                    for kc in range(DCH):
                        for fi, fo in enumerate(fo_g):
                            lw_h = woRH[:, kc * D + fo * 128:kc * D + (fo + 1) * 128]
                            lw_l = woRL[:, kc * D + fo * 128:kc * D + (fo + 1) * 128]
                            rx_h = outH[:, kc * TT:(kc + 1) * TT]
                            rx_l = outL[:, kc * TT:(kc + 1) * TT]
                            e.matmul(self.bank(2 + fi), lw_h, rx_h, start=(kc == 0),
                                     stop=False)
                            e.matmul(self.bank(2 + fi), lw_h, rx_l, start=False, stop=False)
                            mm = e.matmul(self.bank(2 + fi), lw_l, rx_h, start=False,
                                          stop=(kc == DCH - 1))
                    mm.then_inc(s_mmg8.h, 1)
                    s_mmg8.v += 1
                for ts_ in range(TSUB):
                    i = tt * TSUB + ts_
                    e.wait_ge(s_y.h, i + 1)
                    self.wait_bank(e, 6)
                    self.wait_bank(e, 7)
                    for c in range(DCH):
                        instr = e.transpose(self.psP[3][:, c * 128:(c + 1) * 128],
                                            y_fm[:, c * 128:(c + 1) * 128], self.ident[:])
                    instr.then_inc(s_ytr.h, 1)
                    s_ytr.v += 1
        blk.tensor(_t)

        def _v(e):
            e.wait_ge(s_c.h, s_c.v)
            for tt in range(NT):
                for c in range(DCH):
                    bk = c % 2
                    e.wait_ge(s_mm6.h, tt * DCH + c + 1)
                    e.wait_ge(s_rbc.h, tt * DCH + c + 1)
                    e.tensor_copy(rqbc[:], self.bank(5))
                    e.drain()
                    self.free_bank(e, 5)
                    e.tensor_tensor(out_f[:], self.bank(bk), rqbc[:], ALU.mult)
                    e.drain()
                    self.free_bank(e, bk)
                    e.tensor_copy(outH[:, c * TT:(c + 1) * TT], out_f[:])
                    e.drain()
                    e.tensor_tensor(outL[:, c * TT:(c + 1) * TT], out_f[:],
                                    outH[:, c * TT:(c + 1) * TT], ALU.subtract)
                    e.drain()
                    e.sem_inc(s_ev6.h, 1)
                    s_ev6.v += 1
                for gi, fo_g in enumerate(self.fo_groups(2)):
                    e.wait_ge(s_mmg8.h, tt * 4 + gi + 1)
                    for fi, fo in enumerate(fo_g):
                        e.tensor_scalar(oatt[:, fo * TT:(fo + 1) * TT], self.bank(2 + fi),
                                        self.biases["bo"][:, fo:fo + 1], None, ALU.add)
                        e.drain()
                        self.free_bank(e, 2 + fi)
                e.sem_inc(s_oatt.h, 1)
                s_oatt.v += 1
                for ts_ in range(TSUB):
                    i = tt * TSUB + ts_
                    par = i % 2
                    if i >= 1:
                        e.wait_ge(s_ytr.h, i)
                    e.wait_ge(s_xl[par].h, b_xl[par] + 32 * (i // 2 + 1))
                    oat_v = oatt[:].rearrange("p (c t) -> p c t", c=DCH)[
                        :, :, ts_ * 128:(ts_ + 1) * 128]
                    yv = y_fm[:].rearrange("p (c t) -> p c t", c=DCH)
                    e.tensor_tensor(yv, oat_v, xnsbH[par][:]
                                    .rearrange("p (c t) -> p c t", c=DCH), ALU.add)
                    e.drain()
                    e.tensor_tensor(yv, yv, xnsbL[par][:]
                                    .rearrange("p (c t) -> p c t", c=DCH), ALU.add)
                    e.drain()
                    e.sem_inc(s_xlf[par].h, 1)
                    s_xlf[par].v += 1
                    e.sem_inc(s_y.h, 1)
                    s_y.v += 1
                    e.wait_ge(s_yev.h, i + 1)
                    e.tensor_reduce(st1[:], y_tm[par][:], axis=AX.X, op=ALU.add)
                    e.drain()
                    e.tensor_scalar(st1[:], st1[:], 1.0 / D, None, ALU.mult)
                    e.drain()
                    if i >= 2:
                        e.wait_ge(s_sq2.h, i - 1)
                    e.tensor_scalar(ym[par][:], y_tm[par][:], st1[:], None, ALU.subtract)
                    e.drain()
                    e.sem_inc(s_ym.h, 1)
                    s_ym.v += 1
                    e.wait_ge(s_sq2.h, i + 1)
                    e.reciprocal(st1[:], st2[par][:])
                    e.drain()
                    if i >= 2:
                        e.wait_ge(s_sd[par].h, 16 * (i // 2))
                    e.scalar_tensor_tensor(s_tmv[par][:], ym[par][:], st1[:],
                                           self.gamma_bc[:], ALU.mult, ALU.mult)
                    e.drain()
                    e.tensor_tensor(s_tmv[par][:], s_tmv[par][:], self.beta_bc[:], ALU.add)
                    e.drain()
                    e.sem_inc(s_sv.h, 1)
                    s_sv.v += 1
        blk.vector(_v)

        def _a(e):
            for tt in range(NT):
                for ts_ in range(TSUB):
                    i = tt * TSUB + ts_
                    par = i % 2
                    e.wait_ge(s_ytr.h, i + 1)
                    if i >= 2:
                        e.wait_ge(s_ym.h, i - 1)
                    e.activation(y_tm[par][:], self.psP[3][:, 0:D], AF.Copy)
                    e.drain()
                    self.free_bank(e, 6)
                    self.free_bank(e, 7)
                    e.sem_inc(s_yev.h, 1)
                    s_yev.v += 1
                    e.wait_ge(s_ym.h, i + 1)
                    if i >= 2:
                        e.wait_ge(s_sv.h, i - 1)
                    e.activation(sqb[:], ym[par][:], AF.Square, accum_out=st2[par][:])
                    e.drain()
                    e.activation(st2[par][:], st2[par][:], AF.Sqrt,
                                 bias=self.c_eps2[:], scale=1.0 / D)
                    e.drain()
                    e.sem_inc(s_sq2.h, 1)
                    s_sq2.v += 1
                    e.wait_ge(s_sv.h, i + 1)
                    e.dma_start(out=self.h_s[i], in_=s_tmv[par][:]
                                ).then_inc(s_sd[par].h, 16)
                    s_sd[par].v += 16
        blk.scalar(_a)

        if self.debug:
            s_d = self.sem("dbg")

            def _dbg(e):
                e.wait_ge(s_sd[0].h, s_sd[0].v)
                e.wait_ge(s_sd[1].h, s_sd[1].v)
                for i in range(self.TCH):
                    e.dma_start(out=self.d_s[i], in_=self.h_s[i]).then_inc(s_d.h, 16)
                    s_d.v += 16
            blk.sync(_dbg)

    def ph_sweep2(self):
        nc, blk, T = self.nc, self.blk, self.T
        s_sc2 = self.sem("scd")
        s_sd = [self.sem("a8sd0"), self.sem("a8sd1")]
        s_ld = [self.sem(f"l1ld{i}") for i in range(2)]
        s_ldf = [self.sem(f"l1ldf{i}") for i in range(2)]
        b_ld = [s.v for s in s_ld]
        b_ldf = [s.v for s in s_ldf]
        s_v = self.sem("s2v")
        s_sig = self.sem("s2sig")
        s_done = self.sem("s2done")
        s_st = self.sem("s2st")
        A = lambda n, s, dt=F32: self.salloc("s23", n, s, dt)
        ubuf = [A(f"s2u{i}", [128, T]) for i in range(2)]
        gbuf = [A(f"s2g{i}", [128, T]) for i in range(2)]
        t1b = A("s2t1", [128, T])
        uq16 = A("s2uq", [128, T], I16)
        g16 = A("s2g16", [128, T])
        gfp = A("s2gfp", [128, T])
        sig = A("s2sig", [128, T])
        sigq = A("s2sigq", [128, T], U16)
        pbuf = A("s2p", [128, T])
        junk = A("s2junk", [128, 1])

        def _sync(e):
            e.wait_ge(s_sc2.h, 2)
            e.wait_ge(s_sd[0].h, 16 * ((self.TCH + 1) // 2))
            e.wait_ge(s_sd[1].h, 16 * (self.TCH // 2))
            if self.debug:
                sdbg = self.sem("dbg")
                e.wait_ge(sdbg.h, sdbg.v)
            for ic in range(ICH):
                sl = ic % 2
                if ic >= 2:
                    e.wait_ge(s_ldf[sl].h, b_ldf[sl] + ic // 2)
                e.dma_start(out=ubuf[sl][:], in_=self.h_u_up[ic]).then_inc(s_ld[sl].h, 16)
                e.dma_start(out=gbuf[sl][:], in_=self.h_u_gate[ic]).then_inc(s_ld[sl].h, 16)
                s_ld[sl].v += 32
        blk.sync(_sync)

        def _v(e):
            for ic in range(ICH):
                sl = ic % 2
                e.wait_ge(s_ld[sl].h, b_ld[sl] + 32 * (ic // 2 + 1))
                e.tensor_scalar(t1b[:], ubuf[sl][:], self.scal["inv_su"][:], RC,
                                ALU.mult, ALU.add)
                e.drain()
                if ic >= 1:
                    e.wait_ge(s_st.h, 32 * ic)
                e.tensor_scalar(uq16[:], t1b[:], RC, None, ALU.subtract)
                e.drain()
                e.tensor_scalar(t1b[:], gbuf[sl][:], self.scal["inv_sg"][:], RC,
                                ALU.mult, ALU.add)
                e.drain()
                e.sem_inc(s_ldf[sl].h, 1)
                s_ldf[sl].v += 1
                e.tensor_scalar(g16[:], t1b[:], RC, None, ALU.subtract)
                e.drain()
                if ic >= 1:
                    e.wait_ge(s_sig.h, ic)
                e.tensor_scalar(gfp[:], g16[:], self.scal["sg"][:], None, ALU.mult)
                e.drain()
                e.sem_inc(s_v.h, 1)
                s_v.v += 1
                e.wait_ge(s_sig.h, ic + 1)
                e.tensor_scalar(t1b[:], sig[:], 65536.0, RC, ALU.mult, ALU.add)
                e.drain()
                e.tensor_scalar(sigq[:], t1b[:], RC, 65535.0, ALU.subtract, ALU.min)
                e.drain()
                e.tensor_tensor(pbuf[:], gfp[:], sigq[:], ALU.mult)
                e.drain()
                e.tensor_scalar(pbuf[:], pbuf[:], 1.0 / 65536.0, None, ALU.mult)
                e.drain()
                e.tensor_reduce(self.amax_b[:], pbuf[:], axis=AX.X, op=ALU.max, apply_absolute_value=True)
                e.drain()
                if ic == 0:
                    e.tensor_copy(self.scal["m_p"][:], self.amax_b[:])
                else:
                    e.tensor_tensor(self.scal["m_p"][:], self.scal["m_p"][:],
                                    self.amax_b[:], ALU.max)
                e.drain()
                e.sem_inc(s_done.h, 1)
                s_done.v += 1
        blk.vector(_v)

        def _a(e):
            for ic in range(ICH):
                e.wait_ge(s_v.h, ic + 1)
                e.activation(sig[:], gfp[:], AF.Sigmoid)
                e.drain()
                e.sem_inc(s_sig.h, 1)
                s_sig.v += 1
                e.wait_ge(s_done.h, ic + 1)
                e.dma_start(out=self.h_uq[ic], in_=uq16[:]).then_inc(s_st.h, 16)
                e.dma_start(out=self.h_p[ic], in_=pbuf[:]).then_inc(s_st.h, 16)
                s_st.v += 32
        blk.scalar(_a)

        s_ardone = self.ar_scalar("g2ar", [self.scal["m_p"][:]])
        s_sc3 = self.sem("scd")

        def _v2(e):
            e.wait_ge(s_ardone.h, s_ardone.v)
            e.tensor_scalar(self.scal["sg2"][:], self.ar_stage[:, 0:1], 1.0 / QMAX,
                            1e-12, ALU.mult, ALU.max)
            e.drain()
            e.reciprocal(self.scal["inv_sg2"][:], self.scal["sg2"][:])
            e.drain()
            e.sem_inc(s_sc3.h, 1)
            s_sc3.v += 1
        blk.vector(_v2)

        if self.debug:
            s_d = self.sem("dbg")

            def _dbg(e):
                e.wait_ge(s_st.h, s_st.v)
                for ic in range(ICH):
                    e.dma_start(out=self.d_uq[ic], in_=self.h_uq[ic]).then_inc(s_d.h, 16)
                    e.dma_start(out=self.d_p[ic], in_=self.h_p[ic]).then_inc(s_d.h, 16)
                    s_d.v += 32
            blk.scalar(_dbg)

    def ph_sweep3_and_wdprep(self):
        nc, blk, T = self.nc, self.blk, self.T
        s_sc3 = self.sem("scd")
        s_st2 = self.sem("s2st")
        s_ld = [self.sem(f"ppld{i}") for i in range(2)]
        s_ldf = [self.sem(f"ppldf{i}") for i in range(2)]
        b_ld = [s.v for s in s_ld]
        b_ldf = [s.v for s in s_ldf]
        s_v = self.sem("s3v")
        s_st = self.sem("s3st")
        A = lambda n, s, dt=F32: self.salloc("s23", n, s, dt)
        p3 = [A(f"s3p{i}", [128, T]) for i in range(2)]
        uq3 = [A(f"s3uq{i}", [128, T], I16) for i in range(2)]
        g2b = A("s3g2", [128, T])
        pob = A("s3po", [128, T])
        junk = A("s3junk", [128, 1])

        s_dld = [self.sem(f"wdld{i}") for i in range(2)]
        s_dldf = [self.sem(f"wdldf{i}") for i in range(2)]
        s_dg = self.sem("wdg")
        s_dst = [self.sem("wdst0"), self.sem("wdst1")]
        b_dld = [s.v for s in s_dld]
        b_dldf = [s.v for s in s_dldf]
        b_dg = s_dg.v
        b_dst = [s.v for s in s_dst]
        dwin = [A(f"wdin{i}", [128, D]) for i in range(2)]
        dbm = A("wdbm", [128, 32])
        dsb = [A(f"wds{i}", [128, 32]) for i in range(2)]
        drb = [A(f"wdr{i}", [128, 32]) for i in range(2)]
        dtq = A("wdt", [128, D])
        ddq = A("wddq", [128, D])
        dwh = [A(f"wdh{i}", [128, D], BF16) for i in range(2)]
        dwl = [A(f"wdl{i}", [128, D], BF16) for i in range(2)]
        djobs = [(c, s3) for c in range(DCH) for s3 in range(3)]
        self.n_djobs = len(djobs)

        def _sync(e):
            e.wait_ge(s_sc3.h, 3)
            for ic in range(ICH):
                sl = ic % 2
                e.wait_ge(s_st2.h, 32 * (ic + 1))
                if ic >= 2:
                    e.wait_ge(s_ldf[sl].h, b_ldf[sl] + ic // 2)
                e.dma_start(out=p3[sl][:], in_=self.h_p[ic]).then_inc(s_ld[sl].h, 16)
                e.dma_start(out=uq3[sl][:], in_=self.h_uq[ic]).then_inc(s_ld[sl].h, 16)
                s_ld[sl].v += 32
                c, s3 = djobs[ic]
                if ic >= 2:
                    e.wait_ge(s_dldf[sl].h, b_dldf[sl] + ic // 2)
                e.dma_start(out=dwin[sl][:],
                            in_=self.w_down[c * 128:(c + 1) * 128, s3 * D:(s3 + 1) * D]
                            ).then_inc(s_dld[sl].h, 16)
                s_dld[sl].v += 16
        blk.sync(_sync)

        def _v(e):
            for ic in range(ICH):
                sl = ic % 2
                e.wait_ge(s_ld[sl].h, b_ld[sl] + 32 * (ic // 2 + 1))
                e.tensor_scalar(g2b[:], p3[sl][:], self.scal["inv_sg2"][:], RC,
                                ALU.mult, ALU.add)
                e.drain()
                e.tensor_scalar(g2b[:], g2b[:], RC, None, ALU.subtract)
                e.drain()
                if ic >= 1:
                    e.wait_ge(s_st.h, 16 * ic)
                e.tensor_tensor(pob[:], g2b[:], uq3[sl][:], ALU.mult)
                e.drain()
                e.sem_inc(s_ldf[sl].h, 1)
                s_ldf[sl].v += 1
                e.tensor_reduce(self.amax_b[:], pob[:], axis=AX.X, op=ALU.max, apply_absolute_value=True)
                e.drain()
                if ic == 0:
                    e.tensor_copy(self.scal["m_po"][:], self.amax_b[:])
                else:
                    e.tensor_tensor(self.scal["m_po"][:], self.scal["m_po"][:],
                                    self.amax_b[:], ALU.max)
                e.drain()
                e.sem_inc(s_v.h, 1)
                s_v.v += 1
                e.wait_ge(s_dld[sl].h, b_dld[sl] + 16 * (ic // 2 + 1))
                e.tensor_reduce(dbm[:], dwin[sl][:].rearrange("p (a b) -> p a b", b=32),
                                axis=AX.X, op=ALU.max, apply_absolute_value=True)
                e.drain()
                e.tensor_scalar(dsb[sl][:], dbm[:], 1.0 / 7.0, 1e-12,
                                ALU.mult, ALU.max)
                e.drain()
                e.reciprocal(drb[sl][:], dsb[sl][:])
                e.drain()
                e.tensor_tensor(dtq[:], dwin[sl][:],
                                drb[sl][:].rearrange("p (a b) -> p a b", b=1)
                                .broadcast_to([128, 32, 32]), ALU.mult)
                e.drain()
                e.sem_inc(s_dldf[sl].h, 1)
                s_dldf[sl].v += 1
                e.tensor_scalar(dtq[:], dtq[:], RC, RC, ALU.add, ALU.subtract)
                e.drain()
                e.tensor_tensor(ddq[:], dtq[:],
                                dsb[sl][:].rearrange("p (a b) -> p a b", b=1)
                                .broadcast_to([128, 32, 32]), ALU.mult)
                e.drain()
                if ic >= 2:
                    e.wait_ge(s_dst[sl].h, b_dst[sl] + 32 * (ic // 2))
                e.tensor_copy(dwh[sl][:], ddq[:])
                e.drain()
                e.tensor_tensor(dwl[sl][:], ddq[:], dwh[sl][:], ALU.subtract)
                e.drain()
                e.sem_inc(s_dg.h, 1)
                s_dg.v += 1
        blk.vector(_v)

        def _a(e):
            for ic in range(ICH):
                e.wait_ge(s_v.h, ic + 1)
                e.dma_start(out=self.h_po[ic], in_=pob[:]).then_inc(s_st.h, 16)
                s_st.v += 16
                c, s3 = djobs[ic]
                sl = ic % 2
                e.wait_ge(s_dg.h, b_dg + ic + 1)
                e.dma_start(out=self.h_wdnh[c * 128:(c + 1) * 128, s3 * D:(s3 + 1) * D],
                            in_=dwh[sl][:]).then_inc(s_dst[sl].h, 16)
                e.dma_start(out=self.h_wdnl[c * 128:(c + 1) * 128, s3 * D:(s3 + 1) * D],
                            in_=dwl[sl][:]).then_inc(s_dst[sl].h, 16)
                s_dst[sl].v += 32
        blk.scalar(_a)

        s_ardone = self.ar_scalar("oar", [self.scal["m_po"][:]])
        s_sc4 = self.sem("scd")

        def _v2(e):
            e.wait_ge(s_ardone.h, s_ardone.v)
            e.tensor_scalar(self.scal["tmp0"][:], self.ar_stage[:, 0:1],
                            self.scal["sg2"][:], self.scal["su"][:], ALU.mult, ALU.mult)
            e.drain()
            e.tensor_scalar(self.scal["so"][:], self.scal["tmp0"][:], 1.0 / QMAX,
                            1e-12, ALU.mult, ALU.max)
            e.drain()
            e.reciprocal(self.scal["inv_so"][:], self.scal["so"][:])
            e.drain()
            e.tensor_scalar(self.scal["c3"][:], self.scal["inv_so"][:],
                            self.scal["sg2"][:], self.scal["su"][:], ALU.mult, ALU.mult)
            e.drain()
            e.sem_inc(s_sc4.h, 1)
            s_sc4.v += 1
        blk.vector(_v2)

        if self.debug:
            s_d = self.sem("dbg")

            def _dbg(e):
                e.wait_ge(s_st.h, s_st.v)
                for ic in range(ICH):
                    e.dma_start(out=self.d_po[ic], in_=self.h_po[ic]).then_inc(s_d.h, 16)
                    s_d.v += 16
            blk.scalar(_dbg)

    def ph_sweep4(self):
        nc, blk, T, TT, NT, TSUB = self.nc, self.blk, self.T, self.TT, self.NT, self.TSUB
        s_sc4 = self.sem("scd")
        s_dst = [self.sem("wdst0"), self.sem("wdst1")]
        s_st3 = self.sem("s3st")
        s_sd = [self.sem("a8sd0"), self.sem("a8sd1")]

        s_wd = self.sem("s4wd")
        s_po = [self.sem(f"l1ld{i}") for i in range(2)]
        s_pof = [self.sem(f"l1ldf{i}") for i in range(2)]
        b_po = [s.v for s in s_po]
        b_pof = [s.v for s in s_pof]
        s_oi = self.sem("l1xn")
        b_oi = s_oi.v
        s_mm = self.sem("s4mm")
        s_act = self.sem("l1sq")
        b_act = s_act.v
        s_sxld = [self.sem("s4sxA"), self.sem("s4sxB")]
        s_sgv = self.sem("s4sgv")
        s_tr = self.sem("s4tr")
        s_out = [self.sem("s4outA"), self.sem("s4outB")]

        A = lambda n, s, dt=F32: self.salloc("s4", n, s, dt)
        wdTH = A("s4wdh", [128, ICH * D], BF16)
        wdTL = A("s4wdl", [128, ICH * D], BF16)
        pobuf = [A(f"s4pob{i}", [128, TT]) for i in range(2)]
        oiH = A("s4oih", [128, ICH * TT], BF16)
        oiL = A("s4oil", [128, ICH * TT], BF16)
        oi_f = A("s4oif", [128, TT])
        gbuf = A("s4gb", [128, DCH * TT])
        s_tmb = [A(f"s4stm{i}", [128, D]) for i in range(2)]
        x_tmb = [A(f"s4xtm{i}", [128, D]) for i in range(2)]

        def _sync(e):
            e.wait_ge(s_dst[0].h, s_dst[0].v)
            e.wait_ge(s_dst[1].h, s_dst[1].v)
            e.wait_ge(s_st3.h, 16 * ICH)
            if self.debug:
                sdbg = self.sem("dbg")
                e.wait_ge(sdbg.h, sdbg.v)
            for ic in range(ICH):
                e.dma_start(out=wdTH[:, ic * D:(ic + 1) * D],
                            in_=self.h_wdnh[:, ic * 128:(ic + 1) * 128],
                            transpose=True).then_inc(s_wd.h, 16)
                e.dma_start(out=wdTL[:, ic * D:(ic + 1) * D],
                            in_=self.h_wdnl[:, ic * 128:(ic + 1) * 128],
                            transpose=True).then_inc(s_wd.h, 16)
                s_wd.v += 32
            e.wait_ge(s_sc4.h, 4)
            for tt in range(NT):
                for ic in range(ICH):
                    sl = ic % 2
                    j = tt * ICH + ic
                    if j >= 2:
                        e.wait_ge(s_pof[sl].h, b_pof[sl] + j // 2)
                    e.dma_start(out=pobuf[sl][:],
                                in_=self.h_po[ic, :, tt * TT:(tt + 1) * TT]
                                ).then_inc(s_po[sl].h, 16)
                    s_po[sl].v += 16
                for ts_ in range(TSUB):
                    i = tt * TSUB + ts_
                    par = i % 2
                    e.wait_ge(s_sd[par].h, 16 * (i // 2 + 1))
                    if i >= 2:
                        e.wait_ge(s_sgv.h, i - 1)
                        e.wait_ge(s_out[par].h, 16 * (i // 2))
                    e.dma_start(out=s_tmb[par][:], in_=self.h_s[i]
                                ).then_inc(s_sxld[par].h, 16)
                    e.dma_start(out=x_tmb[par][:],
                                in_=self.x_in[i * 128:(i + 1) * 128, :]
                                ).then_inc(s_sxld[par].h, 16)
                    s_sxld[par].v += 32
        blk.sync(_sync)

        def _v(e):
            for tt in range(NT):
                for ic in range(ICH):
                    sl = ic % 2
                    j = tt * ICH + ic
                    e.wait_ge(s_po[sl].h, b_po[sl] + 16 * (j // 2 + 1))
                    e.tensor_scalar(oi_f[:], pobuf[sl][:], self.scal["c3"][:], RC,
                                    ALU.mult, ALU.add)
                    e.drain()
                    e.sem_inc(s_pof[sl].h, 1)
                    s_pof[sl].v += 1
                    e.tensor_scalar(oi_f[:], oi_f[:], RC, None, ALU.subtract)
                    e.drain()
                    if tt >= 1 and ic == 0:
                        e.wait_ge(s_mm.h, tt * DCH)
                    e.tensor_copy(oiH[:, ic * TT:(ic + 1) * TT], oi_f[:])
                    e.drain()
                    e.tensor_tensor(oiL[:, ic * TT:(ic + 1) * TT], oi_f[:],
                                    oiH[:, ic * TT:(ic + 1) * TT], ALU.subtract)
                    e.drain()
                    e.sem_inc(s_oi.h, 1)
                    s_oi.v += 1
                for ts_ in range(TSUB):
                    i = tt * TSUB + ts_
                    par = i % 2
                    e.wait_ge(s_tr.h, i + 1)
                    e.wait_ge(s_sxld[par].h, 32 * (i // 2 + 1))
                    e.tensor_tensor(s_tmb[par][:], self.psP[3][:, 0:D], s_tmb[par][:],
                                    ALU.mult)
                    e.drain()
                    self.free_bank(e, 6)
                    self.free_bank(e, 7)
                    e.tensor_tensor(s_tmb[par][:], s_tmb[par][:], x_tmb[par][:], ALU.add)
                    e.drain()
                    e.sem_inc(s_sgv.h, 1)
                    s_sgv.v += 1
        blk.vector(_v)

        def _t(e):
            e.wait_ge(s_wd.h, 32 * ICH)
            for tt in range(NT):
                for fo in range(DCH):
                    bk = fo % 2
                    e.wait_ge(s_oi.h, b_oi + tt * ICH + ICH)
                    self.wait_bank(e, bk)
                    first = True
                    for ic in range(ICH):
                        lw_h = wdTH[:, ic * D + fo * 128: ic * D + (fo + 1) * 128]
                        lw_l = wdTL[:, ic * D + fo * 128: ic * D + (fo + 1) * 128]
                        rx_h = oiH[:, ic * TT:(ic + 1) * TT]
                        rx_l = oiL[:, ic * TT:(ic + 1) * TT]
                        e.matmul(self.bank(bk), lw_h, rx_h, start=first, stop=False)
                        first = False
                        e.matmul(self.bank(bk), lw_h, rx_l, start=False, stop=False)
                        mm = e.matmul(self.bank(bk), lw_l, rx_h, start=False,
                                      stop=(ic == ICH - 1))
                    mm.then_inc(s_mm.h, 1)
                    s_mm.v += 1
                for ts_ in range(TSUB):
                    e.wait_ge(s_act.h, b_act + tt * DCH + DCH)
                    self.wait_bank(e, 6)
                    self.wait_bank(e, 7)
                    for c in range(DCH):
                        instr = e.transpose(
                            self.psP[3][:, c * 128:(c + 1) * 128],
                            gbuf[:, c * TT + ts_ * 128: c * TT + (ts_ + 1) * 128],
                            self.ident[:])
                    instr.then_inc(s_tr.h, 1)
                    s_tr.v += 1
        blk.tensor(_t)

        def _a(e):
            for tt in range(NT):
                for fo in range(DCH):
                    bk = fo % 2
                    e.wait_ge(s_mm.h, tt * DCH + fo + 1)
                    if tt >= 1 and fo == 0:
                        e.wait_ge(s_tr.h, tt * TSUB)
                    e.activation(gbuf[:, fo * TT:(fo + 1) * TT], self.bank(bk), AF.Silu,
                                 scale=self.scal["so"][:])
                    e.drain()
                    self.free_bank(e, bk)
                    e.sem_inc(s_act.h, 1)
                    s_act.v += 1
                for ts_ in range(TSUB):
                    i = tt * TSUB + ts_
                    par = i % 2
                    e.wait_ge(s_sgv.h, i + 1)
                    e.dma_start(out=self.out[i * 128:(i + 1) * 128, :],
                                in_=s_tmb[par][:]).then_inc(s_out[par].h, 16)
                    s_out[par].v += 16
        blk.scalar(_a)

    def ph_final_wait(self):
        blk = self.blk
        s_out = [self.sem("s4outA"), self.sem("s4outB")]

        def _sync(e):
            e.wait_ge(s_out[0].h, s_out[0].v)
            e.wait_ge(s_out[1].h, s_out[1].v)
        blk.sync(_sync)

        if self.debug:
            s_fd = self.sem("dbg")

            def _a(e):
                e.wait_ge(s_out[0].h, s_out[0].v)
                e.wait_ge(s_out[1].h, s_out[1].v)
                for j, nm in enumerate(["sx", "su", "sg", "sg2", "so"]):
                    e.dma_start(out=self.d_scal[:, j:j + 1], in_=self.scal[nm][:],
                                allow_slow_non_contiguous=True).then_inc(s_fd.h, 16)
                    s_fd.v += 16
                e.wait_ge(s_fd.h, s_fd.v)
            blk.scalar(_a)


def _make_consts():
    ident = np.eye(128, dtype=np.float32)
    selE = np.zeros((DCH, H, 128), np.float32)
    for c in range(DCH):
        selE[c, 2 * c, 0:64] = 1.0
        selE[c, 2 * c + 1, 64:128] = 1.0
    return ident, selE


def _in_maps(inputs, T):
    ident, selE = _make_consts()
    x = np.ascontiguousarray(np.asarray(inputs["L0_attn_x"], np.float32))
    base = {
        "Wq": inputs["L0_attn_Wq"], "Wk": inputs["L0_attn_Wk"], "Wv": inputs["L0_attn_Wv"],
        "Wq2": inputs["L0_attn_Wq2"], "Wk2": inputs["L0_attn_Wk2"],
        "Wv2": inputs["L0_attn_Wv2"], "Wo": inputs["L0_attn_Wo"],
        "bq": inputs["L0_attn_bq"], "bk": inputs["L0_attn_bk"], "bv": inputs["L0_attn_bv"],
        "bq2": inputs["L0_attn_bq2"], "bk2": inputs["L0_attn_bk2"],
        "bv2": inputs["L0_attn_bv2"], "bo": inputs["L0_attn_bo"],
        "gamma": inputs["L0_attn_gamma"], "beta": inputs["L0_attn_beta"],
        "w_gate": inputs["L0_ffn_w_gate"], "w_up": inputs["L0_ffn_w_up"],
        "w_down": inputs["L0_ffn_w_down"],
        "ident": ident, "selE": selE,
    }
    base = {k: np.ascontiguousarray(np.asarray(v, np.float32)) for k, v in base.items()}
    maps = []
    for c in range(NCORE):
        m = dict(base)
        m["x"] = np.ascontiguousarray(x[c // 2, (c % 2) * T:(c % 2 + 1) * T, :])
        maps.append(m)
    return maps


def kernel(**inputs):
    T, TT = 2048, 512
    b = Builder(T=T, TT=TT, debug=False)
    nc = b.build()
    in_maps = _in_maps(inputs, T)
    res = run_bass_kernel_spmd(nc, in_maps, core_ids=list(range(NCORE)))
    out = np.empty((4, 4096, D), np.float32)
    for c in range(NCORE):
        out[c // 2, (c % 2) * T:(c % 2 + 1) * T, :] = np.asarray(res.results[c]["out"])
    return out


# version 11
# speedup vs baseline: 1.0233x; 1.0233x over previous
import contextlib
import numpy as np
from concourse import library_config
import concourse.bass as bass
import concourse.mybir as mybir
from bass_rust import ReduceOp
from concourse.bass_utils import run_bass_kernel_spmd

F32 = mybir.dt.float32
BF16 = mybir.dt.bfloat16
I16 = mybir.dt.int16
U16 = mybir.dt.uint16
ALU = mybir.AluOpType
AF = mybir.ActivationFunctionType
AX = mybir.AxisListType

RC = 12582912.0
QMAX = 32767.0
NCORE = 8
D = 1024
DCH = 8
H = 16
DH = 64
I3 = 3072
ICH = 24
EPS1 = 1e-5
EPS2 = 1e-3
EPSQ = 1e-6


class Sem:
    def __init__(self, nc, name):
        self.h = nc.alloc_semaphore(name)
        self.v = 0


class Builder:
    def __init__(self, T=2048, TT=512, debug=False):
        assert T % TT == 0 and TT % 128 == 0
        self.T = T
        self.TT = TT
        self.NT = T // TT
        self.TSUB = TT // 128
        self.TCH = T // 128
        self.debug = debug
        self.nc = bass.Bass("TRN2", target_bir_lowering=False)
        self.sems = {}
        self.es = {}

    def sem(self, name):
        if name not in self.sems:
            self.sems[name] = Sem(self.nc, f"s_{name}")
        return self.sems[name]

    def scope(self, name):
        self.es[name] = contextlib.ExitStack()

    def salloc(self, scope, name, shape, dt=F32):
        return self.es[scope].enter_context(self.nc.sbuf_tensor(name, shape, dt))

    def close_scope(self, name):
        self.es[name].close()

    def bank(self, i, w=None):
        w = w if w is not None else self.TT
        return self.psP[i // 2][:, (i % 2) * 512:(i % 2) * 512 + w]

    def wait_bank(self, e, i):
        e.wait_ge(self.s_bk[i].h, self.bk_uses[i])
        self.bk_uses[i] += 1

    def free_bank(self, e, i):
        e.sem_inc(self.s_bk[i].h, 1)
        self.s_bk[i].v += 1

    @staticmethod
    def fo_groups(gsz):
        gs, fo = [], 0
        while fo < DCH:
            gs.append(list(range(fo, min(fo + gsz, DCH))))
            fo += gsz
        return gs

    def declare_io(self):
        nc, T = self.nc, self.T
        g = lambda n, s, dt=F32, kind="ExternalInput": nc.dram_tensor(n, s, dt, kind=kind)
        self.x_in = g("x", [T, D])
        for w in ["Wq", "Wk", "Wv", "Wq2", "Wk2", "Wv2", "Wo"]:
            setattr(self, w, g(w, [D, D]))
        for b in ["bq", "bk", "bv", "bq2", "bk2", "bv2", "bo", "gamma", "beta"]:
            setattr(self, b, g(b, [D]))
        self.w_gate = g("w_gate", [I3, D])
        self.w_up = g("w_up", [I3, D])
        self.w_down = g("w_down", [D, I3])
        self.ident_in = g("ident", [128, 128])
        self.selE_in = g("selE", [DCH, H, 128])
        self.out = g("out", [T, D], F32, kind="ExternalOutput")

        dt_ = lambda n, s, dt=F32: nc.dram_tensor(n, s, dt)
        self.h_u_up = dt_("h_u_up", [ICH, 128, T])
        self.h_u_gate = dt_("h_u_gate", [ICH, 128, T])
        self.h_uq = dt_("h_uq", [ICH, 128, T], I16)
        self.h_p = dt_("h_p", [ICH, 128, T])
        self.h_po = dt_("h_po", [ICH, 128, T])
        self.h_s = dt_("h_s", [self.TCH, 128, D])
        self.h_qeh = dt_("h_qeh", [self.NT, 128, self.TSUB * D], BF16)
        self.h_qel = dt_("h_qel", [self.NT, 128, self.TSUB * D], BF16)
        self.h_xnh = dt_("h_xnh", [DCH, 128, T], BF16)
        self.h_xnl = dt_("h_xnl", [DCH, 128, T], BF16)
        self.h_vh = dt_("h_vh", [DCH, 128, T], BF16)
        self.h_vl = dt_("h_vl", [DCH, 128, T], BF16)
        for w in ["wq", "wk", "wv", "wq2", "wk2", "wv2", "wo"]:
            setattr(self, f"h_{w}h", dt_(f"h_{w}h", [DCH, 128, D], BF16))
            setattr(self, f"h_{w}l", dt_(f"h_{w}l", [DCH, 128, D], BF16))
        self.h_wuph = dt_("h_wuph", [I3, D], BF16)
        self.h_wupl = dt_("h_wupl", [I3, D], BF16)
        self.h_wgh = dt_("h_wgh", [I3, D], BF16)
        self.h_wgl = dt_("h_wgl", [I3, D], BF16)
        self.h_wdnh = dt_("h_wdnh", [D, I3], BF16)
        self.h_wdnl = dt_("h_wdnl", [D, I3], BF16)
        self.cc_in = dt_("cc_in", [1, 8])
        self.cc_out = dt_("cc_out", [1, 8])
        self.cc_Sin = dt_("cc_Sin", [DH, H * DH])
        self.cc_Sout = dt_("cc_Sout", [DH, H * DH])

        if self.debug:
            dbg = lambda n, s, dt=F32: nc.dram_tensor(n, s, dt, kind="ExternalOutput")
            self.d_xnh = dbg("d_xnh", [128, DCH * T], BF16)
            self.d_xnl = dbg("d_xnl", [128, DCH * T], BF16)
            self.d_scal = dbg("d_scal", [128, 8])
            self.d_S = dbg("d_S", [DH, H * DH])
            self.d_rq = dbg("d_rq", [H, T])
            self.d_s = dbg("d_s", [self.TCH, 128, D])
            self.d_u = dbg("d_u", [ICH, 128, T])
            self.d_uq = dbg("d_uq", [ICH, 128, T], I16)
            self.d_p = dbg("d_p", [ICH, 128, T])
            self.d_po = dbg("d_po", [ICH, 128, T])
            self.d_vh = dbg("d_vh", [128, DCH * T], BF16)
            self.d_vl = dbg("d_vl", [128, DCH * T], BF16)

    def alloc_static(self):
        nc, T = self.nc, self.T
        S = lambda n, s, dt=F32: nc.alloc_sbuf_tensor(n, s, dt)
        self.ident = S("c_ident", [128, 128])
        self.selE = S("c_selE", [H, DCH * 128])
        self.biases = {}
        for b in ["bq", "bk", "bv", "bq2", "bk2", "bv2", "bo"]:
            self.biases[b] = S(f"c_{b}", [128, DCH])
        self.scal = {n: S(f"sc_{n}", [128, 1]) for n in
                     ["m_up", "m_gate", "m_p", "m_po",
                      "sx", "inv_sx", "su", "inv_su", "sg", "inv_sg",
                      "sg2", "inv_sg2", "so", "inv_so", "c3", "tmp0"]}
        self.sevens = S("c_sevens", [128, 1])
        self.c_eps1 = S("c_eps1", [128, 1])
        self.c_eps2 = S("c_eps2", [128, 1])
        self.ones_row = S("c_ones_row", [1, 128])
        self.rq = S("rq", [H, T])
        self.qks_tm = S("qks_tm", [128, self.TCH * H])
        self.S_sb = S("S_sb", [DH, H * DH])
        self.Sp_f = S("Sp_f", [128, DCH * DH])
        self.SpH = S("SpH", [128, DCH * DH], BF16)
        self.SpL = S("SpL", [128, DCH * DH], BF16)
        self.amax_a = S("amax_a", [128, 1])
        self.amax_b = S("amax_b", [128, 1])
        self.ar_stage = S("ar_stage", [128, 8])
        self.psP = [nc.alloc_psum_tensor(f"psP{i}", [128, 1024], F32) for i in range(4)]
        self.s_bk = [self.sem(f"bk{i}") for i in range(8)]
        self.bk_uses = [0] * 8

    def build(self):
        self.declare_io()
        self.alloc_static()
        nc = self.nc
        with nc.Block() as blk:
            self.blk = blk
            self.ph_consts()
            self.scope("e0")
            self.scope("ln1")
            self.ph_ln1()
            self.ph_sx_scales()
            self.close_scope("ln1")
            self.ph_xi()
            self.ph_prep_and_sweep1()
            self.close_scope("e0")
            self.scope("att")
            self.ph_attn()
            self.close_scope("att")
            self.scope("a68")
            self.ph_A6A8()
            self.close_scope("a68")
            self.scope("s23")
            self.ph_sweep2()
            self.ph_sweep3_and_wdprep()
            self.close_scope("s23")
            self.scope("s4")
            self.ph_sweep4()
            self.close_scope("s4")
            self.ph_final_wait()
        return nc

    def ph_consts(self):
        blk = self.blk
        s_c = self.sem("consts")
        s_g = self.sem("ar_g")

        def _sync(e):
            e.dma_start(out=self.ident[:], in_=self.ident_in[:, :]).then_inc(s_c.h, 16)
            e.dma_start(out=self.selE[:].rearrange("h (c p) -> h c p", c=DCH),
                        in_=self.selE_in[:, :, :].transpose([1, 0, 2])
                        ).then_inc(s_c.h, 16)
            for b in ["bq", "bk", "bv", "bq2", "bk2", "bv2", "bo"]:
                src = getattr(self, b)[:].rearrange("(c p) -> p c", p=128)
                e.dma_start(out=self.biases[b][:], in_=src,
                            allow_slow_non_contiguous=True).then_inc(s_c.h, 16)
            s_c.v = 16 * 9
        blk.sync(_sync)

        def _g(e):
            e.memset(self.sevens[:], 7.0)
            e.memset(self.amax_a[:], 0.0)
            e.memset(self.c_eps1[:], EPS1)
            e.memset(self.c_eps2[:], EPS2)
            e.memset(self.ones_row[:], 1.0)
            e.drain()
            e.sem_inc(s_g.h, 1)
            s_g.v += 1
        blk.gpsimd(_g)

    def ph_ln1(self):
        nc, blk = self.nc, self.blk
        TCH = self.TCH
        s_c = self.sem("consts")
        s_g0 = self.sem("ar_g")
        s_ld = [self.sem(f"l1ld{i}") for i in range(2)]
        s_ldf = [self.sem(f"l1ldf{i}") for i in range(2)]
        s_xm = self.sem("l1xm")
        s_sq = self.sem("l1sq")
        s_xn = self.sem("l1xn")
        s_tr = self.sem("l1tr")
        s_hi = self.sem("l1hi")
        s_lo = self.sem("l1lo")
        s_xst = [self.sem("xstA"), self.sem("xstB")]

        A = lambda n, s, dt=F32: self.salloc("ln1", n, s, dt)
        xt = [A(f"l1xt{i}", [128, D]) for i in range(2)]
        xm = [A(f"l1xm{i}", [128, D]) for i in range(2)]
        xn_tm = [A(f"l1xn{i}", [128, D]) for i in range(2)]
        xnhS = [A(f"l1xnh{i}", [128, D], BF16) for i in range(2)]
        xnlS = [A(f"l1xnl{i}", [128, D], BF16) for i in range(2)]
        sq = A("l1sqb", [128, D])
        st1 = A("l1st1", [128, 1])
        st2 = [A(f"l1st2{i}", [128, 1]) for i in range(2)]
        red = A("l1red", [128, 1])

        def _sync(e):
            for i in range(TCH):
                sl = i % 2
                if i >= 2:
                    e.wait_ge(s_ldf[sl].h, i // 2)
                e.dma_start(out=xt[sl][:], in_=self.x_in[i * 128:(i + 1) * 128, :]
                            ).then_inc(s_ld[sl].h, 16)
                s_ld[sl].v += 16
        blk.sync(_sync)

        def _v(e):
            e.wait_ge(s_g0.h, 1)
            e.wait_ge(s_c.h, s_c.v)
            for i in range(TCH):
                sl = i % 2
                e.wait_ge(s_ld[sl].h, 16 * (i // 2 + 1))
                e.tensor_reduce(st1[:], xt[sl][:], axis=AX.X, op=ALU.add)
                e.drain()
                e.tensor_scalar(st1[:], st1[:], 1.0 / D, None, ALU.mult)
                e.drain()
                if i >= 2:
                    e.wait_ge(s_sq.h, i - 1)
                e.tensor_scalar(xm[sl][:], xt[sl][:], st1[:], None, ALU.subtract)
                e.drain()
                e.sem_inc(s_ldf[sl].h, 1)
                s_ldf[sl].v += 1
                e.sem_inc(s_xm.h, 1)
                s_xm.v += 1
                e.wait_ge(s_sq.h, i + 1)
                e.reciprocal(red[:], st2[sl][:])
                e.drain()
                if i >= 2:
                    e.wait_ge(s_tr.h, i - 1)
                e.tensor_scalar(xn_tm[sl][:], xm[sl][:], red[:], None, ALU.mult)
                e.drain()
                e.tensor_reduce(st1[:], xn_tm[sl][:], axis=AX.X, op=ALU.max, apply_absolute_value=True)
                e.drain()
                e.tensor_tensor(self.amax_a[:], self.amax_a[:], st1[:], ALU.max)
                e.drain()
                e.sem_inc(s_xn.h, 1)
                s_xn.v += 1
                e.wait_ge(s_hi.h, i + 1)
                bj = i % 2
                e.tensor_tensor(xnlS[bj][:], self.psP[bj][:, 0:D], xnhS[bj][:],
                                ALU.subtract)
                e.drain()
                self.free_bank(e, 2 * bj)
                self.free_bank(e, 2 * bj + 1)
                e.sem_inc(s_lo.h, 1)
                s_lo.v += 1
        blk.vector(_v)

        def _a(e):
            e.wait_ge(s_g0.h, 1)
            for i in range(TCH):
                sl = i % 2
                e.wait_ge(s_xm.h, i + 1)
                if i >= 2:
                    e.wait_ge(s_xn.h, i - 1)
                e.activation(sq[:], xm[sl][:], AF.Square, accum_out=st2[sl][:])
                e.drain()
                e.activation(st2[sl][:], st2[sl][:], AF.Sqrt, bias=self.c_eps1[:], scale=1.0 / D)
                e.drain()
                e.sem_inc(s_sq.h, 1)
                s_sq.v += 1
                e.wait_ge(s_tr.h, i + 1)
                bj = i % 2
                if i >= 2:
                    e.wait_ge(s_xst[bj].h, 32 * (i // 2))
                e.activation(xnhS[bj][:], self.psP[bj][:, 0:D], AF.Copy)
                e.drain()
                e.sem_inc(s_hi.h, 1)
                s_hi.v += 1
                e.wait_ge(s_lo.h, i + 1)
                dsth = self.h_xnh[:, :, i * 128:(i + 1) * 128].transpose([1, 0, 2])
                dstl = self.h_xnl[:, :, i * 128:(i + 1) * 128].transpose([1, 0, 2])
                e.dma_start(out=dsth, in_=xnhS[bj][:].rearrange("p (c t) -> p c t", c=DCH)
                            ).then_inc(s_xst[bj].h, 16)
                e.dma_start(out=dstl, in_=xnlS[bj][:].rearrange("p (c t) -> p c t", c=DCH)
                            ).then_inc(s_xst[bj].h, 16)
                s_xst[bj].v += 32
        blk.scalar(_a)

        def _t(e):
            e.wait_ge(s_c.h, s_c.v)
            for i in range(TCH):
                sl = i % 2
                bj = i % 2
                e.wait_ge(s_xn.h, i + 1)
                self.wait_bank(e, 2 * bj)
                self.wait_bank(e, 2 * bj + 1)
                for c in range(DCH):
                    instr = e.transpose(self.psP[bj][:, c * 128:(c + 1) * 128],
                                        xn_tm[sl][:, c * 128:(c + 1) * 128], self.ident[:])
                instr.then_inc(s_tr.h, 1)
                s_tr.v += 1
        blk.tensor(_t)

    def ar_scalar(self, name, srcs):
        blk = self.blk
        n_src = len(srcs)
        s_d = self.sem("ar_d")
        s_g = self.sem("ar_g")
        s_v = self.sem("l1xm")
        s_t = self.sem("l1tr")
        s_done = self.sem("ar_done")

        def _v0(e):
            e.drain()
            e.sem_inc(s_v.h, 1)
            s_v.v += 1
        blk.vector(_v0)

        def _t0(e):
            e.wait_ge(s_v.h, s_v.v)
            self.wait_bank(e, 4)
            for j, srcap in enumerate(srcs):
                instr = e.transpose(self.psP[2][0:1, j * 128:(j + 1) * 128],
                                    srcap, self.ident[:])
            instr.then_inc(s_t.h, 1)
            s_t.v += 1
        blk.tensor(_t0)

        def _v1(e):
            e.wait_ge(s_t.h, s_t.v)
            for j in range(n_src):
                e.tensor_reduce(self.ar_stage[0:1, j:j + 1],
                                self.psP[2][0:1, j * 128:(j + 1) * 128],
                                axis=AX.X, op=ALU.max)
            e.drain()
            self.free_bank(e, 4)
            e.sem_inc(s_v.h, 1)
            s_v.v += 1
        blk.vector(_v1)

        def _sy(e):
            e.wait_ge(s_v.h, s_v.v)
            e.dma_start(out=self.cc_in[0:1, 0:n_src],
                        in_=self.ar_stage[0:1, 0:n_src]).then_inc(s_d.h, 16)
            s_d.v += 16
        blk.sync(_sy)

        def _g2(e):
            e.wait_ge(s_d.h, s_d.v)
            cc = e.collective_compute(
                "AllReduce", ALU.max,
                replica_groups=[list(range(NCORE))],
                ins=[self.cc_in[0:1, 0:n_src].opt()],
                outs=[self.cc_out[0:1, 0:n_src].opt()])
            cc.then_inc(s_g.h, 1)
            s_g.v += 1
        blk.gpsimd(_g2)

        def _sy2(e):
            e.wait_ge(s_g.h, s_g.v)
            e.dma_start(out=self.ar_stage[0:1, 0:n_src],
                        in_=self.cc_out[0:1, 0:n_src]).then_inc(s_d.h, 16)
            s_d.v += 16
        blk.sync(_sy2)

        def _t1(e):
            e.wait_ge(s_d.h, s_d.v)
            self.wait_bank(e, 4)
            mm = e.matmul(self.psP[2][:, 512:512 + n_src], self.ones_row[:],
                          self.ar_stage[0:1, 0:n_src], start=True, stop=True)
            mm.then_inc(s_t.h, 1)
            s_t.v += 1
        blk.tensor(_t1)

        def _v2(e):
            e.wait_ge(s_t.h, s_t.v)
            e.tensor_copy(self.ar_stage[:, 0:n_src], self.psP[2][:, 512:512 + n_src])
            e.drain()
            self.free_bank(e, 4)
            e.sem_inc(s_done.h, 1)
            s_done.v += 1
        blk.vector(_v2)
        return s_done

    def ph_sx_scales(self):
        blk = self.blk
        s_xn = self.sem("l1xn")

        s_done = self.ar_scalar("sxar", [self.amax_a[:]])
        s_sc = self.sem("scd")

        def _v(e):
            e.wait_ge(s_done.h, s_done.v)
            e.tensor_scalar(self.scal["sx"][:], self.ar_stage[:, 0:1], 1.0 / QMAX,
                            1e-12, ALU.mult, ALU.max)
            e.drain()
            e.reciprocal(self.scal["inv_sx"][:], self.scal["sx"][:])
            e.drain()
            e.sem_inc(s_sc.h, 1)
            s_sc.v += 1
        blk.vector(_v)

    def ph_xi(self):
        T, blk = self.T, self.blk
        s_sc = self.sem("scd")
        s_lo = self.sem("l1lo")
        s_xst = [self.sem("xstA"), self.sem("xstB")]
        s_xi = self.sem("l1xm")
        s_ld = [self.sem(f"l1ld{i}") for i in range(2)]
        s_ldf = [self.sem(f"l1ldf{i}") for i in range(2)]
        b_ld = [s.v for s in s_ld]
        b_ldf = [s.v for s in s_ldf]
        self.xiH = self.salloc("e0", "xiH", [128, DCH * T], BF16)
        self.xiL = self.salloc("e0", "xiL", [128, DCH * T], BF16)
        xnc_h = [self.salloc("e0", f"xinh{i}", [128, T], BF16) for i in range(2)]
        xnc_l = [self.salloc("e0", f"xinl{i}", [128, T], BF16) for i in range(2)]
        xi_f = self.salloc("e0", "xi_f", [128, T])

        def _sync(e):
            e.wait_ge(s_xst[0].h, s_xst[0].v)
            e.wait_ge(s_xst[1].h, s_xst[1].v)
            for c in range(DCH):
                sl = c % 2
                if c >= 2:
                    e.wait_ge(s_ldf[sl].h, b_ldf[sl] + c // 2)
                e.dma_start(out=xnc_h[sl][:], in_=self.h_xnh[c]).then_inc(s_ld[sl].h, 16)
                e.dma_start(out=xnc_l[sl][:], in_=self.h_xnl[c]).then_inc(s_ld[sl].h, 16)
                s_ld[sl].v += 32
        blk.sync(_sync)

        def _v(e):
            e.wait_ge(s_sc.h, 1)
            for c in range(DCH):
                sl = c % 2
                cs = slice(c * T, (c + 1) * T)
                e.wait_ge(s_ld[sl].h, b_ld[sl] + 32 * (c // 2 + 1))
                e.tensor_tensor(xi_f[:], xnc_h[sl][:], xnc_l[sl][:], ALU.add)
                e.drain()
                e.sem_inc(s_ldf[sl].h, 1)
                s_ldf[sl].v += 1
                e.tensor_scalar(xi_f[:], xi_f[:], self.scal["inv_sx"][:], RC,
                                ALU.mult, ALU.add)
                e.drain()
                e.tensor_scalar(xi_f[:], xi_f[:], RC, None, ALU.subtract)
                e.drain()
                e.tensor_copy(self.xiH[:, cs], xi_f[:])
                e.drain()
                e.tensor_tensor(self.xiL[:, cs], xi_f[:], self.xiH[:, cs], ALU.subtract)
                e.drain()
                e.sem_inc(s_xi.h, 1)
                s_xi.v += 1
        blk.vector(_v)

    def ph_prep_and_sweep1(self):
        nc, blk, T, TT, NT = self.nc, self.blk, self.T, self.TT, self.NT
        s_xi = self.sem("l1xm")
        s_lo = self.sem("l1lo")

        s_pld = [self.sem(f"ppld{i}") for i in range(2)]
        s_pldf = [self.sem(f"ppldf{i}") for i in range(2)]
        s_pg = self.sem("ppg")
        s_pst = [self.sem("ppst0"), self.sem("ppst1")]
        A = lambda n, s, dt=F32: self.salloc("e0", n, s, dt)
        win = [A(f"ppin{i}", [128, D]) for i in range(2)]
        bmax = A("ppbm", [128, 32])
        sblk = [A(f"pps{i}", [128, 32]) for i in range(2)]
        rblk = [A(f"ppr{i}", [128, 32]) for i in range(2)]
        tq = A("ppt", [128, D])
        wh = [A(f"pph{i}", [128, D], BF16) for i in range(2)]
        wl = [A(f"ppl{i}", [128, D], BF16) for i in range(2)]

        up_jobs = [(self.w_up[ic * 128:(ic + 1) * 128, :],
                    self.h_wuph[ic * 128:(ic + 1) * 128, :],
                    self.h_wupl[ic * 128:(ic + 1) * 128, :], True) for ic in range(ICH)]
        gate_jobs = [(self.w_gate[ic * 128:(ic + 1) * 128, :],
                      self.h_wgh[ic * 128:(ic + 1) * 128, :],
                      self.h_wgl[ic * 128:(ic + 1) * 128, :], True) for ic in range(ICH)]
        aw_jobs = []
        nbase = len(up_jobs) + len(gate_jobs)
        for wname, W in [("wq", self.Wq), ("wq2", self.Wq2), ("wk", self.Wk),
                         ("wk2", self.Wk2), ("wv", self.Wv), ("wv2", self.Wv2),
                         ("wo", self.Wo)]:
            hh = getattr(self, f"h_{wname}h")
            hl = getattr(self, f"h_{wname}l")
            for c in range(DCH):
                aw_jobs.append((W[c * 128:(c + 1) * 128, :], hh[c], hl[c], False))
        self.prep_counts = {"up": len(up_jobs), "gate": nbase,
                            "all": nbase + len(aw_jobs)}

        NQ = 8
        NOC = ICH // NQ
        seq = []
        q_need = {}
        aw_pos = {}

        def _push(jobs):
            p0 = len(seq)
            seq.extend(jobs)
            return list(range(p0, len(seq)))

        hjs = list(range(6, 2 * NQ))
        per = [len(aw_jobs) // len(hjs)] * len(hjs)
        for i in range(len(aw_jobs) - sum(per)):
            per[-1 - i] += 1
        q_need[0] = _push(up_jobs[0:NOC])
        q_need[1] = _push(up_jobs[NOC:2 * NOC])
        awi = 0
        for hj in range(2 * NQ):
            q = hj + 2
            if q < 2 * NQ:
                jl = up_jobs if q < NQ else gate_jobs
                qf = q if q < NQ else q - NQ
                q_need[q] = _push(jl[qf * NOC:(qf + 1) * NOC])
            if hj in hjs:
                n = per[hjs.index(hj)]
                aw_pos[hj] = list(range(awi, awi + n))
                awi += n
            else:
                aw_pos[hj] = []

        s_ald = [self.sem("wdld0"), self.sem("wdld1")]
        s_aldf = self.sem("wdldf0")
        s_ag = self.sem("wdg")
        s_ast = [self.sem("wdst0"), self.sem("wdst1")]
        awin = [A(f"awin{i}", [128, D]) for i in range(2)]
        awh = [A(f"awh{i}", [128, D], BF16) for i in range(2)]
        awl = [A(f"awl{i}", [128, D], BF16) for i in range(2)]

        def emit_aw_load(e, a):
            sl = a % 2
            if a >= 2:
                e.wait_ge(s_aldf.h, a - 1)
            e.dma_start(out=awin[sl][:], in_=aw_jobs[a][0]).then_inc(s_ald[sl].h, 16)
            s_ald[sl].v += 16

        def emit_aw_gp(e, a):
            sl = a % 2
            e.wait_ge(s_ald[sl].h, 16 * (a // 2 + 1))
            if a >= 2:
                e.wait_ge(s_ast[sl].h, 32 * (a // 2))
            e.tensor_copy(awh[sl][:], awin[sl][:])
            e.drain()
            e.tensor_tensor(awl[sl][:], awin[sl][:], awh[sl][:], ALU.subtract)
            e.drain()
            e.sem_inc(s_aldf.h, 1)
            s_aldf.v += 1
            e.sem_inc(s_ag.h, 1)
            s_ag.v += 1

        def emit_aw_store(e, a):
            sl = a % 2
            e.wait_ge(s_ag.h, a + 1)
            e.dma_start(out=aw_jobs[a][1], in_=awh[sl][:]).then_inc(s_ast[sl].h, 16)
            e.dma_start(out=aw_jobs[a][2], in_=awl[sl][:]).then_inc(s_ast[sl].h, 16)
            s_ast[sl].v += 32

        def emit_prep_sync_job(e, j):
            src, dh, dl, lp = seq[j]
            sl = j % 2
            if j >= 2:
                e.wait_ge(s_pldf[sl].h, j // 2)
            e.dma_start(out=win[sl][:], in_=src).then_inc(s_pld[sl].h, 16)
            s_pld[sl].v += 16
            e.wait_ge(s_pg.h, j + 1)
            e.dma_start(out=dh, in_=wh[sl][:]).then_inc(s_pst[sl].h, 16)
            e.dma_start(out=dl, in_=wl[sl][:]).then_inc(s_pst[sl].h, 16)
            s_pst[sl].v += 32

        def emit_prep_dve(e, j):
            src, dh, dl, lp = seq[j]
            sl = j % 2
            e.wait_ge(s_pld[sl].h, 16 * (j // 2 + 1))
            if j >= 2:
                e.wait_ge(s_pst[sl].h, 32 * (j // 2))
            if lp:
                e.tensor_reduce(bmax[:], win[sl][:].rearrange("p (a b) -> p a b", b=32),
                                axis=AX.X, op=ALU.max, apply_absolute_value=True)
                e.drain()
                e.tensor_scalar(sblk[sl][:], bmax[:], 1.0 / 7.0, 1e-12,
                                ALU.mult, ALU.max)
                e.drain()
                e.reciprocal(rblk[sl][:], sblk[sl][:])
                e.drain()
                e.tensor_tensor(tq[:], win[sl][:],
                                rblk[sl][:].rearrange("p (a b) -> p a b", b=1)
                                .broadcast_to([128, 32, 32]), ALU.mult)
                e.drain()
                e.sem_inc(s_pldf[sl].h, 1)
                s_pldf[sl].v += 1
                e.tensor_scalar(tq[:], tq[:], RC, RC, ALU.add, ALU.subtract)
                e.drain()
                e.tensor_tensor(tq[:], tq[:],
                                sblk[sl][:].rearrange("p (a b) -> p a b", b=1)
                                .broadcast_to([128, 32, 32]), ALU.mult)
                e.drain()
                src_dq = tq
            else:
                src_dq = win[sl]
            e.tensor_copy(wh[sl][:], src_dq[:])
            e.drain()
            e.tensor_tensor(wl[sl][:], src_dq[:], wh[sl][:], ALU.subtract)
            e.drain()
            if not lp:
                e.sem_inc(s_pldf[sl].h, 1)
                s_pldf[sl].v += 1
            e.sem_inc(s_pg.h, 1)
            s_pg.v += 1

        NWS = 3
        s_w1 = [self.sem(f"s1w{i}") for i in range(NWS)]
        s_w1f = [self.sem(f"s1wf{i}") for i in range(NWS)]
        s_mm = self.sem("s1mm")
        s_ev = self.sem("s1ev")
        s_st = [self.sem("s1st0"), self.sem("s1st1")]
        HOW = I3 // NQ
        wT_h = [A(f"s1th{i}", [128, DCH * HOW], BF16) for i in range(NWS)]
        wT_l = [A(f"s1tl{i}", [128, DCH * HOW], BF16) for i in range(NWS)]
        ub = [A(f"s1u{i}", [128, TT]) for i in range(2)]
        quarters = [(nm, qf) for nm in ["up", "gate"] for qf in range(NQ)]
        self.s1_total = len(quarters) * NOC * NT

        def emit_s1_wload(e, hj, kcs):
            nm, qf = quarters[hj]
            sl = hj % NWS
            m = max(q_need[hj])
            e.wait_ge(s_pst[0].h, 32 * (m // 2 + 1))
            e.wait_ge(s_pst[1].h, 32 * ((m + 1) // 2))
            if hj >= NWS:
                e.wait_ge(s_w1f[sl].h, hj // NWS)
            hsrc = self.h_wuph if nm == "up" else self.h_wgh
            lsrc = self.h_wupl if nm == "up" else self.h_wgl
            if hj == 0:
                e.wait_ge(s_lo.h, self.TCH)
            for kc in kcs:
                e.dma_start(
                    out=wT_h[sl][:, kc * HOW:(kc + 1) * HOW],
                    in_=hsrc[qf * HOW:(qf + 1) * HOW, kc * 128:(kc + 1) * 128],
                    transpose=True).then_inc(s_w1[sl].h, 16)
                e.dma_start(
                    out=wT_l[sl][:, kc * HOW:(kc + 1) * HOW],
                    in_=lsrc[qf * HOW:(qf + 1) * HOW, kc * 128:(kc + 1) * 128],
                    transpose=True).then_inc(s_w1[sl].h, 16)
                s_w1[sl].v += 32

        KC_SY = list(range(DCH))

        def _sync(e):
            for q in (0, 1):
                for j in q_need[q]:
                    emit_prep_sync_job(e, j)
            emit_s1_wload(e, 0, KC_SY)
            emit_s1_wload(e, 1, KC_SY)
            for hj in range(2 * NQ):
                if hj + 2 < 2 * NQ:
                    for j in q_need[hj + 2]:
                        emit_prep_sync_job(e, j)
                    emit_s1_wload(e, hj + 2, KC_SY)
        blk.sync(_sync)

        def _g(e):
            for hj in range(2 * NQ):
                for a in aw_pos[hj]:
                    emit_aw_gp(e, a)
        blk.gpsimd(_g)

        def emit_s1_evacs(e, hj0, hj1, nmm):
            for hj in range(hj0, hj1):
                nm, qf = quarters[hj]
                acc = self.scal["m_up"] if nm == "up" else self.scal["m_gate"]
                for oc in range(NOC):
                    for tt in range(NT):
                        ubuf = ub[nmm % 2]
                        bk = nmm % 2
                        par = nmm % 2
                        nmm += 1
                        e.wait_ge(s_mm.h, nmm)
                        if nmm > 2:
                            e.wait_ge(s_st[par].h, 16 * ((nmm - 1) // 2))
                        e.tensor_scalar(ubuf[:], self.bank(bk), self.scal["sx"][:],
                                        None, ALU.mult)
                        e.drain()
                        self.free_bank(e, bk)
                        e.tensor_reduce(self.amax_b[:], ubuf[:], axis=AX.X,
                                        op=ALU.max, apply_absolute_value=True)
                        e.drain()
                        if qf == 0 and oc == 0 and tt == 0:
                            e.tensor_copy(acc[:], self.amax_b[:])
                        else:
                            e.tensor_tensor(acc[:], acc[:], self.amax_b[:], ALU.max)
                        e.drain()
                        e.sem_inc(s_ev.h, 1)
                        s_ev.v += 1
                e.sem_inc(s_w1f[hj % NWS].h, 1)
                s_w1f[hj % NWS].v += 1
            return nmm

        def _v(e):
            for q in (0, 1):
                for j in q_need[q]:
                    emit_prep_dve(e, j)
            nmm = 0
            for hj in range(2 * NQ):
                if hj + 2 < 2 * NQ:
                    for j in q_need[hj + 2]:
                        emit_prep_dve(e, j)
                nmm = emit_s1_evacs(e, hj, hj + 1, nmm)
        blk.vector(_v)

        def _t(e):
            e.wait_ge(s_xi.h, s_xi.v)
            nmm = 0
            for hj, (nm, qf) in enumerate(quarters):
                sl = hj % NWS
                e.wait_ge(s_w1[sl].h, 32 * DCH * (hj // NWS + 1))
                for oc in range(NOC):
                    for tt in range(NT):
                        bk = nmm % 2
                        nmm += 1
                        self.wait_bank(e, bk)
                        first = True
                        for kc in range(DCH):
                            lw_h = wT_h[sl][:, kc * HOW + oc * 128: kc * HOW + (oc + 1) * 128]
                            lw_l = wT_l[sl][:, kc * HOW + oc * 128: kc * HOW + (oc + 1) * 128]
                            rx_h = self.xiH[:, kc * T + tt * TT: kc * T + (tt + 1) * TT]
                            rx_l = self.xiL[:, kc * T + tt * TT: kc * T + (tt + 1) * TT]
                            e.matmul(self.bank(bk), lw_h, rx_h, start=first, stop=False)
                            first = False
                            e.matmul(self.bank(bk), lw_h, rx_l, start=False, stop=False)
                            mm = e.matmul(self.bank(bk), lw_l, rx_h, start=False,
                                          stop=(kc == DCH - 1))
                        mm.then_inc(s_mm.h, 1)
                        s_mm.v += 1
        blk.tensor(_t)

        def _a(e):
            nmm = 0
            for hj, (nm, qf) in enumerate(quarters):
                dst = self.h_u_up if nm == "up" else self.h_u_gate
                for oc in range(NOC):
                    ic = qf * NOC + oc
                    for tt in range(NT):
                        nmm += 1
                        e.wait_ge(s_ev.h, nmm)
                        par = (nmm - 1) % 2
                        e.dma_start(out=dst[ic, :, tt * TT:(tt + 1) * TT],
                                    in_=ub[par][:]).then_inc(s_st[par].h, 16)
                        s_st[par].v += 16
                for a in aw_pos[hj]:
                    if a - 2 >= 0:
                        emit_aw_store(e, a - 2)
                    emit_aw_load(e, a)
            for a in (len(aw_jobs) - 2, len(aw_jobs) - 1):
                emit_aw_store(e, a)
        blk.scalar(_a)

        s_done = self.ar_scalar("ugar", [self.scal["m_up"][:], self.scal["m_gate"][:]])
        s_sc2 = self.sem("scd")

        def _v2(e):
            e.wait_ge(s_done.h, s_done.v)
            for j, nm in enumerate(["su", "sg"]):
                e.tensor_scalar(self.scal[nm][:], self.ar_stage[:, j:j + 1], 1.0 / QMAX,
                                1e-12, ALU.mult, ALU.max)
                e.drain()
                e.reciprocal(self.scal["inv_" + nm][:], self.scal[nm][:])
                e.drain()
            e.sem_inc(s_sc2.h, 1)
            s_sc2.v += 1
        blk.vector(_v2)

        if self.debug:
            s_dbg = self.sem("dbg")

            def _dbg(e):
                e.wait_ge(s_st[0].h, s_st[0].v)
                e.wait_ge(s_st[1].h, s_st[1].v)
                for ic in range(ICH):
                    e.dma_start(out=self.d_u[ic], in_=self.h_u_up[ic]
                                ).then_inc(s_dbg.h, 16)
                    s_dbg.v += 16
            blk.scalar(_dbg)

    def attn_iter(self):
        for pi, ph in enumerate(self.attnP):
            pname, gsz = ph[0], ph[7]
            for tt in range(self.NT):
                for st in (0, 1):
                    for grp in self.fo_groups(gsz):
                        yield (pi, pname, tt, st, grp)

    def ph_attn(self):
        nc, blk, T, TT, NT, TSUB = self.nc, self.blk, self.T, self.TT, self.NT, self.TSUB
        s_lo = self.sem("l1lo")
        s_pst = [self.sem("ppst0"), self.sem("ppst1")]
        s_s1st = [self.sem("s1st0"), self.sem("s1st1")]

        s_ws = [self.sem(f"ws{i}") for i in range(2)]
        s_wsf = [self.sem(f"wsf{i}") for i in range(2)]
        self.s_ws, self.s_wsf = s_ws, s_wsf
        self.ws_n_ld = 0
        self.ws_n_use = 0
        self.ws_tot = {}
        self.ws_sltot = [0, 0]
        A = lambda n, s, dt=F32: self.salloc("att", n, s, dt)
        self.wsH = [A(f"wsh{i}", [128, D], BF16) for i in range(2)]
        self.wsL = [A(f"wsl{i}", [128, D], BF16) for i in range(2)]
        self.xsH = [A(f"xsh{i}", [128, self.TT], BF16) for i in range(2)]
        self.xsL = [A(f"xsl{i}", [128, self.TT], BF16) for i in range(2)]

        t1H = A("att1h", [128, DCH * TT], BF16)
        t1L = A("att1l", [128, DCH * TT], BF16)
        qe_fm = A("atqefm", [128, DCH * TT])
        elu_a = A("atelua", [128, TT])
        elu_b = A("atelub", [128, TT])
        tmH = A("attmh", [128, TSUB * D], BF16)
        tmL = A("attml", [128, TSUB * D], BF16)
        qeH = A("atqeh2", [128, TSUB * D], BF16)
        qeL = A("atqel2", [128, TSUB * D], BF16)
        pk1 = A("atpk1", [128, D])
        pk2 = A("atpk2", [128, D])
        qksr = A("atqksr", [128, H])
        vstH = [A(f"atvsth{i}", [128, TT], BF16) for i in range(2)]
        vstL = [A(f"atvstl{i}", [128, TT], BF16) for i in range(2)]
        self.b_qest = [32 * ((self.NT + 1) // 2), 32 * (self.NT // 2)]

        s_mmg = self.sem("atmmg")
        s_ev1 = self.sem("atev1")
        s_qe = self.sem("atqe")
        s_exp = self.sem("atexp")
        s_elu_z = self.sem("atz")
        s_tr = self.sem("attr")
        s_hi = self.sem("athi")
        s_lo2 = self.sem("atlo")
        s_qest = [self.sem("atqestA"), self.sem("atqestB")]
        s_qeld = self.sem("atqeld")
        s_qefree = self.sem("atqefree")
        s_qks = self.sem("atqks")

        self.attnP = [
            ("q", self.h_wqh, self.h_wql, "bq", self.h_wq2h, self.h_wq2l, "bq2", 8),
            ("k", self.h_wkh, self.h_wkl, "bk", self.h_wk2h, self.h_wk2l, "bk2", 6),
            ("v", self.h_wvh, self.h_wvl, "bv", self.h_wv2h, self.h_wv2l, "bv2", 8)]
        psS = self.psP[3][0:DH, :]

        s_ast = [self.sem("wdst0"), self.sem("wdst1")]

        def _sync(e):
            e.wait_ge(s_pst[0].h, s_pst[0].v)
            e.wait_ge(s_pst[1].h, s_pst[1].v)
            e.wait_ge(s_ast[0].h, s_ast[0].v)
            e.wait_ge(s_ast[1].h, s_ast[1].v)
            e.wait_ge(s_s1st[0].h, 16 * ((self.s1_total + 1) // 2))
            e.wait_ge(s_s1st[1].h, 16 * (self.s1_total // 2))
            if self.debug:
                sdbg = self.sem("dbg")
                e.wait_ge(sdbg.h, sdbg.v)
            for (pi, pname, tt, st, grp) in self.attn_iter():
                ph = self.attnP[pi]
                hsrc = ph[1] if st == 0 else ph[4]
                lsrc = ph[2] if st == 0 else ph[5]
                if pname == "k" and st == 0 and grp[0] == 0:
                    e.wait_ge(s_qest[tt % 2].h, 32 * (tt // 2 + 1))
                    if tt >= 1:
                        e.wait_ge(s_qefree.h, 2 * tt)
                    e.dma_start(out=qeH[:], in_=self.h_qeh[tt]).then_inc(s_qeld.h, 16)
                    e.dma_start(out=qeL[:], in_=self.h_qel[tt]).then_inc(s_qeld.h, 16)
                    s_qeld.v += 32
                for kc in range(DCH):
                    n = self.ws_n_ld
                    self.ws_n_ld += 1
                    sl = n % 2
                    if n >= 2:
                        e.wait_ge(s_wsf[sl].h, n // 2)
                    e.dma_start(out=self.wsH[sl][:], in_=hsrc[kc]).then_inc(s_ws[sl].h, 16)
                    e.dma_start(out=self.wsL[sl][:], in_=lsrc[kc]).then_inc(s_ws[sl].h, 16)
                    self.ws_sltot[sl] += 32
                    if st == 0:
                        cs = slice(tt * self.TT, (tt + 1) * self.TT)
                        e.dma_start(out=self.xsH[sl][:], in_=self.h_xnh[kc][:, cs]
                                    ).then_inc(s_ws[sl].h, 16)
                        e.dma_start(out=self.xsL[sl][:], in_=self.h_xnl[kc][:, cs]
                                    ).then_inc(s_ws[sl].h, 16)
                        self.ws_sltot[sl] += 32
                    self.ws_tot[n] = self.ws_sltot[sl]
                    s_ws[sl].v = self.ws_sltot[sl]
        blk.sync(_sync)

        def _t(e):
            e.wait_ge(s_lo.h, self.TCH)

            def take(kc_last):
                n = self.ws_n_use
                self.ws_n_use += 1
                sl = n % 2
                e.wait_ge(s_ws[sl].h, self.ws_tot[n])
                return sl

            cur = [None]

            def mm_group(grp, rhsH, rhsL, col_of):
                for fi, fo in enumerate(grp):
                    self.wait_bank(e, fi)
                last_sl = None
                for kc in range(DCH):
                    sl = take(kc == DCH - 1)
                    last_sl = sl
                    for fi, fo in enumerate(grp):
                        lw_h = self.wsH[sl][:, fo * 128:(fo + 1) * 128]
                        lw_l = self.wsL[sl][:, fo * 128:(fo + 1) * 128]
                        if rhsH is None:
                            rx_h = self.xsH[sl][:]
                            rx_l = self.xsL[sl][:]
                        else:
                            rx_h = rhsH[:, col_of(kc)]
                            rx_l = rhsL[:, col_of(kc)]
                        e.matmul(self.bank(fi), lw_h, rx_h, start=(kc == 0), stop=False)
                        e.matmul(self.bank(fi), lw_h, rx_l, start=False, stop=False)
                        mm = e.matmul(self.bank(fi), lw_l, rx_h, start=False,
                                      stop=(kc == DCH - 1))
                    if kc < DCH - 1:
                        mm.then_inc(s_wsf[sl].h, 1)
                        s_wsf[sl].v += 1
                mm.then_inc(s_mmg.h, 1)
                s_mmg.v += 1
                e.drain()
                e.sem_inc(s_wsf[last_sl].h, 1)
                s_wsf[last_sl].v += 1

            for pi, ph in enumerate(self.attnP):
                pname, gsz = ph[0], ph[7]
                for tt in range(NT):
                    for grp in self.fo_groups(gsz):
                        mm_group(grp, None, None, None)
                    e.wait_ge(s_ev1.h, pi * NT + tt + 1)
                    for grp in self.fo_groups(gsz):
                        mm_group(grp, t1H, t1L,
                                 lambda kc: slice(kc * TT, (kc + 1) * TT))
                    if pname in ("q", "k"):
                        if pname == "k":
                            e.wait_ge(s_qks.h, tt)
                        e.wait_ge(s_qe.h, pi * NT + tt + 1)
                        for ts_ in range(TSUB):
                            bj = ts_ % 2
                            self.wait_bank(e, 2 * bj)
                            self.wait_bank(e, 2 * bj + 1)
                            for c in range(DCH):
                                instr = e.transpose(
                                    self.psP[bj][:, c * 128:(c + 1) * 128],
                                    qe_fm[:, c * TT + ts_ * 128: c * TT + (ts_ + 1) * 128],
                                    self.ident[:])
                            instr.then_inc(s_tr.h, 1)
                            s_tr.v += 1
                    if pname == "k":
                        if tt == 0:
                            self.wait_bank(e, 6)
                            self.wait_bank(e, 7)
                        e.wait_ge(s_lo2.h, (NT + tt) * TSUB + TSUB)
                        e.wait_ge(s_qeld.h, 32 * (tt + 1))
                        for ts_ in range(TSUB):
                            for h in range(H):
                                qh_ = qeH[:, ts_ * D + h * DH: ts_ * D + (h + 1) * DH]
                                ql_ = qeL[:, ts_ * D + h * DH: ts_ * D + (h + 1) * DH]
                                kh_ = tmH[:, ts_ * D + h * DH: ts_ * D + (h + 1) * DH]
                                kl_ = tmL[:, ts_ * D + h * DH: ts_ * D + (h + 1) * DH]
                                po = psS[:, h * DH:(h + 1) * DH]
                                firstS = (tt == 0 and ts_ == 0 and h % 8 == 0)
                                e.matmul(po, kh_, qh_, start=firstS, stop=False,
                                         skip_group_check=True)
                                e.matmul(po, kh_, ql_, start=False, stop=False,
                                         skip_group_check=True)
                                mm = e.matmul(po, kl_, qh_, start=False,
                                              stop=(tt == NT - 1 and ts_ == TSUB - 1),
                                              skip_group_check=True)
                        mm.then_inc(s_qefree.h, 1)
                        s_qefree.v += 1
            e.wait_ge(s_qks.h, NT + 1)
            per = 4
            self.n_rq = 0
            for base in range(0, self.TCH, per):
                self.wait_bank(e, 0)
                for j in range(per):
                    i = base + j
                    if i >= self.TCH:
                        break
                    instr = e.transpose(self.psP[0][0:H, j * 128:(j + 1) * 128],
                                        self.qks_tm[:, i * H:(i + 1) * H], self.ident[:])
                instr.then_inc(s_tr.h, 1)
                s_tr.v += 1
                self.n_rq += 1
        blk.tensor(_t)

        def _v(e):
            ngrp = 0
            for pi, ph in enumerate(self.attnP):
                pname, gsz = ph[0], ph[7]
                bias1 = self.biases[ph[3]]
                bias2 = self.biases[ph[6]]
                for tt in range(NT):
                    for grp in self.fo_groups(gsz):
                        ngrp += 1
                        e.wait_ge(s_mmg.h, ngrp)
                        for fi, fo in enumerate(grp):
                            dst_h = t1H[:, fo * TT:(fo + 1) * TT]
                            dst_l = t1L[:, fo * TT:(fo + 1) * TT]
                            e.tensor_scalar(dst_h, self.bank(fi), bias1[:, fo:fo + 1],
                                            None, ALU.add)
                            e.drain()
                            e.scalar_tensor_tensor(dst_l, self.bank(fi),
                                                   bias1[:, fo:fo + 1], dst_h,
                                                   ALU.add, ALU.subtract)
                            e.drain()
                            self.free_bank(e, fi)
                    e.sem_inc(s_ev1.h, 1)
                    s_ev1.v += 1
                    for grp in self.fo_groups(gsz):
                        ngrp += 1
                        e.wait_ge(s_mmg.h, ngrp)
                        for fi, fo in enumerate(grp):
                            if pname in ("q", "k"):
                                e.tensor_scalar(elu_a[:], self.bank(fi),
                                                bias2[:, fo:fo + 1], 0.0, ALU.add, ALU.min)
                                e.drain()
                                e.sem_inc(s_elu_z.h, 1)
                                s_elu_z.v += 1
                                e.tensor_scalar(elu_b[:], self.bank(fi),
                                                bias2[:, fo:fo + 1], 0.0, ALU.add, ALU.max)
                                e.drain()
                                self.free_bank(e, fi)
                                e.wait_ge(s_exp.h, s_elu_z.v)
                                e.tensor_tensor(qe_fm[:, fo * TT:(fo + 1) * TT],
                                                elu_b[:], elu_a[:], ALU.add)
                                e.drain()
                            else:
                                idx = tt * DCH + fo
                                par = idx % 2
                                if idx >= 2:
                                    e.wait_ge(s_qest[par].h,
                                              self.b_qest[par] + 32 * (idx // 2))
                                dst_h = vstH[par][:]
                                dst_l = vstL[par][:]
                                e.tensor_scalar(dst_h, self.bank(fi), bias2[:, fo:fo + 1],
                                                None, ALU.add)
                                e.drain()
                                e.scalar_tensor_tensor(dst_l, self.bank(fi),
                                                       bias2[:, fo:fo + 1], dst_h,
                                                       ALU.add, ALU.subtract)
                                e.drain()
                                self.free_bank(e, fi)
                                e.sem_inc(s_elu_z.h, 1)
                                s_elu_z.v += 1
                    e.sem_inc(s_qe.h, 1)
                    s_qe.v += 1
                    if pname in ("q", "k"):
                        for ts_ in range(TSUB):
                            bj = ts_ % 2
                            idx = (pi * NT + tt) * TSUB + ts_ + 1
                            e.wait_ge(s_hi.h, idx)
                            e.tensor_tensor(tmL[:, ts_ * D:(ts_ + 1) * D],
                                            self.psP[bj][:, 0:D],
                                            tmH[:, ts_ * D:(ts_ + 1) * D], ALU.subtract)
                            e.drain()
                            self.free_bank(e, 2 * bj)
                            self.free_bank(e, 2 * bj + 1)
                            e.sem_inc(s_lo2.h, 1)
                            s_lo2.v += 1
                    if pname == "k":
                        e.wait_ge(s_qeld.h, 32 * (tt + 1))
                        for ts_ in range(TSUB):
                            qh_ = qeH[:, ts_ * D:(ts_ + 1) * D]
                            ql_ = qeL[:, ts_ * D:(ts_ + 1) * D]
                            kh_ = tmH[:, ts_ * D:(ts_ + 1) * D]
                            kl_ = tmL[:, ts_ * D:(ts_ + 1) * D]
                            e.tensor_tensor(pk1[:], qh_, kh_, ALU.mult)
                            e.tensor_tensor(pk2[:], qh_, kl_, ALU.mult)
                            e.drain()
                            e.tensor_tensor(pk1[:], pk1[:], pk2[:], ALU.add)
                            e.drain()
                            e.tensor_tensor(pk2[:], ql_, kh_, ALU.mult)
                            e.drain()
                            e.tensor_tensor(pk1[:], pk1[:], pk2[:], ALU.add)
                            e.drain()
                            e.tensor_reduce(qksr[:],
                                            pk1[:].rearrange("p (h d) -> p h d", h=H),
                                            axis=AX.X, op=ALU.add)
                            e.drain()
                            i_sub = tt * TSUB + ts_
                            e.tensor_scalar(self.qks_tm[:, i_sub * H:(i_sub + 1) * H],
                                            qksr[:], EPSQ, None, ALU.add)
                            e.drain()
                        e.sem_inc(s_qefree.h, 1)
                        s_qefree.v += 1
                        e.sem_inc(s_qks.h, 1)
                        s_qks.v += 1
                if pname == "k":
                    e.wait_ge(s_qefree.h, 2 * NT)
                    e.tensor_copy(self.S_sb[:], psS[:])
                    e.drain()
                    self.free_bank(e, 6)
                    self.free_bank(e, 7)
                    e.reciprocal(self.qks_tm[:], self.qks_tm[:])
                    e.drain()
                    e.sem_inc(s_qks.h, 1)
                    s_qks.v += 1
            for r in range(self.TCH // 4 + (1 if self.TCH % 4 else 0)):
                e.wait_ge(s_tr.h, 2 * NT * TSUB + r + 1)
                base = r * 4
                n_here = min(4, self.TCH - base)
                e.tensor_copy(self.rq[:, base * 128:(base + n_here) * 128],
                              self.psP[0][0:H, 0:n_here * 128])
                e.drain()
                self.free_bank(e, 0)
        blk.vector(_v)

        def _a(e):
            nz = 0
            for pi, ph in enumerate(self.attnP):
                pname, gsz = ph[0], ph[7]
                if pname == "v":
                    for tt in range(NT):
                        for fo in range(DCH):
                            idx = tt * DCH + fo
                            par = idx % 2
                            nz += 1
                            e.wait_ge(s_elu_z.h, nz)
                            cs = slice(tt * TT, (tt + 1) * TT)
                            e.dma_start(out=self.h_vh[fo][:, cs], in_=vstH[par][:]
                                        ).then_inc(s_qest[par].h, 16)
                            e.dma_start(out=self.h_vl[fo][:, cs], in_=vstL[par][:]
                                        ).then_inc(s_qest[par].h, 16)
                            s_qest[par].v += 32
                    continue
                for tt in range(NT):
                    for grp in self.fo_groups(gsz):
                        for fi, fo in enumerate(grp):
                            nz += 1
                            e.wait_ge(s_elu_z.h, nz)
                            e.activation(elu_a[:], elu_a[:], AF.Exp)
                            e.drain()
                            e.sem_inc(s_exp.h, 1)
                            s_exp.v += 1
                    for ts_ in range(TSUB):
                        bj = ts_ % 2
                        idx = (pi * NT + tt) * TSUB + ts_ + 1
                        if ts_ == 0:
                            if pname == "q" and tt >= 1:
                                pt = (tt - 1) % 2
                                e.wait_ge(s_qest[pt].h, 32 * ((tt - 1) // 2 + 1))
                            if pname == "k" and tt == 0:
                                e.wait_ge(s_qest[0].h, s_qest[0].v)
                                e.wait_ge(s_qest[1].h, s_qest[1].v)
                        e.wait_ge(s_tr.h, idx)
                        e.activation(tmH[:, ts_ * D:(ts_ + 1) * D],
                                     self.psP[bj][:, 0:D], AF.Copy)
                        e.drain()
                        e.sem_inc(s_hi.h, 1)
                        s_hi.v += 1
                    if pname == "q":
                        e.wait_ge(s_lo2.h, tt * TSUB + TSUB)
                        e.dma_start(out=self.h_qeh[tt], in_=tmH[:]
                                    ).then_inc(s_qest[tt % 2].h, 16)
                        e.dma_start(out=self.h_qel[tt], in_=tmL[:]
                                    ).then_inc(s_qest[tt % 2].h, 16)
                        s_qest[tt % 2].v += 32
        blk.scalar(_a)

        s_sar = self.sem("atsar")
        s_sld = self.sem("atsld")
        s_spl = self.sem("atspl")

        def _sy2(e):
            e.wait_ge(s_qks.h, NT + 1)
            e.dma_start(out=self.cc_Sin[:, :], in_=self.S_sb[:]).then_inc(s_sld.h, 16)
            s_sld.v += 16
        blk.sync(_sy2)

        def _g2(e):
            e.wait_ge(s_sld.h, 16)
            cc = e.collective_compute(
                "AllReduce", ALU.add,
                replica_groups=[[0, 1], [2, 3], [4, 5], [6, 7]],
                ins=[self.cc_Sin[:, :].opt()], outs=[self.cc_Sout[:, :].opt()])
            cc.then_inc(s_sar.h, 1)
            s_sar.v += 1
        blk.gpsimd(_g2)

        def _sy3(e):
            e.wait_ge(s_sar.h, 1)
            for c in range(DCH):
                e.dma_start(out=self.Sp_f[0:DH, c * DH:(c + 1) * DH],
                            in_=self.cc_Sout[:, (2 * c) * DH:(2 * c + 1) * DH]
                            ).then_inc(s_sld.h, 16)
                e.dma_start(out=self.Sp_f[DH:128, c * DH:(c + 1) * DH],
                            in_=self.cc_Sout[:, (2 * c + 1) * DH:(2 * c + 2) * DH]
                            ).then_inc(s_sld.h, 16)
                s_sld.v += 32
        blk.sync(_sy3)

        def _v3(e):
            e.wait_ge(s_sld.h, s_sld.v)
            e.tensor_copy(self.SpH[:], self.Sp_f[:])
            e.drain()
            e.tensor_tensor(self.SpL[:], self.Sp_f[:], self.SpH[:], ALU.subtract)
            e.drain()
            e.sem_inc(s_spl.h, 1)
            s_spl.v += 1
        blk.vector(_v3)

        if self.debug:
            s_d = self.sem("dbg")

            def _dbg(e):
                e.wait_ge(s_spl.h, 1)
                for sq_ in (self.sem("atqestA"), self.sem("atqestB")):
                    e.wait_ge(sq_.h, sq_.v)
                e.dma_start(out=self.d_S[:], in_=self.S_sb[:]).then_inc(s_d.h, 16)
                e.dma_start(out=self.d_rq[:], in_=self.rq[:]).then_inc(s_d.h, 16)
                e.dma_start(out=self.d_xnh[:].rearrange("p (c t) -> c p t", c=DCH),
                            in_=self.h_xnh[:, :, :]).then_inc(s_d.h, 16)
                e.dma_start(out=self.d_xnl[:].rearrange("p (c t) -> c p t", c=DCH),
                            in_=self.h_xnl[:, :, :]).then_inc(s_d.h, 16)
                e.dma_start(out=self.d_vh[:].rearrange("p (c t) -> c p t", c=DCH),
                            in_=self.h_vh[:, :, :]).then_inc(s_d.h, 16)
                e.dma_start(out=self.d_vl[:].rearrange("p (c t) -> c p t", c=DCH),
                            in_=self.h_vl[:, :, :]).then_inc(s_d.h, 16)
                s_d.v += 96
            blk.scalar(_dbg)

    def ph_A6A8(self):
        nc, blk, T, TT, NT, TSUB = self.nc, self.blk, self.T, self.TT, self.NT, self.TSUB
        s_spl = self.sem("atspl")

        s_mm6 = self.sem("a8mm6")
        s_rbc = self.sem("a8rbc")
        s_ev6 = self.sem("a8ev6")
        s_mmg8 = self.sem("a8mmg")
        s_oatt = self.sem("a8oatt")
        s_y = self.sem("a8y")
        s_ytr = self.sem("a8ytr")
        s_yev = self.sem("a8yev")
        s_ym = self.sem("a8ym")
        s_sq2 = self.sem("a8sq2")
        s_sv = self.sem("a8sv")
        s_sd = [self.sem("a8sd0"), self.sem("a8sd1")]

        A = lambda n, s, dt=F32: self.salloc("a68", n, s, dt)
        self.gamma_bc = A("gamma_bc", [128, D])
        self.beta_bc = A("beta_bc", [128, D])
        woRH = A("a8worh", [128, DCH * D], BF16)
        woRL = A("a8worl", [128, DCH * D], BF16)
        s_wo = self.sem("a8wo")
        vqH = [A(f"a8vqh{i}", [128, TT], BF16) for i in range(2)]
        vqL = [A(f"a8vql{i}", [128, TT], BF16) for i in range(2)]
        xnsbH = [A(f"a8xnh{i}", [128, DCH * 128], BF16) for i in range(2)]
        xnsbL = [A(f"a8xnl{i}", [128, DCH * 128], BF16) for i in range(2)]
        s_vld = [self.sem("ppld0"), self.sem("ppld1")]
        b_vld = [s.v for s in s_vld]
        s_qefree = self.sem("atqefree")
        b_qefree = s_qefree.v
        s_xl = [self.sem(f"l1ld{i}") for i in range(2)]
        s_xlf = [self.sem(f"l1ldf{i}") for i in range(2)]
        b_xl = [s.v for s in s_xl]
        b_xlf = [s.v for s in s_xlf]
        s_qest = [self.sem("atqestA"), self.sem("atqestB")]
        s_c = self.sem("consts")
        outH = A("a8outh", [128, DCH * TT], BF16)
        outL = A("a8outl", [128, DCH * TT], BF16)
        out_f = A("a8outf", [128, TT])
        rqbc = A("a8rqbc", [128, TT])
        oatt = A("a8oatt", [128, DCH * TT])
        y_fm = A("a8yfm", [128, DCH * 128])
        y_tm = [A(f"a8ytm{i}", [128, D]) for i in range(2)]
        ym = [A(f"a8ym{i}", [128, D]) for i in range(2)]
        sqb = A("a8sqb", [128, D])
        s_tmv = [A(f"a8stm{i}", [128, D]) for i in range(2)]
        st1 = A("a8st1", [128, 1])
        st2 = [A(f"a8st2{i}", [128, 1]) for i in range(2)]

        def _sync(e):
            e.wait_ge(s_qest[0].h, s_qest[0].v)
            e.wait_ge(s_qest[1].h, s_qest[1].v)
            e.wait_ge(s_spl.h, 1)
            for kc in range(DCH):
                e.dma_start(out=woRH[:, kc * D:(kc + 1) * D], in_=self.h_woh[kc]
                            ).then_inc(s_wo.h, 16)
                e.dma_start(out=woRL[:, kc * D:(kc + 1) * D], in_=self.h_wol[kc]
                            ).then_inc(s_wo.h, 16)
                s_wo.v += 32
            e.dma_start(out=self.gamma_bc[:],
                        in_=self.gamma[:].partition_broadcast(128)).then_inc(s_c.h, 16)
            e.dma_start(out=self.beta_bc[:],
                        in_=self.beta[:].partition_broadcast(128)).then_inc(s_c.h, 16)
            s_c.v += 32
            nvq = 0
            for tt in range(NT):
                for c in range(DCH):
                    idxs = tt * DCH + c
                    e.wait_ge(s_qest[idxs % 2].h,
                              self.b_qest[idxs % 2] + 32 * (idxs // 2 + 1))
                    sl = nvq % 2
                    if nvq >= 2:
                        e.wait_ge(s_qefree.h, b_qefree + nvq - 1)
                    nvq += 1
                    cs = slice(tt * TT, (tt + 1) * TT)
                    e.dma_start(out=vqH[sl][:], in_=self.h_vh[c][:, cs]
                                ).then_inc(s_vld[sl].h, 16)
                    e.dma_start(out=vqL[sl][:], in_=self.h_vl[c][:, cs]
                                ).then_inc(s_vld[sl].h, 16)
                    s_vld[sl].v += 32
                for ts_ in range(TSUB):
                    i = tt * TSUB + ts_
                    sl = i % 2
                    if i >= 2:
                        e.wait_ge(s_xlf[sl].h, b_xlf[sl] + i // 2)
                    colv = slice(i * 128, (i + 1) * 128)
                    e.dma_start(out=xnsbH[sl][:].rearrange("p (c t) -> p c t", c=DCH),
                                in_=self.h_xnh[:, :, colv].transpose([1, 0, 2])
                                ).then_inc(s_xl[sl].h, 16)
                    e.dma_start(out=xnsbL[sl][:].rearrange("p (c t) -> p c t", c=DCH),
                                in_=self.h_xnl[:, :, colv].transpose([1, 0, 2])
                                ).then_inc(s_xl[sl].h, 16)
                    s_xl[sl].v += 32
        blk.sync(_sync)

        def _t(e):
            e.wait_ge(s_spl.h, 1)
            for tt in range(NT):
                for c in range(DCH):
                    bk = c % 2
                    nvq = tt * DCH + c
                    vsl = nvq % 2
                    e.wait_ge(s_vld[vsl].h, b_vld[vsl] + 32 * (nvq // 2 + 1))
                    self.wait_bank(e, bk)
                    for half in range(2):
                        p0, p1 = half * DH, half * DH + DH
                        sh = self.SpH[p0:p1, c * DH:(c + 1) * DH]
                        sl_ = self.SpL[p0:p1, c * DH:(c + 1) * DH]
                        vh = vqH[vsl][p0:p1, :]
                        vl = vqL[vsl][p0:p1, :]
                        po = self.bank(bk)[p0:p1, :]
                        e.matmul(po, sh, vh, start=True, stop=False)
                        e.matmul(po, sh, vl, start=False, stop=False)
                        mm = e.matmul(po, sl_, vh, start=False, stop=True)
                    mm.then_inc(s_qefree.h, 1)
                    s_qefree.v += 1
                    mm2_note = None
                    e.drain()
                    e.sem_inc(s_mm6.h, 1)
                    s_mm6.v += 1
                    self.wait_bank(e, 5)
                    mm = e.matmul(self.bank(5), self.selE[:, c * 128:(c + 1) * 128],
                                  self.rq[:, tt * TT:(tt + 1) * TT], start=True, stop=True)
                    mm.then_inc(s_rbc.h, 1)
                    s_rbc.v += 1
                e.wait_ge(s_ev6.h, tt * DCH + DCH)
                if tt == 0:
                    e.wait_ge(s_wo.h, s_wo.v)
                for fo_g in self.fo_groups(2):
                    for fi, fo in enumerate(fo_g):
                        self.wait_bank(e, 2 + fi)
                    for kc in range(DCH):
                        for fi, fo in enumerate(fo_g):
                            lw_h = woRH[:, kc * D + fo * 128:kc * D + (fo + 1) * 128]
                            lw_l = woRL[:, kc * D + fo * 128:kc * D + (fo + 1) * 128]
                            rx_h = outH[:, kc * TT:(kc + 1) * TT]
                            rx_l = outL[:, kc * TT:(kc + 1) * TT]
                            e.matmul(self.bank(2 + fi), lw_h, rx_h, start=(kc == 0),
                                     stop=False)
                            e.matmul(self.bank(2 + fi), lw_h, rx_l, start=False, stop=False)
                            mm = e.matmul(self.bank(2 + fi), lw_l, rx_h, start=False,
                                          stop=(kc == DCH - 1))
                    mm.then_inc(s_mmg8.h, 1)
                    s_mmg8.v += 1
                for ts_ in range(TSUB):
                    i = tt * TSUB + ts_
                    e.wait_ge(s_y.h, i + 1)
                    self.wait_bank(e, 6)
                    self.wait_bank(e, 7)
                    for c in range(DCH):
                        instr = e.transpose(self.psP[3][:, c * 128:(c + 1) * 128],
                                            y_fm[:, c * 128:(c + 1) * 128], self.ident[:])
                    instr.then_inc(s_ytr.h, 1)
                    s_ytr.v += 1
        blk.tensor(_t)

        def _v(e):
            e.wait_ge(s_c.h, s_c.v)
            for tt in range(NT):
                for c in range(DCH):
                    bk = c % 2
                    e.wait_ge(s_mm6.h, tt * DCH + c + 1)
                    e.wait_ge(s_rbc.h, tt * DCH + c + 1)
                    e.tensor_copy(rqbc[:], self.bank(5))
                    e.drain()
                    self.free_bank(e, 5)
                    e.tensor_tensor(out_f[:], self.bank(bk), rqbc[:], ALU.mult)
                    e.drain()
                    self.free_bank(e, bk)
                    e.tensor_copy(outH[:, c * TT:(c + 1) * TT], out_f[:])
                    e.drain()
                    e.tensor_tensor(outL[:, c * TT:(c + 1) * TT], out_f[:],
                                    outH[:, c * TT:(c + 1) * TT], ALU.subtract)
                    e.drain()
                    e.sem_inc(s_ev6.h, 1)
                    s_ev6.v += 1
                for gi, fo_g in enumerate(self.fo_groups(2)):
                    e.wait_ge(s_mmg8.h, tt * 4 + gi + 1)
                    for fi, fo in enumerate(fo_g):
                        e.tensor_scalar(oatt[:, fo * TT:(fo + 1) * TT], self.bank(2 + fi),
                                        self.biases["bo"][:, fo:fo + 1], None, ALU.add)
                        e.drain()
                        self.free_bank(e, 2 + fi)
                e.sem_inc(s_oatt.h, 1)
                s_oatt.v += 1
                for ts_ in range(TSUB):
                    i = tt * TSUB + ts_
                    par = i % 2
                    if i >= 1:
                        e.wait_ge(s_ytr.h, i)
                    e.wait_ge(s_xl[par].h, b_xl[par] + 32 * (i // 2 + 1))
                    oat_v = oatt[:].rearrange("p (c t) -> p c t", c=DCH)[
                        :, :, ts_ * 128:(ts_ + 1) * 128]
                    yv = y_fm[:].rearrange("p (c t) -> p c t", c=DCH)
                    e.tensor_tensor(yv, oat_v, xnsbH[par][:]
                                    .rearrange("p (c t) -> p c t", c=DCH), ALU.add)
                    e.drain()
                    e.tensor_tensor(yv, yv, xnsbL[par][:]
                                    .rearrange("p (c t) -> p c t", c=DCH), ALU.add)
                    e.drain()
                    e.sem_inc(s_xlf[par].h, 1)
                    s_xlf[par].v += 1
                    e.sem_inc(s_y.h, 1)
                    s_y.v += 1
                    e.wait_ge(s_yev.h, i + 1)
                    e.tensor_reduce(st1[:], y_tm[par][:], axis=AX.X, op=ALU.add)
                    e.drain()
                    e.tensor_scalar(st1[:], st1[:], 1.0 / D, None, ALU.mult)
                    e.drain()
                    if i >= 2:
                        e.wait_ge(s_sq2.h, i - 1)
                    e.tensor_scalar(ym[par][:], y_tm[par][:], st1[:], None, ALU.subtract)
                    e.drain()
                    e.sem_inc(s_ym.h, 1)
                    s_ym.v += 1
                    e.wait_ge(s_sq2.h, i + 1)
                    e.reciprocal(st1[:], st2[par][:])
                    e.drain()
                    if i >= 2:
                        e.wait_ge(s_sd[par].h, 16 * (i // 2))
                    e.scalar_tensor_tensor(s_tmv[par][:], ym[par][:], st1[:],
                                           self.gamma_bc[:], ALU.mult, ALU.mult)
                    e.drain()
                    e.tensor_tensor(s_tmv[par][:], s_tmv[par][:], self.beta_bc[:], ALU.add)
                    e.drain()
                    e.sem_inc(s_sv.h, 1)
                    s_sv.v += 1
        blk.vector(_v)

        def _a(e):
            for tt in range(NT):
                for ts_ in range(TSUB):
                    i = tt * TSUB + ts_
                    par = i % 2
                    e.wait_ge(s_ytr.h, i + 1)
                    if i >= 2:
                        e.wait_ge(s_ym.h, i - 1)
                    e.activation(y_tm[par][:], self.psP[3][:, 0:D], AF.Copy)
                    e.drain()
                    self.free_bank(e, 6)
                    self.free_bank(e, 7)
                    e.sem_inc(s_yev.h, 1)
                    s_yev.v += 1
                    e.wait_ge(s_ym.h, i + 1)
                    if i >= 2:
                        e.wait_ge(s_sv.h, i - 1)
                    e.activation(sqb[:], ym[par][:], AF.Square, accum_out=st2[par][:])
                    e.drain()
                    e.activation(st2[par][:], st2[par][:], AF.Sqrt,
                                 bias=self.c_eps2[:], scale=1.0 / D)
                    e.drain()
                    e.sem_inc(s_sq2.h, 1)
                    s_sq2.v += 1
                    e.wait_ge(s_sv.h, i + 1)
                    e.dma_start(out=self.h_s[i], in_=s_tmv[par][:]
                                ).then_inc(s_sd[par].h, 16)
                    s_sd[par].v += 16
        blk.scalar(_a)

        if self.debug:
            s_d = self.sem("dbg")

            def _dbg(e):
                e.wait_ge(s_sd[0].h, s_sd[0].v)
                e.wait_ge(s_sd[1].h, s_sd[1].v)
                for i in range(self.TCH):
                    e.dma_start(out=self.d_s[i], in_=self.h_s[i]).then_inc(s_d.h, 16)
                    s_d.v += 16
            blk.sync(_dbg)

    def ph_sweep2(self):
        nc, blk, T = self.nc, self.blk, self.T
        s_sc2 = self.sem("scd")
        s_sd = [self.sem("a8sd0"), self.sem("a8sd1")]
        s_ld = [self.sem(f"l1ld{i}") for i in range(2)]
        s_ldf = [self.sem(f"l1ldf{i}") for i in range(2)]
        b_ld = [s.v for s in s_ld]
        b_ldf = [s.v for s in s_ldf]
        s_v = self.sem("s2v")
        s_sig = self.sem("s2sig")
        s_done = self.sem("s2done")
        s_st = self.sem("s2st")
        A = lambda n, s, dt=F32: self.salloc("s23", n, s, dt)
        ubuf = [A(f"s2u{i}", [128, T]) for i in range(2)]
        gbuf = [A(f"s2g{i}", [128, T]) for i in range(2)]
        t1b = A("s2t1", [128, T])
        uq16 = A("s2uq", [128, T], I16)
        g16 = A("s2g16", [128, T])
        gfp = A("s2gfp", [128, T])
        sig = A("s2sig", [128, T])
        sigq = A("s2sigq", [128, T], U16)
        pbuf = A("s2p", [128, T])
        junk = A("s2junk", [128, 1])

        def _sync(e):
            e.wait_ge(s_sc2.h, 2)
            e.wait_ge(s_sd[0].h, 16 * ((self.TCH + 1) // 2))
            e.wait_ge(s_sd[1].h, 16 * (self.TCH // 2))
            if self.debug:
                sdbg = self.sem("dbg")
                e.wait_ge(sdbg.h, sdbg.v)
            for ic in range(ICH):
                sl = ic % 2
                if ic >= 2:
                    e.wait_ge(s_ldf[sl].h, b_ldf[sl] + ic // 2)
                e.dma_start(out=ubuf[sl][:], in_=self.h_u_up[ic]).then_inc(s_ld[sl].h, 16)
                e.dma_start(out=gbuf[sl][:], in_=self.h_u_gate[ic]).then_inc(s_ld[sl].h, 16)
                s_ld[sl].v += 32
        blk.sync(_sync)

        def _v(e):
            for ic in range(ICH):
                sl = ic % 2
                e.wait_ge(s_ld[sl].h, b_ld[sl] + 32 * (ic // 2 + 1))
                e.tensor_scalar(t1b[:], ubuf[sl][:], self.scal["inv_su"][:], RC,
                                ALU.mult, ALU.add)
                e.drain()
                if ic >= 1:
                    e.wait_ge(s_st.h, 32 * ic)
                e.tensor_scalar(uq16[:], t1b[:], RC, None, ALU.subtract)
                e.drain()
                e.tensor_scalar(t1b[:], gbuf[sl][:], self.scal["inv_sg"][:], RC,
                                ALU.mult, ALU.add)
                e.drain()
                e.sem_inc(s_ldf[sl].h, 1)
                s_ldf[sl].v += 1
                e.tensor_scalar(g16[:], t1b[:], RC, None, ALU.subtract)
                e.drain()
                if ic >= 1:
                    e.wait_ge(s_sig.h, ic)
                e.tensor_scalar(gfp[:], g16[:], self.scal["sg"][:], None, ALU.mult)
                e.drain()
                e.sem_inc(s_v.h, 1)
                s_v.v += 1
                e.wait_ge(s_sig.h, ic + 1)
                e.tensor_scalar(t1b[:], sig[:], 65536.0, RC, ALU.mult, ALU.add)
                e.drain()
                e.tensor_scalar(sigq[:], t1b[:], RC, 65535.0, ALU.subtract, ALU.min)
                e.drain()
                e.tensor_tensor(pbuf[:], gfp[:], sigq[:], ALU.mult)
                e.drain()
                e.tensor_scalar(pbuf[:], pbuf[:], 1.0 / 65536.0, None, ALU.mult)
                e.drain()
                e.tensor_reduce(self.amax_b[:], pbuf[:], axis=AX.X, op=ALU.max, apply_absolute_value=True)
                e.drain()
                if ic == 0:
                    e.tensor_copy(self.scal["m_p"][:], self.amax_b[:])
                else:
                    e.tensor_tensor(self.scal["m_p"][:], self.scal["m_p"][:],
                                    self.amax_b[:], ALU.max)
                e.drain()
                e.sem_inc(s_done.h, 1)
                s_done.v += 1
        blk.vector(_v)

        def _a(e):
            for ic in range(ICH):
                e.wait_ge(s_v.h, ic + 1)
                e.activation(sig[:], gfp[:], AF.Sigmoid)
                e.drain()
                e.sem_inc(s_sig.h, 1)
                s_sig.v += 1
                e.wait_ge(s_done.h, ic + 1)
                e.dma_start(out=self.h_uq[ic], in_=uq16[:]).then_inc(s_st.h, 16)
                e.dma_start(out=self.h_p[ic], in_=pbuf[:]).then_inc(s_st.h, 16)
                s_st.v += 32
        blk.scalar(_a)

        s_ardone = self.ar_scalar("g2ar", [self.scal["m_p"][:]])
        s_sc3 = self.sem("scd")

        def _v2(e):
            e.wait_ge(s_ardone.h, s_ardone.v)
            e.tensor_scalar(self.scal["sg2"][:], self.ar_stage[:, 0:1], 1.0 / QMAX,
                            1e-12, ALU.mult, ALU.max)
            e.drain()
            e.reciprocal(self.scal["inv_sg2"][:], self.scal["sg2"][:])
            e.drain()
            e.sem_inc(s_sc3.h, 1)
            s_sc3.v += 1
        blk.vector(_v2)

        if self.debug:
            s_d = self.sem("dbg")

            def _dbg(e):
                e.wait_ge(s_st.h, s_st.v)
                for ic in range(ICH):
                    e.dma_start(out=self.d_uq[ic], in_=self.h_uq[ic]).then_inc(s_d.h, 16)
                    e.dma_start(out=self.d_p[ic], in_=self.h_p[ic]).then_inc(s_d.h, 16)
                    s_d.v += 32
            blk.scalar(_dbg)

    def ph_sweep3_and_wdprep(self):
        nc, blk, T = self.nc, self.blk, self.T
        s_sc3 = self.sem("scd")
        s_st2 = self.sem("s2st")
        s_ld = [self.sem(f"ppld{i}") for i in range(2)]
        s_ldf = [self.sem(f"ppldf{i}") for i in range(2)]
        b_ld = [s.v for s in s_ld]
        b_ldf = [s.v for s in s_ldf]
        s_v = self.sem("s3v")
        s_st = self.sem("s3st")
        A = lambda n, s, dt=F32: self.salloc("s23", n, s, dt)
        p3 = [A(f"s3p{i}", [128, T]) for i in range(2)]
        uq3 = [A(f"s3uq{i}", [128, T], I16) for i in range(2)]
        g2b = A("s3g2", [128, T])
        pob = A("s3po", [128, T])
        junk = A("s3junk", [128, 1])

        s_dld = [self.sem(f"wdld{i}") for i in range(2)]
        s_dldf = [self.sem(f"wdldf{i}") for i in range(2)]
        s_dg = self.sem("wdg")
        s_dst = [self.sem("wdst0"), self.sem("wdst1")]
        b_dld = [s.v for s in s_dld]
        b_dldf = [s.v for s in s_dldf]
        b_dg = s_dg.v
        b_dst = [s.v for s in s_dst]
        dwin = [A(f"wdin{i}", [128, D]) for i in range(2)]
        dbm = A("wdbm", [128, 32])
        dsb = [A(f"wds{i}", [128, 32]) for i in range(2)]
        drb = [A(f"wdr{i}", [128, 32]) for i in range(2)]
        dtq = A("wdt", [128, D])
        ddq = A("wddq", [128, D])
        dwh = [A(f"wdh{i}", [128, D], BF16) for i in range(2)]
        dwl = [A(f"wdl{i}", [128, D], BF16) for i in range(2)]
        djobs = [(c, s3) for c in range(DCH) for s3 in range(3)]
        self.n_djobs = len(djobs)

        def _sync(e):
            e.wait_ge(s_sc3.h, 3)
            for ic in range(ICH):
                sl = ic % 2
                e.wait_ge(s_st2.h, 32 * (ic + 1))
                if ic >= 2:
                    e.wait_ge(s_ldf[sl].h, b_ldf[sl] + ic // 2)
                e.dma_start(out=p3[sl][:], in_=self.h_p[ic]).then_inc(s_ld[sl].h, 16)
                e.dma_start(out=uq3[sl][:], in_=self.h_uq[ic]).then_inc(s_ld[sl].h, 16)
                s_ld[sl].v += 32
                c, s3 = djobs[ic]
                if ic >= 2:
                    e.wait_ge(s_dldf[sl].h, b_dldf[sl] + ic // 2)
                e.dma_start(out=dwin[sl][:],
                            in_=self.w_down[c * 128:(c + 1) * 128, s3 * D:(s3 + 1) * D]
                            ).then_inc(s_dld[sl].h, 16)
                s_dld[sl].v += 16
        blk.sync(_sync)

        def _v(e):
            for ic in range(ICH):
                sl = ic % 2
                e.wait_ge(s_ld[sl].h, b_ld[sl] + 32 * (ic // 2 + 1))
                e.tensor_scalar(g2b[:], p3[sl][:], self.scal["inv_sg2"][:], RC,
                                ALU.mult, ALU.add)
                e.drain()
                e.tensor_scalar(g2b[:], g2b[:], RC, None, ALU.subtract)
                e.drain()
                if ic >= 1:
                    e.wait_ge(s_st.h, 16 * ic)
                e.tensor_tensor(pob[:], g2b[:], uq3[sl][:], ALU.mult)
                e.drain()
                e.sem_inc(s_ldf[sl].h, 1)
                s_ldf[sl].v += 1
                e.tensor_reduce(self.amax_b[:], pob[:], axis=AX.X, op=ALU.max, apply_absolute_value=True)
                e.drain()
                if ic == 0:
                    e.tensor_copy(self.scal["m_po"][:], self.amax_b[:])
                else:
                    e.tensor_tensor(self.scal["m_po"][:], self.scal["m_po"][:],
                                    self.amax_b[:], ALU.max)
                e.drain()
                e.sem_inc(s_v.h, 1)
                s_v.v += 1
                e.wait_ge(s_dld[sl].h, b_dld[sl] + 16 * (ic // 2 + 1))
                e.tensor_reduce(dbm[:], dwin[sl][:].rearrange("p (a b) -> p a b", b=32),
                                axis=AX.X, op=ALU.max, apply_absolute_value=True)
                e.drain()
                e.tensor_scalar(dsb[sl][:], dbm[:], 1.0 / 7.0, 1e-12,
                                ALU.mult, ALU.max)
                e.drain()
                e.reciprocal(drb[sl][:], dsb[sl][:])
                e.drain()
                e.tensor_tensor(dtq[:], dwin[sl][:],
                                drb[sl][:].rearrange("p (a b) -> p a b", b=1)
                                .broadcast_to([128, 32, 32]), ALU.mult)
                e.drain()
                e.sem_inc(s_dldf[sl].h, 1)
                s_dldf[sl].v += 1
                e.tensor_scalar(dtq[:], dtq[:], RC, RC, ALU.add, ALU.subtract)
                e.drain()
                e.tensor_tensor(ddq[:], dtq[:],
                                dsb[sl][:].rearrange("p (a b) -> p a b", b=1)
                                .broadcast_to([128, 32, 32]), ALU.mult)
                e.drain()
                if ic >= 2:
                    e.wait_ge(s_dst[sl].h, b_dst[sl] + 32 * (ic // 2))
                e.tensor_copy(dwh[sl][:], ddq[:])
                e.drain()
                e.tensor_tensor(dwl[sl][:], ddq[:], dwh[sl][:], ALU.subtract)
                e.drain()
                e.sem_inc(s_dg.h, 1)
                s_dg.v += 1
        blk.vector(_v)

        def _a(e):
            for ic in range(ICH):
                e.wait_ge(s_v.h, ic + 1)
                e.dma_start(out=self.h_po[ic], in_=pob[:]).then_inc(s_st.h, 16)
                s_st.v += 16
                c, s3 = djobs[ic]
                sl = ic % 2
                e.wait_ge(s_dg.h, b_dg + ic + 1)
                e.dma_start(out=self.h_wdnh[c * 128:(c + 1) * 128, s3 * D:(s3 + 1) * D],
                            in_=dwh[sl][:]).then_inc(s_dst[sl].h, 16)
                e.dma_start(out=self.h_wdnl[c * 128:(c + 1) * 128, s3 * D:(s3 + 1) * D],
                            in_=dwl[sl][:]).then_inc(s_dst[sl].h, 16)
                s_dst[sl].v += 32
        blk.scalar(_a)

        s_ardone = self.ar_scalar("oar", [self.scal["m_po"][:]])
        s_sc4 = self.sem("scd")

        def _v2(e):
            e.wait_ge(s_ardone.h, s_ardone.v)
            e.tensor_scalar(self.scal["tmp0"][:], self.ar_stage[:, 0:1],
                            self.scal["sg2"][:], self.scal["su"][:], ALU.mult, ALU.mult)
            e.drain()
            e.tensor_scalar(self.scal["so"][:], self.scal["tmp0"][:], 1.0 / QMAX,
                            1e-12, ALU.mult, ALU.max)
            e.drain()
            e.reciprocal(self.scal["inv_so"][:], self.scal["so"][:])
            e.drain()
            e.tensor_scalar(self.scal["c3"][:], self.scal["inv_so"][:],
                            self.scal["sg2"][:], self.scal["su"][:], ALU.mult, ALU.mult)
            e.drain()
            e.sem_inc(s_sc4.h, 1)
            s_sc4.v += 1
        blk.vector(_v2)

        if self.debug:
            s_d = self.sem("dbg")

            def _dbg(e):
                e.wait_ge(s_st.h, s_st.v)
                for ic in range(ICH):
                    e.dma_start(out=self.d_po[ic], in_=self.h_po[ic]).then_inc(s_d.h, 16)
                    s_d.v += 16
            blk.scalar(_dbg)

    def ph_sweep4(self):
        nc, blk, T, TT, NT, TSUB = self.nc, self.blk, self.T, self.TT, self.NT, self.TSUB
        s_sc4 = self.sem("scd")
        s_dst = [self.sem("wdst0"), self.sem("wdst1")]
        s_st3 = self.sem("s3st")
        s_sd = [self.sem("a8sd0"), self.sem("a8sd1")]

        s_wd = self.sem("s4wd")
        s_po = [self.sem(f"l1ld{i}") for i in range(2)]
        s_pof = [self.sem(f"l1ldf{i}") for i in range(2)]
        b_po = [s.v for s in s_po]
        b_pof = [s.v for s in s_pof]
        s_oi = self.sem("l1xn")
        b_oi = s_oi.v
        s_mm = self.sem("s4mm")
        s_act = self.sem("l1sq")
        b_act = s_act.v
        s_sxld = [self.sem("s4sxA"), self.sem("s4sxB")]
        s_sgv = self.sem("s4sgv")
        s_tr = self.sem("s4tr")
        s_out = [self.sem("s4outA"), self.sem("s4outB")]

        A = lambda n, s, dt=F32: self.salloc("s4", n, s, dt)
        wdTH = A("s4wdh", [128, ICH * D], BF16)
        wdTL = A("s4wdl", [128, ICH * D], BF16)
        pobuf = [A(f"s4pob{i}", [128, TT]) for i in range(2)]
        oiH = A("s4oih", [128, ICH * TT], BF16)
        oiL = A("s4oil", [128, ICH * TT], BF16)
        oi_f = A("s4oif", [128, TT])
        gbuf = A("s4gb", [128, DCH * TT])
        s_tmb = [A(f"s4stm{i}", [128, D]) for i in range(2)]
        x_tmb = [A(f"s4xtm{i}", [128, D]) for i in range(2)]

        def _sync(e):
            e.wait_ge(s_dst[0].h, s_dst[0].v)
            e.wait_ge(s_dst[1].h, s_dst[1].v)
            e.wait_ge(s_st3.h, 16 * ICH)
            if self.debug:
                sdbg = self.sem("dbg")
                e.wait_ge(sdbg.h, sdbg.v)
            for ic in range(ICH):
                e.dma_start(out=wdTH[:, ic * D:(ic + 1) * D],
                            in_=self.h_wdnh[:, ic * 128:(ic + 1) * 128],
                            transpose=True).then_inc(s_wd.h, 16)
                e.dma_start(out=wdTL[:, ic * D:(ic + 1) * D],
                            in_=self.h_wdnl[:, ic * 128:(ic + 1) * 128],
                            transpose=True).then_inc(s_wd.h, 16)
                s_wd.v += 32
            e.wait_ge(s_sc4.h, 4)
            for tt in range(NT):
                for ic in range(ICH):
                    sl = ic % 2
                    j = tt * ICH + ic
                    if j >= 2:
                        e.wait_ge(s_pof[sl].h, b_pof[sl] + j // 2)
                    e.dma_start(out=pobuf[sl][:],
                                in_=self.h_po[ic, :, tt * TT:(tt + 1) * TT]
                                ).then_inc(s_po[sl].h, 16)
                    s_po[sl].v += 16
                for ts_ in range(TSUB):
                    i = tt * TSUB + ts_
                    par = i % 2
                    e.wait_ge(s_sd[par].h, 16 * (i // 2 + 1))
                    if i >= 2:
                        e.wait_ge(s_sgv.h, i - 1)
                        e.wait_ge(s_out[par].h, 16 * (i // 2))
                    e.dma_start(out=s_tmb[par][:], in_=self.h_s[i]
                                ).then_inc(s_sxld[par].h, 16)
                    e.dma_start(out=x_tmb[par][:],
                                in_=self.x_in[i * 128:(i + 1) * 128, :]
                                ).then_inc(s_sxld[par].h, 16)
                    s_sxld[par].v += 32
        blk.sync(_sync)

        def _v(e):
            for tt in range(NT):
                for ic in range(ICH):
                    sl = ic % 2
                    j = tt * ICH + ic
                    e.wait_ge(s_po[sl].h, b_po[sl] + 16 * (j // 2 + 1))
                    e.tensor_scalar(oi_f[:], pobuf[sl][:], self.scal["c3"][:], RC,
                                    ALU.mult, ALU.add)
                    e.drain()
                    e.sem_inc(s_pof[sl].h, 1)
                    s_pof[sl].v += 1
                    e.tensor_scalar(oi_f[:], oi_f[:], RC, None, ALU.subtract)
                    e.drain()
                    if tt >= 1 and ic == 0:
                        e.wait_ge(s_mm.h, tt * DCH)
                    e.tensor_copy(oiH[:, ic * TT:(ic + 1) * TT], oi_f[:])
                    e.drain()
                    e.tensor_tensor(oiL[:, ic * TT:(ic + 1) * TT], oi_f[:],
                                    oiH[:, ic * TT:(ic + 1) * TT], ALU.subtract)
                    e.drain()
                    e.sem_inc(s_oi.h, 1)
                    s_oi.v += 1
                for ts_ in range(TSUB):
                    i = tt * TSUB + ts_
                    par = i % 2
                    e.wait_ge(s_tr.h, i + 1)
                    e.wait_ge(s_sxld[par].h, 32 * (i // 2 + 1))
                    e.tensor_tensor(s_tmb[par][:], self.psP[3][:, 0:D], s_tmb[par][:],
                                    ALU.mult)
                    e.drain()
                    self.free_bank(e, 6)
                    self.free_bank(e, 7)
                    e.tensor_tensor(s_tmb[par][:], s_tmb[par][:], x_tmb[par][:], ALU.add)
                    e.drain()
                    e.sem_inc(s_sgv.h, 1)
                    s_sgv.v += 1
        blk.vector(_v)

        def _t(e):
            e.wait_ge(s_wd.h, 32 * ICH)
            for tt in range(NT):
                for fo in range(DCH):
                    bk = fo % 2
                    e.wait_ge(s_oi.h, b_oi + tt * ICH + ICH)
                    self.wait_bank(e, bk)
                    first = True
                    for ic in range(ICH):
                        lw_h = wdTH[:, ic * D + fo * 128: ic * D + (fo + 1) * 128]
                        lw_l = wdTL[:, ic * D + fo * 128: ic * D + (fo + 1) * 128]
                        rx_h = oiH[:, ic * TT:(ic + 1) * TT]
                        rx_l = oiL[:, ic * TT:(ic + 1) * TT]
                        e.matmul(self.bank(bk), lw_h, rx_h, start=first, stop=False)
                        first = False
                        e.matmul(self.bank(bk), lw_h, rx_l, start=False, stop=False)
                        mm = e.matmul(self.bank(bk), lw_l, rx_h, start=False,
                                      stop=(ic == ICH - 1))
                    mm.then_inc(s_mm.h, 1)
                    s_mm.v += 1
                for ts_ in range(TSUB):
                    e.wait_ge(s_act.h, b_act + tt * DCH + DCH)
                    self.wait_bank(e, 6)
                    self.wait_bank(e, 7)
                    for c in range(DCH):
                        instr = e.transpose(
                            self.psP[3][:, c * 128:(c + 1) * 128],
                            gbuf[:, c * TT + ts_ * 128: c * TT + (ts_ + 1) * 128],
                            self.ident[:])
                    instr.then_inc(s_tr.h, 1)
                    s_tr.v += 1
        blk.tensor(_t)

        def _a(e):
            for tt in range(NT):
                for fo in range(DCH):
                    bk = fo % 2
                    e.wait_ge(s_mm.h, tt * DCH + fo + 1)
                    if tt >= 1 and fo == 0:
                        e.wait_ge(s_tr.h, tt * TSUB)
                    e.activation(gbuf[:, fo * TT:(fo + 1) * TT], self.bank(bk), AF.Silu,
                                 scale=self.scal["so"][:])
                    e.drain()
                    self.free_bank(e, bk)
                    e.sem_inc(s_act.h, 1)
                    s_act.v += 1
                for ts_ in range(TSUB):
                    i = tt * TSUB + ts_
                    par = i % 2
                    e.wait_ge(s_sgv.h, i + 1)
                    e.dma_start(out=self.out[i * 128:(i + 1) * 128, :],
                                in_=s_tmb[par][:]).then_inc(s_out[par].h, 16)
                    s_out[par].v += 16
        blk.scalar(_a)

    def ph_final_wait(self):
        blk = self.blk
        s_out = [self.sem("s4outA"), self.sem("s4outB")]

        def _sync(e):
            e.wait_ge(s_out[0].h, s_out[0].v)
            e.wait_ge(s_out[1].h, s_out[1].v)
        blk.sync(_sync)

        if self.debug:
            s_fd = self.sem("dbg")

            def _a(e):
                e.wait_ge(s_out[0].h, s_out[0].v)
                e.wait_ge(s_out[1].h, s_out[1].v)
                for j, nm in enumerate(["sx", "su", "sg", "sg2", "so"]):
                    e.dma_start(out=self.d_scal[:, j:j + 1], in_=self.scal[nm][:],
                                allow_slow_non_contiguous=True).then_inc(s_fd.h, 16)
                    s_fd.v += 16
                e.wait_ge(s_fd.h, s_fd.v)
            blk.scalar(_a)


def _make_consts():
    ident = np.eye(128, dtype=np.float32)
    selE = np.zeros((DCH, H, 128), np.float32)
    for c in range(DCH):
        selE[c, 2 * c, 0:64] = 1.0
        selE[c, 2 * c + 1, 64:128] = 1.0
    return ident, selE


def _in_maps(inputs, T):
    ident, selE = _make_consts()
    x = np.ascontiguousarray(np.asarray(inputs["L0_attn_x"], np.float32))
    base = {
        "Wq": inputs["L0_attn_Wq"], "Wk": inputs["L0_attn_Wk"], "Wv": inputs["L0_attn_Wv"],
        "Wq2": inputs["L0_attn_Wq2"], "Wk2": inputs["L0_attn_Wk2"],
        "Wv2": inputs["L0_attn_Wv2"], "Wo": inputs["L0_attn_Wo"],
        "bq": inputs["L0_attn_bq"], "bk": inputs["L0_attn_bk"], "bv": inputs["L0_attn_bv"],
        "bq2": inputs["L0_attn_bq2"], "bk2": inputs["L0_attn_bk2"],
        "bv2": inputs["L0_attn_bv2"], "bo": inputs["L0_attn_bo"],
        "gamma": inputs["L0_attn_gamma"], "beta": inputs["L0_attn_beta"],
        "w_gate": inputs["L0_ffn_w_gate"], "w_up": inputs["L0_ffn_w_up"],
        "w_down": inputs["L0_ffn_w_down"],
        "ident": ident, "selE": selE,
    }
    base = {k: np.ascontiguousarray(np.asarray(v, np.float32)) for k, v in base.items()}
    maps = []
    for c in range(NCORE):
        m = dict(base)
        m["x"] = np.ascontiguousarray(x[c // 2, (c % 2) * T:(c % 2 + 1) * T, :])
        maps.append(m)
    return maps


def kernel(**inputs):
    T, TT = 2048, 512
    b = Builder(T=T, TT=TT, debug=False)
    nc = b.build()
    in_maps = _in_maps(inputs, T)
    res = run_bass_kernel_spmd(nc, in_maps, core_ids=list(range(NCORE)))
    out = np.empty((4, 4096, D), np.float32)
    for c in range(NCORE):
        out[c // 2, (c % 2) * T:(c % 2 + 1) * T, :] = np.asarray(res.results[c]["out"])
    return out


# version 12
# speedup vs baseline: 1.0417x; 1.0180x over previous
import contextlib
import numpy as np
from concourse import library_config
import concourse.bass as bass
import concourse.mybir as mybir
from bass_rust import ReduceOp
from concourse.bass_utils import run_bass_kernel_spmd

F32 = mybir.dt.float32
BF16 = mybir.dt.bfloat16
I16 = mybir.dt.int16
U16 = mybir.dt.uint16
ALU = mybir.AluOpType
AF = mybir.ActivationFunctionType
AX = mybir.AxisListType

RC = 12582912.0
QMAX = 32767.0
NCORE = 8
D = 1024
DCH = 8
H = 16
DH = 64
I3 = 3072
ICH = 24
EPS1 = 1e-5
EPS2 = 1e-3
EPSQ = 1e-6


class Sem:
    def __init__(self, nc, name):
        self.h = nc.alloc_semaphore(name)
        self.v = 0


class Builder:
    def __init__(self, T=2048, TT=512, debug=False):
        assert T % TT == 0 and TT % 128 == 0
        self.T = T
        self.TT = TT
        self.NT = T // TT
        self.TSUB = TT // 128
        self.TCH = T // 128
        self.debug = debug
        self.nc = bass.Bass("TRN2", target_bir_lowering=False)
        self.sems = {}
        self.es = {}

    def sem(self, name):
        if name not in self.sems:
            self.sems[name] = Sem(self.nc, f"s_{name}")
        return self.sems[name]

    def scope(self, name):
        self.es[name] = contextlib.ExitStack()

    def salloc(self, scope, name, shape, dt=F32):
        return self.es[scope].enter_context(self.nc.sbuf_tensor(name, shape, dt))

    def close_scope(self, name):
        self.es[name].close()

    def bank(self, i, w=None):
        w = w if w is not None else self.TT
        return self.psP[i // 2][:, (i % 2) * 512:(i % 2) * 512 + w]

    def wait_bank(self, e, i):
        e.wait_ge(self.s_bk[i].h, self.bk_uses[i])
        self.bk_uses[i] += 1

    def free_bank(self, e, i):
        e.sem_inc(self.s_bk[i].h, 1)
        self.s_bk[i].v += 1

    @staticmethod
    def fo_groups(gsz):
        gs, fo = [], 0
        while fo < DCH:
            gs.append(list(range(fo, min(fo + gsz, DCH))))
            fo += gsz
        return gs

    def declare_io(self):
        nc, T = self.nc, self.T
        g = lambda n, s, dt=F32, kind="ExternalInput": nc.dram_tensor(n, s, dt, kind=kind)
        self.x_in = g("x", [T, D])
        for w in ["Wq", "Wk", "Wv", "Wq2", "Wk2", "Wv2", "Wo"]:
            setattr(self, w, g(w, [D, D]))
        for b in ["bq", "bk", "bv", "bq2", "bk2", "bv2", "bo", "gamma", "beta"]:
            setattr(self, b, g(b, [D]))
        self.w_gate = g("w_gate", [I3, D])
        self.w_up = g("w_up", [I3, D])
        self.w_down = g("w_down", [D, I3])
        self.ident_in = g("ident", [128, 128])
        self.selE_in = g("selE", [DCH, H, 128])
        self.out = g("out", [T, D], F32, kind="ExternalOutput")

        dt_ = lambda n, s, dt=F32: nc.dram_tensor(n, s, dt)
        self.h_u_up = dt_("h_u_up", [ICH, 128, T])
        self.h_u_gate = dt_("h_u_gate", [ICH, 128, T])
        self.h_uq = dt_("h_uq", [ICH, 128, T], I16)
        self.h_p = dt_("h_p", [ICH, 128, T])
        self.h_po = dt_("h_po", [ICH, 128, T])
        self.h_s = dt_("h_s", [self.TCH, 128, D])
        self.h_qeh = dt_("h_qeh", [self.NT, 128, self.TSUB * D], BF16)
        self.h_qel = dt_("h_qel", [self.NT, 128, self.TSUB * D], BF16)
        self.h_xnh = dt_("h_xnh", [DCH, 128, T], BF16)
        self.h_xnl = dt_("h_xnl", [DCH, 128, T], BF16)
        self.h_vh = dt_("h_vh", [DCH, 128, T], BF16)
        self.h_vl = dt_("h_vl", [DCH, 128, T], BF16)
        for w in ["wq", "wk", "wv", "wq2", "wk2", "wv2", "wo"]:
            setattr(self, f"h_{w}h", dt_(f"h_{w}h", [DCH, 128, D], BF16))
            setattr(self, f"h_{w}l", dt_(f"h_{w}l", [DCH, 128, D], BF16))
        self.h_wuph = dt_("h_wuph", [I3, D], BF16)
        self.h_wupl = dt_("h_wupl", [I3, D], BF16)
        self.h_wgh = dt_("h_wgh", [I3, D], BF16)
        self.h_wgl = dt_("h_wgl", [I3, D], BF16)
        self.h_wdnh = dt_("h_wdnh", [D, I3], BF16)
        self.h_wdnl = dt_("h_wdnl", [D, I3], BF16)
        self.cc_in = dt_("cc_in", [1, 8])
        self.cc_out = dt_("cc_out", [1, 8])
        self.cc_Sin = dt_("cc_Sin", [DH, H * DH])
        self.cc_Sout = dt_("cc_Sout", [DH, H * DH])

        if self.debug:
            dbg = lambda n, s, dt=F32: nc.dram_tensor(n, s, dt, kind="ExternalOutput")
            self.d_xnh = dbg("d_xnh", [128, DCH * T], BF16)
            self.d_xnl = dbg("d_xnl", [128, DCH * T], BF16)
            self.d_scal = dbg("d_scal", [128, 8])
            self.d_S = dbg("d_S", [DH, H * DH])
            self.d_rq = dbg("d_rq", [H, T])
            self.d_s = dbg("d_s", [self.TCH, 128, D])
            self.d_u = dbg("d_u", [ICH, 128, T])
            self.d_uq = dbg("d_uq", [ICH, 128, T], I16)
            self.d_p = dbg("d_p", [ICH, 128, T])
            self.d_po = dbg("d_po", [ICH, 128, T])
            self.d_vh = dbg("d_vh", [128, DCH * T], BF16)
            self.d_vl = dbg("d_vl", [128, DCH * T], BF16)

    def alloc_static(self):
        nc, T = self.nc, self.T
        S = lambda n, s, dt=F32: nc.alloc_sbuf_tensor(n, s, dt)
        self.ident = S("c_ident", [128, 128])
        self.selE = S("c_selE", [H, DCH * 128])
        self.biases = {}
        for b in ["bq", "bk", "bv", "bq2", "bk2", "bv2", "bo"]:
            self.biases[b] = S(f"c_{b}", [128, DCH])
        self.scal = {n: S(f"sc_{n}", [128, 1]) for n in
                     ["m_up", "m_gate", "m_p", "m_po",
                      "sx", "inv_sx", "su", "inv_su", "sg", "inv_sg",
                      "sg2", "inv_sg2", "so", "inv_so", "c3", "tmp0"]}
        self.sevens = S("c_sevens", [128, 1])
        self.c_eps1 = S("c_eps1", [128, 1])
        self.c_eps2 = S("c_eps2", [128, 1])
        self.ones_row = S("c_ones_row", [1, 128])
        self.rq = S("rq", [H, T])
        self.qks_tm = S("qks_tm", [128, self.TCH * H])
        self.S_sb = S("S_sb", [DH, H * DH])
        self.Sp_f = S("Sp_f", [128, DCH * DH])
        self.SpH = S("SpH", [128, DCH * DH], BF16)
        self.SpL = S("SpL", [128, DCH * DH], BF16)
        self.amax_a = S("amax_a", [128, 1])
        self.amax_b = S("amax_b", [128, 1])
        self.ar_stage = S("ar_stage", [128, 8])
        self.psP = [nc.alloc_psum_tensor(f"psP{i}", [128, 1024], F32) for i in range(4)]
        self.s_bk = [self.sem(f"bk{i}") for i in range(8)]
        self.bk_uses = [0] * 8

    def build(self):
        self.declare_io()
        self.alloc_static()
        nc = self.nc
        with nc.Block() as blk:
            self.blk = blk
            self.ph_consts()
            self.scope("e0")
            self.scope("ln1")
            self.ph_ln1()
            self.ph_sx_scales()
            self.close_scope("ln1")
            self.ph_xi()
            self.ph_prep_and_sweep1()
            self.close_scope("e0")
            self.scope("att")
            self.ph_attn()
            self.close_scope("att")
            self.scope("a68")
            self.ph_A6A8()
            self.close_scope("a68")
            self.scope("s23")
            self.ph_sweep2()
            self.ph_sweep3_and_wdprep()
            self.close_scope("s23")
            self.scope("s4")
            self.ph_sweep4()
            self.close_scope("s4")
            self.ph_final_wait()
        return nc

    def ph_consts(self):
        blk = self.blk
        s_c = self.sem("consts")
        s_g = self.sem("ar_g")

        def _sync(e):
            e.dma_start(out=self.ident[:], in_=self.ident_in[:, :]).then_inc(s_c.h, 16)
            e.dma_start(out=self.selE[:].rearrange("h (c p) -> h c p", c=DCH),
                        in_=self.selE_in[:, :, :].transpose([1, 0, 2])
                        ).then_inc(s_c.h, 16)
            for b in ["bq", "bk", "bv", "bq2", "bk2", "bv2", "bo"]:
                src = getattr(self, b)[:].rearrange("(c p) -> p c", p=128)
                e.dma_start(out=self.biases[b][:], in_=src,
                            allow_slow_non_contiguous=True).then_inc(s_c.h, 16)
            s_c.v = 16 * 9
        blk.sync(_sync)

        def _g(e):
            e.memset(self.sevens[:], 7.0)
            e.memset(self.amax_a[:], 0.0)
            e.memset(self.c_eps1[:], EPS1)
            e.memset(self.c_eps2[:], EPS2)
            e.memset(self.ones_row[:], 1.0)
            e.drain()
            e.sem_inc(s_g.h, 1)
            s_g.v += 1
        blk.gpsimd(_g)

    def ph_ln1(self):
        nc, blk = self.nc, self.blk
        TCH = self.TCH
        s_c = self.sem("consts")
        s_g0 = self.sem("ar_g")
        s_ld = [self.sem(f"l1ld{i}") for i in range(2)]
        s_ldf = [self.sem(f"l1ldf{i}") for i in range(2)]
        s_xm = self.sem("l1xm")
        s_sq = self.sem("l1sq")
        s_xn = self.sem("l1xn")
        s_tr = self.sem("l1tr")
        s_hi = self.sem("l1hi")
        s_lo = self.sem("l1lo")
        s_xst = [self.sem("xstA"), self.sem("xstB")]

        A = lambda n, s, dt=F32: self.salloc("ln1", n, s, dt)
        xt = [A(f"l1xt{i}", [128, D]) for i in range(2)]
        xm = [A(f"l1xm{i}", [128, D]) for i in range(2)]
        xn_tm = [A(f"l1xn{i}", [128, D]) for i in range(2)]
        xnhS = [A(f"l1xnh{i}", [128, D], BF16) for i in range(2)]
        xnlS = [A(f"l1xnl{i}", [128, D], BF16) for i in range(2)]
        sq = A("l1sqb", [128, D])
        st1 = A("l1st1", [128, 1])
        st2 = [A(f"l1st2{i}", [128, 1]) for i in range(2)]
        red = A("l1red", [128, 1])

        def _sync(e):
            for i in range(TCH):
                sl = i % 2
                if i >= 2:
                    e.wait_ge(s_ldf[sl].h, i // 2)
                e.dma_start(out=xt[sl][:], in_=self.x_in[i * 128:(i + 1) * 128, :]
                            ).then_inc(s_ld[sl].h, 16)
                s_ld[sl].v += 16
        blk.sync(_sync)

        def _v(e):
            e.wait_ge(s_g0.h, 1)
            e.wait_ge(s_c.h, s_c.v)
            for i in range(TCH):
                sl = i % 2
                e.wait_ge(s_ld[sl].h, 16 * (i // 2 + 1))
                e.tensor_reduce(st1[:], xt[sl][:], axis=AX.X, op=ALU.add)
                e.drain()
                e.tensor_scalar(st1[:], st1[:], 1.0 / D, None, ALU.mult)
                e.drain()
                if i >= 2:
                    e.wait_ge(s_sq.h, i - 1)
                e.tensor_scalar(xm[sl][:], xt[sl][:], st1[:], None, ALU.subtract)
                e.drain()
                e.sem_inc(s_ldf[sl].h, 1)
                s_ldf[sl].v += 1
                e.sem_inc(s_xm.h, 1)
                s_xm.v += 1
                e.wait_ge(s_sq.h, i + 1)
                e.reciprocal(red[:], st2[sl][:])
                e.drain()
                if i >= 2:
                    e.wait_ge(s_tr.h, i - 1)
                e.tensor_scalar(xn_tm[sl][:], xm[sl][:], red[:], None, ALU.mult)
                e.drain()
                e.tensor_reduce(st1[:], xn_tm[sl][:], axis=AX.X, op=ALU.max, apply_absolute_value=True)
                e.drain()
                e.tensor_tensor(self.amax_a[:], self.amax_a[:], st1[:], ALU.max)
                e.drain()
                e.sem_inc(s_xn.h, 1)
                s_xn.v += 1
                e.wait_ge(s_hi.h, i + 1)
                bj = i % 2
                e.tensor_tensor(xnlS[bj][:], self.psP[bj][:, 0:D], xnhS[bj][:],
                                ALU.subtract)
                e.drain()
                self.free_bank(e, 2 * bj)
                self.free_bank(e, 2 * bj + 1)
                e.sem_inc(s_lo.h, 1)
                s_lo.v += 1
        blk.vector(_v)

        def _a(e):
            e.wait_ge(s_g0.h, 1)
            for i in range(TCH):
                sl = i % 2
                e.wait_ge(s_xm.h, i + 1)
                if i >= 2:
                    e.wait_ge(s_xn.h, i - 1)
                e.activation(sq[:], xm[sl][:], AF.Square, accum_out=st2[sl][:])
                e.drain()
                e.activation(st2[sl][:], st2[sl][:], AF.Sqrt, bias=self.c_eps1[:], scale=1.0 / D)
                e.drain()
                e.sem_inc(s_sq.h, 1)
                s_sq.v += 1
                e.wait_ge(s_tr.h, i + 1)
                bj = i % 2
                if i >= 2:
                    e.wait_ge(s_xst[bj].h, 32 * (i // 2))
                e.activation(xnhS[bj][:], self.psP[bj][:, 0:D], AF.Copy)
                e.drain()
                e.sem_inc(s_hi.h, 1)
                s_hi.v += 1
                e.wait_ge(s_lo.h, i + 1)
                dsth = self.h_xnh[:, :, i * 128:(i + 1) * 128].transpose([1, 0, 2])
                dstl = self.h_xnl[:, :, i * 128:(i + 1) * 128].transpose([1, 0, 2])
                e.dma_start(out=dsth, in_=xnhS[bj][:].rearrange("p (c t) -> p c t", c=DCH)
                            ).then_inc(s_xst[bj].h, 16)
                e.dma_start(out=dstl, in_=xnlS[bj][:].rearrange("p (c t) -> p c t", c=DCH)
                            ).then_inc(s_xst[bj].h, 16)
                s_xst[bj].v += 32
        blk.scalar(_a)

        def _t(e):
            e.wait_ge(s_c.h, s_c.v)
            for i in range(TCH):
                sl = i % 2
                bj = i % 2
                e.wait_ge(s_xn.h, i + 1)
                self.wait_bank(e, 2 * bj)
                self.wait_bank(e, 2 * bj + 1)
                for c in range(DCH):
                    instr = e.transpose(self.psP[bj][:, c * 128:(c + 1) * 128],
                                        xn_tm[sl][:, c * 128:(c + 1) * 128], self.ident[:])
                instr.then_inc(s_tr.h, 1)
                s_tr.v += 1
        blk.tensor(_t)

    def ar_scalar(self, name, srcs):
        blk = self.blk
        n_src = len(srcs)
        s_d = self.sem("ar_d")
        s_g = self.sem("ar_g")
        s_v = self.sem("l1xm")
        s_t = self.sem("l1tr")
        s_done = self.sem("ar_done")

        def _v0(e):
            e.drain()
            e.sem_inc(s_v.h, 1)
            s_v.v += 1
        blk.vector(_v0)

        def _t0(e):
            e.wait_ge(s_v.h, s_v.v)
            self.wait_bank(e, 4)
            for j, srcap in enumerate(srcs):
                instr = e.transpose(self.psP[2][0:1, j * 128:(j + 1) * 128],
                                    srcap, self.ident[:])
            instr.then_inc(s_t.h, 1)
            s_t.v += 1
        blk.tensor(_t0)

        def _v1(e):
            e.wait_ge(s_t.h, s_t.v)
            for j in range(n_src):
                e.tensor_reduce(self.ar_stage[0:1, j:j + 1],
                                self.psP[2][0:1, j * 128:(j + 1) * 128],
                                axis=AX.X, op=ALU.max)
            e.drain()
            self.free_bank(e, 4)
            e.sem_inc(s_v.h, 1)
            s_v.v += 1
        blk.vector(_v1)

        def _sy(e):
            e.wait_ge(s_v.h, s_v.v)
            e.dma_start(out=self.cc_in[0:1, 0:n_src],
                        in_=self.ar_stage[0:1, 0:n_src]).then_inc(s_d.h, 16)
            s_d.v += 16
        blk.sync(_sy)

        def _g2(e):
            e.wait_ge(s_d.h, s_d.v)
            cc = e.collective_compute(
                "AllReduce", ALU.max,
                replica_groups=[list(range(NCORE))],
                ins=[self.cc_in[0:1, 0:n_src].opt()],
                outs=[self.cc_out[0:1, 0:n_src].opt()])
            cc.then_inc(s_g.h, 1)
            s_g.v += 1
        blk.gpsimd(_g2)

        def _sy2(e):
            e.wait_ge(s_g.h, s_g.v)
            e.dma_start(out=self.ar_stage[0:1, 0:n_src],
                        in_=self.cc_out[0:1, 0:n_src]).then_inc(s_d.h, 16)
            s_d.v += 16
        blk.sync(_sy2)

        def _t1(e):
            e.wait_ge(s_d.h, s_d.v)
            self.wait_bank(e, 4)
            mm = e.matmul(self.psP[2][:, 512:512 + n_src], self.ones_row[:],
                          self.ar_stage[0:1, 0:n_src], start=True, stop=True)
            mm.then_inc(s_t.h, 1)
            s_t.v += 1
        blk.tensor(_t1)

        def _v2(e):
            e.wait_ge(s_t.h, s_t.v)
            e.tensor_copy(self.ar_stage[:, 0:n_src], self.psP[2][:, 512:512 + n_src])
            e.drain()
            self.free_bank(e, 4)
            e.sem_inc(s_done.h, 1)
            s_done.v += 1
        blk.vector(_v2)
        return s_done

    def ph_sx_scales(self):
        blk = self.blk
        s_xn = self.sem("l1xn")

        s_done = self.ar_scalar("sxar", [self.amax_a[:]])
        s_sc = self.sem("scd")

        def _v(e):
            e.wait_ge(s_done.h, s_done.v)
            e.tensor_scalar(self.scal["sx"][:], self.ar_stage[:, 0:1], 1.0 / QMAX,
                            1e-12, ALU.mult, ALU.max)
            e.drain()
            e.reciprocal(self.scal["inv_sx"][:], self.scal["sx"][:])
            e.drain()
            e.sem_inc(s_sc.h, 1)
            s_sc.v += 1
        blk.vector(_v)

    def ph_xi(self):
        T, blk = self.T, self.blk
        s_sc = self.sem("scd")
        s_lo = self.sem("l1lo")
        s_xst = [self.sem("xstA"), self.sem("xstB")]
        s_xi = self.sem("l1xm")
        s_ld = [self.sem(f"l1ld{i}") for i in range(2)]
        s_ldf = [self.sem(f"l1ldf{i}") for i in range(2)]
        b_ld = [s.v for s in s_ld]
        b_ldf = [s.v for s in s_ldf]
        self.xiH = self.salloc("e0", "xiH", [128, DCH * T], BF16)
        self.xiL = self.salloc("e0", "xiL", [128, DCH * T], BF16)
        xnc_h = [self.salloc("e0", f"xinh{i}", [128, T], BF16) for i in range(2)]
        xnc_l = [self.salloc("e0", f"xinl{i}", [128, T], BF16) for i in range(2)]
        xi_f = self.salloc("e0", "xi_f", [128, T])

        def _sync(e):
            e.wait_ge(s_xst[0].h, s_xst[0].v)
            e.wait_ge(s_xst[1].h, s_xst[1].v)
            for c in range(DCH):
                sl = c % 2
                if c >= 2:
                    e.wait_ge(s_ldf[sl].h, b_ldf[sl] + c // 2)
                e.dma_start(out=xnc_h[sl][:], in_=self.h_xnh[c]).then_inc(s_ld[sl].h, 16)
                e.dma_start(out=xnc_l[sl][:], in_=self.h_xnl[c]).then_inc(s_ld[sl].h, 16)
                s_ld[sl].v += 32
        blk.sync(_sync)

        def _v(e):
            e.wait_ge(s_sc.h, 1)
            for c in range(DCH):
                sl = c % 2
                cs = slice(c * T, (c + 1) * T)
                e.wait_ge(s_ld[sl].h, b_ld[sl] + 32 * (c // 2 + 1))
                e.tensor_tensor(xi_f[:], xnc_h[sl][:], xnc_l[sl][:], ALU.add)
                e.drain()
                e.sem_inc(s_ldf[sl].h, 1)
                s_ldf[sl].v += 1
                e.tensor_scalar(xi_f[:], xi_f[:], self.scal["inv_sx"][:], RC,
                                ALU.mult, ALU.add)
                e.drain()
                e.tensor_scalar(xi_f[:], xi_f[:], RC, None, ALU.subtract)
                e.drain()
                e.tensor_copy(self.xiH[:, cs], xi_f[:])
                e.drain()
                e.tensor_tensor(self.xiL[:, cs], xi_f[:], self.xiH[:, cs], ALU.subtract)
                e.drain()
                e.sem_inc(s_xi.h, 1)
                s_xi.v += 1
        blk.vector(_v)

    def ph_prep_and_sweep1(self):
        nc, blk, T, TT, NT = self.nc, self.blk, self.T, self.TT, self.NT
        s_xi = self.sem("l1xm")
        s_lo = self.sem("l1lo")

        s_pld = [self.sem(f"ppld{i}") for i in range(2)]
        s_pldf = [self.sem(f"ppldf{i}") for i in range(2)]
        s_pg = self.sem("ppg")
        s_pst = [self.sem("ppst0"), self.sem("ppst1")]
        A = lambda n, s, dt=F32: self.salloc("e0", n, s, dt)
        win = [A(f"ppin{i}", [128, D]) for i in range(2)]
        bmax = A("ppbm", [128, 32])
        sblk = [A(f"pps{i}", [128, 32]) for i in range(2)]
        rblk = [A(f"ppr{i}", [128, 32]) for i in range(2)]
        tq = A("ppt", [128, D])
        wh = [A(f"pph{i}", [128, D], BF16) for i in range(2)]
        wl = [A(f"ppl{i}", [128, D], BF16) for i in range(2)]

        up_jobs = [(self.w_up[ic * 128:(ic + 1) * 128, :],
                    self.h_wuph[ic * 128:(ic + 1) * 128, :],
                    self.h_wupl[ic * 128:(ic + 1) * 128, :], True) for ic in range(ICH)]
        gate_jobs = [(self.w_gate[ic * 128:(ic + 1) * 128, :],
                      self.h_wgh[ic * 128:(ic + 1) * 128, :],
                      self.h_wgl[ic * 128:(ic + 1) * 128, :], True) for ic in range(ICH)]
        aw_jobs = []
        nbase = len(up_jobs) + len(gate_jobs)
        for wname, W in [("wq", self.Wq), ("wq2", self.Wq2), ("wk", self.Wk),
                         ("wk2", self.Wk2), ("wv", self.Wv), ("wv2", self.Wv2),
                         ("wo", self.Wo)]:
            hh = getattr(self, f"h_{wname}h")
            hl = getattr(self, f"h_{wname}l")
            for c in range(DCH):
                aw_jobs.append((W[c * 128:(c + 1) * 128, :], hh[c], hl[c], False))
        self.prep_counts = {"up": len(up_jobs), "gate": nbase,
                            "all": nbase + len(aw_jobs)}

        NQ = 8
        NOC = ICH // NQ
        seq = []
        q_need = {}
        aw_pos = {}

        def _push(jobs):
            p0 = len(seq)
            seq.extend(jobs)
            return list(range(p0, len(seq)))

        hjs = list(range(6, 2 * NQ))
        per = [len(aw_jobs) // len(hjs)] * len(hjs)
        for i in range(len(aw_jobs) - sum(per)):
            per[-1 - i] += 1
        q_need[0] = _push(up_jobs[0:NOC])
        q_need[1] = _push(up_jobs[NOC:2 * NOC])
        awi = 0
        for hj in range(2 * NQ):
            q = hj + 2
            if q < 2 * NQ:
                jl = up_jobs if q < NQ else gate_jobs
                qf = q if q < NQ else q - NQ
                q_need[q] = _push(jl[qf * NOC:(qf + 1) * NOC])
            if hj in hjs:
                n = per[hjs.index(hj)]
                aw_pos[hj] = list(range(awi, awi + n))
                awi += n
            else:
                aw_pos[hj] = []

        s_ald = [self.sem("wdld0"), self.sem("wdld1")]
        s_aldf = self.sem("wdldf0")
        s_ag = self.sem("wdg")
        s_ast = [self.sem("wdst0"), self.sem("wdst1")]
        awin = [A(f"awin{i}", [128, D]) for i in range(2)]
        awh = [A(f"awh{i}", [128, D], BF16) for i in range(2)]
        awl = [A(f"awl{i}", [128, D], BF16) for i in range(2)]

        def emit_aw_load(e, a):
            sl = a % 2
            if a >= 2:
                e.wait_ge(s_aldf.h, a - 1)
            e.dma_start(out=awin[sl][:], in_=aw_jobs[a][0]).then_inc(s_ald[sl].h, 16)
            s_ald[sl].v += 16

        def emit_aw_gp(e, a):
            sl = a % 2
            e.wait_ge(s_ald[sl].h, 16 * (a // 2 + 1))
            if a >= 2:
                e.wait_ge(s_ast[sl].h, 32 * (a // 2))
            e.tensor_copy(awh[sl][:], awin[sl][:])
            e.drain()
            e.tensor_tensor(awl[sl][:], awin[sl][:], awh[sl][:], ALU.subtract)
            e.drain()
            e.sem_inc(s_aldf.h, 1)
            s_aldf.v += 1
            e.sem_inc(s_ag.h, 1)
            s_ag.v += 1

        def emit_aw_store(e, a):
            sl = a % 2
            e.wait_ge(s_ag.h, a + 1)
            e.dma_start(out=aw_jobs[a][1], in_=awh[sl][:]).then_inc(s_ast[sl].h, 16)
            e.dma_start(out=aw_jobs[a][2], in_=awl[sl][:]).then_inc(s_ast[sl].h, 16)
            s_ast[sl].v += 32

        def emit_prep_sync_job(e, j):
            src, dh, dl, lp = seq[j]
            sl = j % 2
            if j >= 2:
                e.wait_ge(s_pldf[sl].h, j // 2)
            e.dma_start(out=win[sl][:], in_=src).then_inc(s_pld[sl].h, 16)
            s_pld[sl].v += 16
            e.wait_ge(s_pg.h, j + 1)
            e.dma_start(out=dh, in_=wh[sl][:]).then_inc(s_pst[sl].h, 16)
            e.dma_start(out=dl, in_=wl[sl][:]).then_inc(s_pst[sl].h, 16)
            s_pst[sl].v += 32

        def emit_prep_dve(e, j):
            src, dh, dl, lp = seq[j]
            sl = j % 2
            e.wait_ge(s_pld[sl].h, 16 * (j // 2 + 1))
            if j >= 2:
                e.wait_ge(s_pst[sl].h, 32 * (j // 2))
            if lp:
                e.tensor_reduce(bmax[:], win[sl][:].rearrange("p (a b) -> p a b", b=32),
                                axis=AX.X, op=ALU.max, apply_absolute_value=True)
                e.drain()
                e.tensor_scalar(sblk[sl][:], bmax[:], 1.0 / 7.0, 1e-12,
                                ALU.mult, ALU.max)
                e.drain()
                e.reciprocal(rblk[sl][:], sblk[sl][:])
                e.drain()
                e.tensor_tensor(tq[:], win[sl][:],
                                rblk[sl][:].rearrange("p (a b) -> p a b", b=1)
                                .broadcast_to([128, 32, 32]), ALU.mult)
                e.drain()
                e.sem_inc(s_pldf[sl].h, 1)
                s_pldf[sl].v += 1
                e.tensor_scalar(tq[:], tq[:], RC, RC, ALU.add, ALU.subtract)
                e.drain()
                e.tensor_tensor(tq[:], tq[:],
                                sblk[sl][:].rearrange("p (a b) -> p a b", b=1)
                                .broadcast_to([128, 32, 32]), ALU.mult)
                e.drain()
                src_dq = tq
            else:
                src_dq = win[sl]
            e.tensor_copy(wh[sl][:], src_dq[:])
            e.drain()
            e.tensor_tensor(wl[sl][:], src_dq[:], wh[sl][:], ALU.subtract)
            e.drain()
            if not lp:
                e.sem_inc(s_pldf[sl].h, 1)
                s_pldf[sl].v += 1
            e.sem_inc(s_pg.h, 1)
            s_pg.v += 1

        NWS = 3
        s_w1 = [self.sem(f"s1w{i}") for i in range(NWS)]
        s_w1f = [self.sem(f"s1wf{i}") for i in range(NWS)]
        s_mm = self.sem("s1mm")
        s_ev = self.sem("s1ev")
        s_st = [self.sem("s1st0"), self.sem("s1st1")]
        HOW = I3 // NQ
        wT_h = [A(f"s1th{i}", [128, DCH * HOW], BF16) for i in range(NWS)]
        wT_l = [A(f"s1tl{i}", [128, DCH * HOW], BF16) for i in range(NWS)]
        ub = [A(f"s1u{i}", [128, TT]) for i in range(2)]
        quarters = [(nm, qf) for nm in ["up", "gate"] for qf in range(NQ)]
        self.s1_total = len(quarters) * NOC * NT

        def emit_s1_wload(e, hj, kcs):
            nm, qf = quarters[hj]
            sl = hj % NWS
            m = max(q_need[hj])
            e.wait_ge(s_pst[0].h, 32 * (m // 2 + 1))
            e.wait_ge(s_pst[1].h, 32 * ((m + 1) // 2))
            if hj >= NWS:
                e.wait_ge(s_w1f[sl].h, hj // NWS)
            hsrc = self.h_wuph if nm == "up" else self.h_wgh
            lsrc = self.h_wupl if nm == "up" else self.h_wgl
            if hj == 0:
                e.wait_ge(s_lo.h, self.TCH)
            for kc in kcs:
                e.dma_start(
                    out=wT_h[sl][:, kc * HOW:(kc + 1) * HOW],
                    in_=hsrc[qf * HOW:(qf + 1) * HOW, kc * 128:(kc + 1) * 128],
                    transpose=True).then_inc(s_w1[sl].h, 16)
                e.dma_start(
                    out=wT_l[sl][:, kc * HOW:(kc + 1) * HOW],
                    in_=lsrc[qf * HOW:(qf + 1) * HOW, kc * 128:(kc + 1) * 128],
                    transpose=True).then_inc(s_w1[sl].h, 16)
                s_w1[sl].v += 32

        KC_SY = list(range(DCH))

        def _sync(e):
            for q in (0, 1):
                for j in q_need[q]:
                    emit_prep_sync_job(e, j)
            emit_s1_wload(e, 0, KC_SY)
            emit_s1_wload(e, 1, KC_SY)
            for hj in range(2 * NQ):
                if hj + 2 < 2 * NQ:
                    for j in q_need[hj + 2]:
                        emit_prep_sync_job(e, j)
                    emit_s1_wload(e, hj + 2, KC_SY)
        blk.sync(_sync)

        def _g(e):
            for hj in range(2 * NQ):
                for a in aw_pos[hj]:
                    emit_aw_gp(e, a)
        blk.gpsimd(_g)

        def emit_s1_evacs(e, hj0, hj1, nmm):
            for hj in range(hj0, hj1):
                nm, qf = quarters[hj]
                acc = self.scal["m_up"] if nm == "up" else self.scal["m_gate"]
                for oc in range(NOC):
                    for tt in range(NT):
                        ubuf = ub[nmm % 2]
                        bk = nmm % 2
                        par = nmm % 2
                        nmm += 1
                        e.wait_ge(s_mm.h, nmm)
                        if nmm > 2:
                            e.wait_ge(s_st[par].h, 16 * ((nmm - 1) // 2))
                        e.tensor_scalar(ubuf[:], self.bank(bk), self.scal["sx"][:],
                                        None, ALU.mult)
                        e.drain()
                        self.free_bank(e, bk)
                        e.tensor_reduce(self.amax_b[:], ubuf[:], axis=AX.X,
                                        op=ALU.max, apply_absolute_value=True)
                        e.drain()
                        if qf == 0 and oc == 0 and tt == 0:
                            e.tensor_copy(acc[:], self.amax_b[:])
                        else:
                            e.tensor_tensor(acc[:], acc[:], self.amax_b[:], ALU.max)
                        e.drain()
                        e.sem_inc(s_ev.h, 1)
                        s_ev.v += 1
                e.sem_inc(s_w1f[hj % NWS].h, 1)
                s_w1f[hj % NWS].v += 1
            return nmm

        def _v(e):
            for q in (0, 1):
                for j in q_need[q]:
                    emit_prep_dve(e, j)
            nmm = 0
            for hj in range(2 * NQ):
                if hj + 2 < 2 * NQ:
                    for j in q_need[hj + 2]:
                        emit_prep_dve(e, j)
                nmm = emit_s1_evacs(e, hj, hj + 1, nmm)
        blk.vector(_v)

        def _t(e):
            e.wait_ge(s_xi.h, s_xi.v)
            nmm = 0
            for hj, (nm, qf) in enumerate(quarters):
                sl = hj % NWS
                e.wait_ge(s_w1[sl].h, 32 * DCH * (hj // NWS + 1))
                for oc in range(NOC):
                    for tt in range(NT):
                        bk = nmm % 2
                        nmm += 1
                        self.wait_bank(e, bk)
                        first = True
                        for kc in range(DCH):
                            lw_h = wT_h[sl][:, kc * HOW + oc * 128: kc * HOW + (oc + 1) * 128]
                            lw_l = wT_l[sl][:, kc * HOW + oc * 128: kc * HOW + (oc + 1) * 128]
                            rx_h = self.xiH[:, kc * T + tt * TT: kc * T + (tt + 1) * TT]
                            rx_l = self.xiL[:, kc * T + tt * TT: kc * T + (tt + 1) * TT]
                            e.matmul(self.bank(bk), lw_h, rx_h, start=first, stop=False)
                            first = False
                            e.matmul(self.bank(bk), lw_h, rx_l, start=False, stop=False)
                            mm = e.matmul(self.bank(bk), lw_l, rx_h, start=False,
                                          stop=(kc == DCH - 1))
                        mm.then_inc(s_mm.h, 1)
                        s_mm.v += 1
        blk.tensor(_t)

        def _a(e):
            nmm = 0
            for hj, (nm, qf) in enumerate(quarters):
                dst = self.h_u_up if nm == "up" else self.h_u_gate
                for oc in range(NOC):
                    ic = qf * NOC + oc
                    for tt in range(NT):
                        nmm += 1
                        e.wait_ge(s_ev.h, nmm)
                        par = (nmm - 1) % 2
                        e.dma_start(out=dst[ic, :, tt * TT:(tt + 1) * TT],
                                    in_=ub[par][:]).then_inc(s_st[par].h, 16)
                        s_st[par].v += 16
                for a in aw_pos[hj]:
                    if a - 2 >= 0:
                        emit_aw_store(e, a - 2)
                    emit_aw_load(e, a)
            for a in (len(aw_jobs) - 2, len(aw_jobs) - 1):
                emit_aw_store(e, a)
        blk.scalar(_a)

        s_done = self.ar_scalar("ugar", [self.scal["m_up"][:], self.scal["m_gate"][:]])
        s_sc2 = self.sem("scd")

        def _v2(e):
            e.wait_ge(s_done.h, s_done.v)
            for j, nm in enumerate(["su", "sg"]):
                e.tensor_scalar(self.scal[nm][:], self.ar_stage[:, j:j + 1], 1.0 / QMAX,
                                1e-12, ALU.mult, ALU.max)
                e.drain()
                e.reciprocal(self.scal["inv_" + nm][:], self.scal[nm][:])
                e.drain()
            e.sem_inc(s_sc2.h, 1)
            s_sc2.v += 1
        blk.vector(_v2)

        if self.debug:
            s_dbg = self.sem("dbg")

            def _dbg(e):
                e.wait_ge(s_st[0].h, s_st[0].v)
                e.wait_ge(s_st[1].h, s_st[1].v)
                for ic in range(ICH):
                    e.dma_start(out=self.d_u[ic], in_=self.h_u_up[ic]
                                ).then_inc(s_dbg.h, 16)
                    s_dbg.v += 16
            blk.scalar(_dbg)

    def attn_iter(self):
        for pi, ph in enumerate(self.attnP):
            pname, gsz = ph[0], ph[7]
            for tt in range(self.NT):
                for st in (0, 1):
                    for grp in self.fo_groups(gsz):
                        yield (pi, pname, tt, st, grp)

    def ph_attn(self):
        nc, blk, T, TT, NT, TSUB = self.nc, self.blk, self.T, self.TT, self.NT, self.TSUB
        s_lo = self.sem("l1lo")
        s_pst = [self.sem("ppst0"), self.sem("ppst1")]
        s_s1st = [self.sem("s1st0"), self.sem("s1st1")]

        s_stg = [self.sem(f"ws{i}") for i in range(2)]
        s_stgf = [self.sem(f"wsf{i}") for i in range(2)]
        s_xs = [self.sem(f"atxs{i}") for i in range(2)]
        s_xsf = self.sem("atxsf")
        A = lambda n, s, dt=F32: self.salloc("att", n, s, dt)
        wSH = [A(f"wsrh{i}", [128, DCH * D], BF16) for i in range(2)]
        wSL = [A(f"wsrl{i}", [128, DCH * D], BF16) for i in range(2)]
        xsTH = [A(f"xsth{i}", [128, DCH * TT], BF16) for i in range(2)]
        xsTL = [A(f"xstl{i}", [128, DCH * TT], BF16) for i in range(2)]

        t1H = A("att1h", [128, DCH * TT], BF16)
        t1L = A("att1l", [128, DCH * TT], BF16)
        qe_fm = A("atqefm", [128, DCH * TT])
        elu_a = A("atelua", [128, TT])
        elu_b = A("atelub", [128, TT])
        tmH = A("attmh", [128, TSUB * D], BF16)
        tmL = A("attml", [128, TSUB * D], BF16)
        qeH = A("atqeh2", [128, TSUB * D], BF16)
        qeL = A("atqel2", [128, TSUB * D], BF16)
        pk1 = A("atpk1", [128, D])
        pk2 = A("atpk2", [128, D])
        qksr = A("atqksr", [128, H])
        vstH = [A(f"atvsth{i}", [128, TT], BF16) for i in range(2)]
        vstL = [A(f"atvstl{i}", [128, TT], BF16) for i in range(2)]
        self.b_qest = [32 * ((self.NT + 1) // 2), 32 * (self.NT // 2)]

        s_mmg = self.sem("atmmg")
        s_ev1 = self.sem("atev1")
        s_qe = self.sem("atqe")
        s_exp = self.sem("atexp")
        s_elu_z = self.sem("atz")
        s_tr = self.sem("attr")
        s_hi = self.sem("athi")
        s_lo2 = self.sem("atlo")
        s_qest = [self.sem("atqestA"), self.sem("atqestB")]
        s_qeld = self.sem("atqeld")
        s_qefree = self.sem("atqefree")
        s_qks = self.sem("atqks")

        self.attnP = [
            ("q", self.h_wqh, self.h_wql, "bq", self.h_wq2h, self.h_wq2l, "bq2", 8),
            ("k", self.h_wkh, self.h_wkl, "bk", self.h_wk2h, self.h_wk2l, "bk2", 6),
            ("v", self.h_wvh, self.h_wvl, "bv", self.h_wv2h, self.h_wv2l, "bv2", 8)]
        psS = self.psP[3][0:DH, :]

        s_ast = [self.sem("wdst0"), self.sem("wdst1")]

        def emit_wstage(e, s):
            ph = self.attnP[s // 2]
            hsrc = ph[1] if s % 2 == 0 else ph[4]
            lsrc = ph[2] if s % 2 == 0 else ph[5]
            sl = s % 2
            if s >= 2:
                e.wait_ge(s_stgf[sl].h, s // 2)
            for kc in range(DCH):
                e.dma_start(out=wSH[sl][:, kc * D:(kc + 1) * D], in_=hsrc[kc]
                            ).then_inc(s_stg[sl].h, 16)
                e.dma_start(out=wSL[sl][:, kc * D:(kc + 1) * D], in_=lsrc[kc]
                            ).then_inc(s_stg[sl].h, 16)
                s_stg[sl].v += 32

        def _sync(e):
            e.wait_ge(s_pst[0].h, s_pst[0].v)
            e.wait_ge(s_pst[1].h, s_pst[1].v)
            e.wait_ge(s_ast[0].h, s_ast[0].v)
            e.wait_ge(s_ast[1].h, s_ast[1].v)
            e.wait_ge(s_s1st[0].h, 16 * ((self.s1_total + 1) // 2))
            e.wait_ge(s_s1st[1].h, 16 * (self.s1_total // 2))
            if self.debug:
                sdbg = self.sem("dbg")
                e.wait_ge(sdbg.h, sdbg.v)
            emit_wstage(e, 0)
            emit_wstage(e, 1)
            for pi, ph in enumerate(self.attnP):
                pname = ph[0]
                for tt in range(NT):
                    if pname == "k":
                        e.wait_ge(s_qest[tt % 2].h, 32 * (tt // 2 + 1))
                        if tt >= 1:
                            e.wait_ge(s_qefree.h, 2 * tt)
                        e.dma_start(out=qeH[:], in_=self.h_qeh[tt]).then_inc(s_qeld.h, 16)
                        e.dma_start(out=qeL[:], in_=self.h_qel[tt]).then_inc(s_qeld.h, 16)
                        s_qeld.v += 32
                    n = pi * NT + tt
                    sl = n % 2
                    if n >= 2:
                        e.wait_ge(s_xsf.h, n - 1)
                    cs = slice(tt * self.TT, (tt + 1) * self.TT)
                    for kc in range(DCH):
                        e.dma_start(out=xsTH[sl][:, kc * TT:(kc + 1) * TT],
                                    in_=self.h_xnh[kc][:, cs]).then_inc(s_xs[sl].h, 16)
                        e.dma_start(out=xsTL[sl][:, kc * TT:(kc + 1) * TT],
                                    in_=self.h_xnl[kc][:, cs]).then_inc(s_xs[sl].h, 16)
                        s_xs[sl].v += 32
                    if tt == 0 and pi > 0:
                        emit_wstage(e, 2 * pi + 1)
                if 2 * pi + 2 < 6:
                    emit_wstage(e, 2 * pi + 2)
        blk.sync(_sync)

        def _t(e):
            e.wait_ge(s_lo.h, self.TCH)

            def mm_group(grp, s, rhsH, rhsL, col_of, xn=None):
                wsl = s % 2
                for fi, fo in enumerate(grp):
                    self.wait_bank(e, fi)
                for kc in range(DCH):
                    for fi, fo in enumerate(grp):
                        lw_h = wSH[wsl][:, kc * D + fo * 128:kc * D + (fo + 1) * 128]
                        lw_l = wSL[wsl][:, kc * D + fo * 128:kc * D + (fo + 1) * 128]
                        if rhsH is None:
                            rx_h = xsTH[xn][:, kc * TT:(kc + 1) * TT]
                            rx_l = xsTL[xn][:, kc * TT:(kc + 1) * TT]
                        else:
                            rx_h = rhsH[:, col_of(kc)]
                            rx_l = rhsL[:, col_of(kc)]
                        e.matmul(self.bank(fi), lw_h, rx_h, start=(kc == 0), stop=False)
                        e.matmul(self.bank(fi), lw_h, rx_l, start=False, stop=False)
                        mm = e.matmul(self.bank(fi), lw_l, rx_h, start=False,
                                      stop=(kc == DCH - 1))
                mm.then_inc(s_mmg.h, 1)
                s_mmg.v += 1
                return mm

            for pi, ph in enumerate(self.attnP):
                pname, gsz = ph[0], ph[7]
                for tt in range(NT):
                    n = pi * NT + tt
                    xsl = n % 2
                    e.wait_ge(s_xs[xsl].h, 32 * DCH * (n // 2 + 1))
                    if tt == 0:
                        e.wait_ge(s_stg[0].h, 32 * DCH * (pi + 1))
                    grps = self.fo_groups(gsz)
                    for gi, grp in enumerate(grps):
                        mm = mm_group(grp, 2 * pi, None, None, None, xn=xsl)
                    e.drain()
                    e.sem_inc(s_xsf.h, 1)
                    s_xsf.v += 1
                    if tt == NT - 1:
                        e.sem_inc(s_stgf[0].h, 1)
                        s_stgf[0].v += 1
                    if tt == 0:
                        e.wait_ge(s_stg[1].h, 32 * DCH * (pi + 1))
                    e.wait_ge(s_ev1.h, pi * NT + tt + 1)
                    for gi, grp in enumerate(grps):
                        mm = mm_group(grp, 2 * pi + 1, t1H, t1L,
                                      lambda kc: slice(kc * TT, (kc + 1) * TT))
                    if tt == NT - 1:
                        e.drain()
                        e.sem_inc(s_stgf[1].h, 1)
                        s_stgf[1].v += 1
                    if pname in ("q", "k"):
                        if pname == "k":
                            e.wait_ge(s_qks.h, tt)
                        e.wait_ge(s_qe.h, pi * NT + tt + 1)
                        for ts_ in range(TSUB):
                            bj = ts_ % 2
                            self.wait_bank(e, 2 * bj)
                            self.wait_bank(e, 2 * bj + 1)
                            for c in range(DCH):
                                instr = e.transpose(
                                    self.psP[bj][:, c * 128:(c + 1) * 128],
                                    qe_fm[:, c * TT + ts_ * 128: c * TT + (ts_ + 1) * 128],
                                    self.ident[:])
                            instr.then_inc(s_tr.h, 1)
                            s_tr.v += 1
                    if pname == "k":
                        if tt == 0:
                            self.wait_bank(e, 6)
                            self.wait_bank(e, 7)
                        e.wait_ge(s_lo2.h, (NT + tt) * TSUB + TSUB)
                        e.wait_ge(s_qeld.h, 32 * (tt + 1))
                        for ts_ in range(TSUB):
                            for h in range(H):
                                qh_ = qeH[:, ts_ * D + h * DH: ts_ * D + (h + 1) * DH]
                                ql_ = qeL[:, ts_ * D + h * DH: ts_ * D + (h + 1) * DH]
                                kh_ = tmH[:, ts_ * D + h * DH: ts_ * D + (h + 1) * DH]
                                kl_ = tmL[:, ts_ * D + h * DH: ts_ * D + (h + 1) * DH]
                                po = psS[:, h * DH:(h + 1) * DH]
                                firstS = (tt == 0 and ts_ == 0 and h % 8 == 0)
                                e.matmul(po, kh_, qh_, start=firstS, stop=False,
                                         skip_group_check=True)
                                e.matmul(po, kh_, ql_, start=False, stop=False,
                                         skip_group_check=True)
                                mm = e.matmul(po, kl_, qh_, start=False,
                                              stop=(tt == NT - 1 and ts_ == TSUB - 1),
                                              skip_group_check=True)
                        mm.then_inc(s_qefree.h, 1)
                        s_qefree.v += 1
            e.wait_ge(s_qks.h, NT + 1)
            per = 4
            self.n_rq = 0
            for base in range(0, self.TCH, per):
                self.wait_bank(e, 0)
                for j in range(per):
                    i = base + j
                    if i >= self.TCH:
                        break
                    instr = e.transpose(self.psP[0][0:H, j * 128:(j + 1) * 128],
                                        self.qks_tm[:, i * H:(i + 1) * H], self.ident[:])
                instr.then_inc(s_tr.h, 1)
                s_tr.v += 1
                self.n_rq += 1
        blk.tensor(_t)

        def _v(e):
            ngrp = 0
            for pi, ph in enumerate(self.attnP):
                pname, gsz = ph[0], ph[7]
                bias1 = self.biases[ph[3]]
                bias2 = self.biases[ph[6]]
                for tt in range(NT):
                    for grp in self.fo_groups(gsz):
                        ngrp += 1
                        e.wait_ge(s_mmg.h, ngrp)
                        for fi, fo in enumerate(grp):
                            dst_h = t1H[:, fo * TT:(fo + 1) * TT]
                            dst_l = t1L[:, fo * TT:(fo + 1) * TT]
                            e.tensor_scalar(dst_h, self.bank(fi), bias1[:, fo:fo + 1],
                                            None, ALU.add)
                            e.drain()
                            e.scalar_tensor_tensor(dst_l, self.bank(fi),
                                                   bias1[:, fo:fo + 1], dst_h,
                                                   ALU.add, ALU.subtract)
                            e.drain()
                            self.free_bank(e, fi)
                    e.sem_inc(s_ev1.h, 1)
                    s_ev1.v += 1
                    for grp in self.fo_groups(gsz):
                        ngrp += 1
                        e.wait_ge(s_mmg.h, ngrp)
                        for fi, fo in enumerate(grp):
                            if pname in ("q", "k"):
                                e.tensor_scalar(elu_a[:], self.bank(fi),
                                                bias2[:, fo:fo + 1], 0.0, ALU.add, ALU.min)
                                e.drain()
                                e.sem_inc(s_elu_z.h, 1)
                                s_elu_z.v += 1
                                e.tensor_scalar(elu_b[:], self.bank(fi),
                                                bias2[:, fo:fo + 1], 0.0, ALU.add, ALU.max)
                                e.drain()
                                self.free_bank(e, fi)
                                e.wait_ge(s_exp.h, s_elu_z.v)
                                e.tensor_tensor(qe_fm[:, fo * TT:(fo + 1) * TT],
                                                elu_b[:], elu_a[:], ALU.add)
                                e.drain()
                            else:
                                idx = tt * DCH + fo
                                par = idx % 2
                                if idx >= 2:
                                    e.wait_ge(s_qest[par].h,
                                              self.b_qest[par] + 32 * (idx // 2))
                                dst_h = vstH[par][:]
                                dst_l = vstL[par][:]
                                e.tensor_scalar(dst_h, self.bank(fi), bias2[:, fo:fo + 1],
                                                None, ALU.add)
                                e.drain()
                                e.scalar_tensor_tensor(dst_l, self.bank(fi),
                                                       bias2[:, fo:fo + 1], dst_h,
                                                       ALU.add, ALU.subtract)
                                e.drain()
                                self.free_bank(e, fi)
                                e.sem_inc(s_elu_z.h, 1)
                                s_elu_z.v += 1
                    e.sem_inc(s_qe.h, 1)
                    s_qe.v += 1
                    if pname in ("q", "k"):
                        for ts_ in range(TSUB):
                            bj = ts_ % 2
                            idx = (pi * NT + tt) * TSUB + ts_ + 1
                            e.wait_ge(s_hi.h, idx)
                            e.tensor_tensor(tmL[:, ts_ * D:(ts_ + 1) * D],
                                            self.psP[bj][:, 0:D],
                                            tmH[:, ts_ * D:(ts_ + 1) * D], ALU.subtract)
                            e.drain()
                            self.free_bank(e, 2 * bj)
                            self.free_bank(e, 2 * bj + 1)
                            e.sem_inc(s_lo2.h, 1)
                            s_lo2.v += 1
                    if pname == "k":
                        e.wait_ge(s_qeld.h, 32 * (tt + 1))
                        for ts_ in range(TSUB):
                            qh_ = qeH[:, ts_ * D:(ts_ + 1) * D]
                            ql_ = qeL[:, ts_ * D:(ts_ + 1) * D]
                            kh_ = tmH[:, ts_ * D:(ts_ + 1) * D]
                            kl_ = tmL[:, ts_ * D:(ts_ + 1) * D]
                            e.tensor_tensor(pk1[:], qh_, kh_, ALU.mult)
                            e.tensor_tensor(pk2[:], qh_, kl_, ALU.mult)
                            e.drain()
                            e.tensor_tensor(pk1[:], pk1[:], pk2[:], ALU.add)
                            e.drain()
                            e.tensor_tensor(pk2[:], ql_, kh_, ALU.mult)
                            e.drain()
                            e.tensor_tensor(pk1[:], pk1[:], pk2[:], ALU.add)
                            e.drain()
                            e.tensor_reduce(qksr[:],
                                            pk1[:].rearrange("p (h d) -> p h d", h=H),
                                            axis=AX.X, op=ALU.add)
                            e.drain()
                            i_sub = tt * TSUB + ts_
                            e.tensor_scalar(self.qks_tm[:, i_sub * H:(i_sub + 1) * H],
                                            qksr[:], EPSQ, None, ALU.add)
                            e.drain()
                        e.sem_inc(s_qefree.h, 1)
                        s_qefree.v += 1
                        e.sem_inc(s_qks.h, 1)
                        s_qks.v += 1
                if pname == "k":
                    e.wait_ge(s_qefree.h, 2 * NT)
                    e.tensor_copy(self.S_sb[:], psS[:])
                    e.drain()
                    self.free_bank(e, 6)
                    self.free_bank(e, 7)
                    e.reciprocal(self.qks_tm[:], self.qks_tm[:])
                    e.drain()
                    e.sem_inc(s_qks.h, 1)
                    s_qks.v += 1
            for r in range(self.TCH // 4 + (1 if self.TCH % 4 else 0)):
                e.wait_ge(s_tr.h, 2 * NT * TSUB + r + 1)
                base = r * 4
                n_here = min(4, self.TCH - base)
                e.tensor_copy(self.rq[:, base * 128:(base + n_here) * 128],
                              self.psP[0][0:H, 0:n_here * 128])
                e.drain()
                self.free_bank(e, 0)
        blk.vector(_v)

        def _a(e):
            nz = 0
            for pi, ph in enumerate(self.attnP):
                pname, gsz = ph[0], ph[7]
                if pname == "v":
                    for tt in range(NT):
                        for fo in range(DCH):
                            idx = tt * DCH + fo
                            par = idx % 2
                            nz += 1
                            e.wait_ge(s_elu_z.h, nz)
                            cs = slice(tt * TT, (tt + 1) * TT)
                            e.dma_start(out=self.h_vh[fo][:, cs], in_=vstH[par][:]
                                        ).then_inc(s_qest[par].h, 16)
                            e.dma_start(out=self.h_vl[fo][:, cs], in_=vstL[par][:]
                                        ).then_inc(s_qest[par].h, 16)
                            s_qest[par].v += 32
                    continue
                for tt in range(NT):
                    for grp in self.fo_groups(gsz):
                        for fi, fo in enumerate(grp):
                            nz += 1
                            e.wait_ge(s_elu_z.h, nz)
                            e.activation(elu_a[:], elu_a[:], AF.Exp)
                            e.drain()
                            e.sem_inc(s_exp.h, 1)
                            s_exp.v += 1
                    for ts_ in range(TSUB):
                        bj = ts_ % 2
                        idx = (pi * NT + tt) * TSUB + ts_ + 1
                        if ts_ == 0:
                            if pname == "q" and tt >= 1:
                                pt = (tt - 1) % 2
                                e.wait_ge(s_qest[pt].h, 32 * ((tt - 1) // 2 + 1))
                            if pname == "k" and tt == 0:
                                e.wait_ge(s_qest[0].h, s_qest[0].v)
                                e.wait_ge(s_qest[1].h, s_qest[1].v)
                        e.wait_ge(s_tr.h, idx)
                        e.activation(tmH[:, ts_ * D:(ts_ + 1) * D],
                                     self.psP[bj][:, 0:D], AF.Copy)
                        e.drain()
                        e.sem_inc(s_hi.h, 1)
                        s_hi.v += 1
                    if pname == "q":
                        e.wait_ge(s_lo2.h, tt * TSUB + TSUB)
                        e.dma_start(out=self.h_qeh[tt], in_=tmH[:]
                                    ).then_inc(s_qest[tt % 2].h, 16)
                        e.dma_start(out=self.h_qel[tt], in_=tmL[:]
                                    ).then_inc(s_qest[tt % 2].h, 16)
                        s_qest[tt % 2].v += 32
        blk.scalar(_a)

        s_sar = self.sem("atsar")
        s_sld = self.sem("atsld")
        s_spl = self.sem("atspl")

        def _sy2(e):
            e.wait_ge(s_qks.h, NT + 1)
            e.dma_start(out=self.cc_Sin[:, :], in_=self.S_sb[:]).then_inc(s_sld.h, 16)
            s_sld.v += 16
        blk.sync(_sy2)

        def _g2(e):
            e.wait_ge(s_sld.h, 16)
            cc = e.collective_compute(
                "AllReduce", ALU.add,
                replica_groups=[[0, 1], [2, 3], [4, 5], [6, 7]],
                ins=[self.cc_Sin[:, :].opt()], outs=[self.cc_Sout[:, :].opt()])
            cc.then_inc(s_sar.h, 1)
            s_sar.v += 1
        blk.gpsimd(_g2)

        def _sy3(e):
            e.wait_ge(s_sar.h, 1)
            for c in range(DCH):
                e.dma_start(out=self.Sp_f[0:DH, c * DH:(c + 1) * DH],
                            in_=self.cc_Sout[:, (2 * c) * DH:(2 * c + 1) * DH]
                            ).then_inc(s_sld.h, 16)
                e.dma_start(out=self.Sp_f[DH:128, c * DH:(c + 1) * DH],
                            in_=self.cc_Sout[:, (2 * c + 1) * DH:(2 * c + 2) * DH]
                            ).then_inc(s_sld.h, 16)
                s_sld.v += 32
        blk.sync(_sy3)

        def _v3(e):
            e.wait_ge(s_sld.h, s_sld.v)
            e.tensor_copy(self.SpH[:], self.Sp_f[:])
            e.drain()
            e.tensor_tensor(self.SpL[:], self.Sp_f[:], self.SpH[:], ALU.subtract)
            e.drain()
            e.sem_inc(s_spl.h, 1)
            s_spl.v += 1
        blk.vector(_v3)

        if self.debug:
            s_d = self.sem("dbg")

            def _dbg(e):
                e.wait_ge(s_spl.h, 1)
                for sq_ in (self.sem("atqestA"), self.sem("atqestB")):
                    e.wait_ge(sq_.h, sq_.v)
                e.dma_start(out=self.d_S[:], in_=self.S_sb[:]).then_inc(s_d.h, 16)
                e.dma_start(out=self.d_rq[:], in_=self.rq[:]).then_inc(s_d.h, 16)
                e.dma_start(out=self.d_xnh[:].rearrange("p (c t) -> c p t", c=DCH),
                            in_=self.h_xnh[:, :, :]).then_inc(s_d.h, 16)
                e.dma_start(out=self.d_xnl[:].rearrange("p (c t) -> c p t", c=DCH),
                            in_=self.h_xnl[:, :, :]).then_inc(s_d.h, 16)
                e.dma_start(out=self.d_vh[:].rearrange("p (c t) -> c p t", c=DCH),
                            in_=self.h_vh[:, :, :]).then_inc(s_d.h, 16)
                e.dma_start(out=self.d_vl[:].rearrange("p (c t) -> c p t", c=DCH),
                            in_=self.h_vl[:, :, :]).then_inc(s_d.h, 16)
                s_d.v += 96
            blk.scalar(_dbg)

    def ph_A6A8(self):
        nc, blk, T, TT, NT, TSUB = self.nc, self.blk, self.T, self.TT, self.NT, self.TSUB
        s_spl = self.sem("atspl")

        s_mm6 = self.sem("a8mm6")
        s_rbc = self.sem("a8rbc")
        s_ev6 = self.sem("a8ev6")
        s_mmg8 = self.sem("a8mmg")
        s_oatt = self.sem("a8oatt")
        s_y = self.sem("a8y")
        s_ytr = self.sem("a8ytr")
        s_yev = self.sem("a8yev")
        s_ym = self.sem("a8ym")
        s_sq2 = self.sem("a8sq2")
        s_sv = self.sem("a8sv")
        s_sd = [self.sem("a8sd0"), self.sem("a8sd1")]

        A = lambda n, s, dt=F32: self.salloc("a68", n, s, dt)
        self.gamma_bc = A("gamma_bc", [128, D])
        self.beta_bc = A("beta_bc", [128, D])
        woRH = A("a8worh", [128, DCH * D], BF16)
        woRL = A("a8worl", [128, DCH * D], BF16)
        s_wo = self.sem("a8wo")
        vqH = [A(f"a8vqh{i}", [128, TT], BF16) for i in range(2)]
        vqL = [A(f"a8vql{i}", [128, TT], BF16) for i in range(2)]
        xnsbH = [A(f"a8xnh{i}", [128, DCH * 128], BF16) for i in range(2)]
        xnsbL = [A(f"a8xnl{i}", [128, DCH * 128], BF16) for i in range(2)]
        s_vld = [self.sem("ppld0"), self.sem("ppld1")]
        b_vld = [s.v for s in s_vld]
        s_qefree = self.sem("atqefree")
        b_qefree = s_qefree.v
        s_xl = [self.sem(f"l1ld{i}") for i in range(2)]
        s_xlf = [self.sem(f"l1ldf{i}") for i in range(2)]
        b_xl = [s.v for s in s_xl]
        b_xlf = [s.v for s in s_xlf]
        s_qest = [self.sem("atqestA"), self.sem("atqestB")]
        s_c = self.sem("consts")
        outH = A("a8outh", [128, DCH * TT], BF16)
        outL = A("a8outl", [128, DCH * TT], BF16)
        out_f = A("a8outf", [128, TT])
        rqbc = A("a8rqbc", [128, TT])
        oatt = A("a8oatt", [128, DCH * TT])
        y_fm = A("a8yfm", [128, DCH * 128])
        y_tm = [A(f"a8ytm{i}", [128, D]) for i in range(2)]
        ym = [A(f"a8ym{i}", [128, D]) for i in range(2)]
        sqb = A("a8sqb", [128, D])
        s_tmv = [A(f"a8stm{i}", [128, D]) for i in range(2)]
        st1 = A("a8st1", [128, 1])
        st2 = [A(f"a8st2{i}", [128, 1]) for i in range(2)]

        def _sync(e):
            e.wait_ge(s_qest[0].h, s_qest[0].v)
            e.wait_ge(s_qest[1].h, s_qest[1].v)
            e.wait_ge(s_spl.h, 1)
            for kc in range(DCH):
                e.dma_start(out=woRH[:, kc * D:(kc + 1) * D], in_=self.h_woh[kc]
                            ).then_inc(s_wo.h, 16)
                e.dma_start(out=woRL[:, kc * D:(kc + 1) * D], in_=self.h_wol[kc]
                            ).then_inc(s_wo.h, 16)
                s_wo.v += 32
            e.dma_start(out=self.gamma_bc[:],
                        in_=self.gamma[:].partition_broadcast(128)).then_inc(s_c.h, 16)
            e.dma_start(out=self.beta_bc[:],
                        in_=self.beta[:].partition_broadcast(128)).then_inc(s_c.h, 16)
            s_c.v += 32
            nvq = 0
            for tt in range(NT):
                for c in range(DCH):
                    idxs = tt * DCH + c
                    e.wait_ge(s_qest[idxs % 2].h,
                              self.b_qest[idxs % 2] + 32 * (idxs // 2 + 1))
                    sl = nvq % 2
                    if nvq >= 2:
                        e.wait_ge(s_qefree.h, b_qefree + nvq - 1)
                    nvq += 1
                    cs = slice(tt * TT, (tt + 1) * TT)
                    e.dma_start(out=vqH[sl][:], in_=self.h_vh[c][:, cs]
                                ).then_inc(s_vld[sl].h, 16)
                    e.dma_start(out=vqL[sl][:], in_=self.h_vl[c][:, cs]
                                ).then_inc(s_vld[sl].h, 16)
                    s_vld[sl].v += 32
                for ts_ in range(TSUB):
                    i = tt * TSUB + ts_
                    sl = i % 2
                    if i >= 2:
                        e.wait_ge(s_xlf[sl].h, b_xlf[sl] + i // 2)
                    colv = slice(i * 128, (i + 1) * 128)
                    e.dma_start(out=xnsbH[sl][:].rearrange("p (c t) -> p c t", c=DCH),
                                in_=self.h_xnh[:, :, colv].transpose([1, 0, 2])
                                ).then_inc(s_xl[sl].h, 16)
                    e.dma_start(out=xnsbL[sl][:].rearrange("p (c t) -> p c t", c=DCH),
                                in_=self.h_xnl[:, :, colv].transpose([1, 0, 2])
                                ).then_inc(s_xl[sl].h, 16)
                    s_xl[sl].v += 32
        blk.sync(_sync)

        def _t(e):
            e.wait_ge(s_spl.h, 1)
            for tt in range(NT):
                for c in range(DCH):
                    bk = c % 2
                    nvq = tt * DCH + c
                    vsl = nvq % 2
                    e.wait_ge(s_vld[vsl].h, b_vld[vsl] + 32 * (nvq // 2 + 1))
                    self.wait_bank(e, bk)
                    for half in range(2):
                        p0, p1 = half * DH, half * DH + DH
                        sh = self.SpH[p0:p1, c * DH:(c + 1) * DH]
                        sl_ = self.SpL[p0:p1, c * DH:(c + 1) * DH]
                        vh = vqH[vsl][p0:p1, :]
                        vl = vqL[vsl][p0:p1, :]
                        po = self.bank(bk)[p0:p1, :]
                        e.matmul(po, sh, vh, start=True, stop=False)
                        e.matmul(po, sh, vl, start=False, stop=False)
                        mm = e.matmul(po, sl_, vh, start=False, stop=True)
                    mm.then_inc(s_qefree.h, 1)
                    s_qefree.v += 1
                    mm2_note = None
                    e.drain()
                    e.sem_inc(s_mm6.h, 1)
                    s_mm6.v += 1
                    self.wait_bank(e, 5)
                    mm = e.matmul(self.bank(5), self.selE[:, c * 128:(c + 1) * 128],
                                  self.rq[:, tt * TT:(tt + 1) * TT], start=True, stop=True)
                    mm.then_inc(s_rbc.h, 1)
                    s_rbc.v += 1
                e.wait_ge(s_ev6.h, tt * DCH + DCH)
                if tt == 0:
                    e.wait_ge(s_wo.h, s_wo.v)
                for fo_g in self.fo_groups(2):
                    for fi, fo in enumerate(fo_g):
                        self.wait_bank(e, 2 + fi)
                    for kc in range(DCH):
                        for fi, fo in enumerate(fo_g):
                            lw_h = woRH[:, kc * D + fo * 128:kc * D + (fo + 1) * 128]
                            lw_l = woRL[:, kc * D + fo * 128:kc * D + (fo + 1) * 128]
                            rx_h = outH[:, kc * TT:(kc + 1) * TT]
                            rx_l = outL[:, kc * TT:(kc + 1) * TT]
                            e.matmul(self.bank(2 + fi), lw_h, rx_h, start=(kc == 0),
                                     stop=False)
                            e.matmul(self.bank(2 + fi), lw_h, rx_l, start=False, stop=False)
                            mm = e.matmul(self.bank(2 + fi), lw_l, rx_h, start=False,
                                          stop=(kc == DCH - 1))
                    mm.then_inc(s_mmg8.h, 1)
                    s_mmg8.v += 1
                for ts_ in range(TSUB):
                    i = tt * TSUB + ts_
                    e.wait_ge(s_y.h, i + 1)
                    self.wait_bank(e, 6)
                    self.wait_bank(e, 7)
                    for c in range(DCH):
                        instr = e.transpose(self.psP[3][:, c * 128:(c + 1) * 128],
                                            y_fm[:, c * 128:(c + 1) * 128], self.ident[:])
                    instr.then_inc(s_ytr.h, 1)
                    s_ytr.v += 1
        blk.tensor(_t)

        def _v(e):
            e.wait_ge(s_c.h, s_c.v)
            for tt in range(NT):
                for c in range(DCH):
                    bk = c % 2
                    e.wait_ge(s_mm6.h, tt * DCH + c + 1)
                    e.wait_ge(s_rbc.h, tt * DCH + c + 1)
                    e.tensor_copy(rqbc[:], self.bank(5))
                    e.drain()
                    self.free_bank(e, 5)
                    e.tensor_tensor(out_f[:], self.bank(bk), rqbc[:], ALU.mult)
                    e.drain()
                    self.free_bank(e, bk)
                    e.tensor_copy(outH[:, c * TT:(c + 1) * TT], out_f[:])
                    e.drain()
                    e.tensor_tensor(outL[:, c * TT:(c + 1) * TT], out_f[:],
                                    outH[:, c * TT:(c + 1) * TT], ALU.subtract)
                    e.drain()
                    e.sem_inc(s_ev6.h, 1)
                    s_ev6.v += 1
                for gi, fo_g in enumerate(self.fo_groups(2)):
                    e.wait_ge(s_mmg8.h, tt * 4 + gi + 1)
                    for fi, fo in enumerate(fo_g):
                        e.tensor_scalar(oatt[:, fo * TT:(fo + 1) * TT], self.bank(2 + fi),
                                        self.biases["bo"][:, fo:fo + 1], None, ALU.add)
                        e.drain()
                        self.free_bank(e, 2 + fi)
                e.sem_inc(s_oatt.h, 1)
                s_oatt.v += 1
                for ts_ in range(TSUB):
                    i = tt * TSUB + ts_
                    par = i % 2
                    if i >= 1:
                        e.wait_ge(s_ytr.h, i)
                    e.wait_ge(s_xl[par].h, b_xl[par] + 32 * (i // 2 + 1))
                    oat_v = oatt[:].rearrange("p (c t) -> p c t", c=DCH)[
                        :, :, ts_ * 128:(ts_ + 1) * 128]
                    yv = y_fm[:].rearrange("p (c t) -> p c t", c=DCH)
                    e.tensor_tensor(yv, oat_v, xnsbH[par][:]
                                    .rearrange("p (c t) -> p c t", c=DCH), ALU.add)
                    e.drain()
                    e.tensor_tensor(yv, yv, xnsbL[par][:]
                                    .rearrange("p (c t) -> p c t", c=DCH), ALU.add)
                    e.drain()
                    e.sem_inc(s_xlf[par].h, 1)
                    s_xlf[par].v += 1
                    e.sem_inc(s_y.h, 1)
                    s_y.v += 1
                    e.wait_ge(s_yev.h, i + 1)
                    e.tensor_reduce(st1[:], y_tm[par][:], axis=AX.X, op=ALU.add)
                    e.drain()
                    e.tensor_scalar(st1[:], st1[:], 1.0 / D, None, ALU.mult)
                    e.drain()
                    if i >= 2:
                        e.wait_ge(s_sq2.h, i - 1)
                    e.tensor_scalar(ym[par][:], y_tm[par][:], st1[:], None, ALU.subtract)
                    e.drain()
                    e.sem_inc(s_ym.h, 1)
                    s_ym.v += 1
                    e.wait_ge(s_sq2.h, i + 1)
                    e.reciprocal(st1[:], st2[par][:])
                    e.drain()
                    if i >= 2:
                        e.wait_ge(s_sd[par].h, 16 * (i // 2))
                    e.scalar_tensor_tensor(s_tmv[par][:], ym[par][:], st1[:],
                                           self.gamma_bc[:], ALU.mult, ALU.mult)
                    e.drain()
                    e.tensor_tensor(s_tmv[par][:], s_tmv[par][:], self.beta_bc[:], ALU.add)
                    e.drain()
                    e.sem_inc(s_sv.h, 1)
                    s_sv.v += 1
        blk.vector(_v)

        def _a(e):
            for tt in range(NT):
                for ts_ in range(TSUB):
                    i = tt * TSUB + ts_
                    par = i % 2
                    e.wait_ge(s_ytr.h, i + 1)
                    if i >= 2:
                        e.wait_ge(s_ym.h, i - 1)
                    e.activation(y_tm[par][:], self.psP[3][:, 0:D], AF.Copy)
                    e.drain()
                    self.free_bank(e, 6)
                    self.free_bank(e, 7)
                    e.sem_inc(s_yev.h, 1)
                    s_yev.v += 1
                    e.wait_ge(s_ym.h, i + 1)
                    if i >= 2:
                        e.wait_ge(s_sv.h, i - 1)
                    e.activation(sqb[:], ym[par][:], AF.Square, accum_out=st2[par][:])
                    e.drain()
                    e.activation(st2[par][:], st2[par][:], AF.Sqrt,
                                 bias=self.c_eps2[:], scale=1.0 / D)
                    e.drain()
                    e.sem_inc(s_sq2.h, 1)
                    s_sq2.v += 1
                    e.wait_ge(s_sv.h, i + 1)
                    e.dma_start(out=self.h_s[i], in_=s_tmv[par][:]
                                ).then_inc(s_sd[par].h, 16)
                    s_sd[par].v += 16
        blk.scalar(_a)

        if self.debug:
            s_d = self.sem("dbg")

            def _dbg(e):
                e.wait_ge(s_sd[0].h, s_sd[0].v)
                e.wait_ge(s_sd[1].h, s_sd[1].v)
                for i in range(self.TCH):
                    e.dma_start(out=self.d_s[i], in_=self.h_s[i]).then_inc(s_d.h, 16)
                    s_d.v += 16
            blk.sync(_dbg)

    def ph_sweep2(self):
        nc, blk, T = self.nc, self.blk, self.T
        s_sc2 = self.sem("scd")
        s_sd = [self.sem("a8sd0"), self.sem("a8sd1")]
        s_ld = [self.sem(f"l1ld{i}") for i in range(2)]
        s_ldf = [self.sem(f"l1ldf{i}") for i in range(2)]
        b_ld = [s.v for s in s_ld]
        b_ldf = [s.v for s in s_ldf]
        s_v = self.sem("s2v")
        s_sig = self.sem("s2sig")
        s_done = self.sem("s2done")
        s_st = self.sem("s2st")
        A = lambda n, s, dt=F32: self.salloc("s23", n, s, dt)
        ubuf = [A(f"s2u{i}", [128, T]) for i in range(2)]
        gbuf = [A(f"s2g{i}", [128, T]) for i in range(2)]
        t1b = A("s2t1", [128, T])
        uq16 = A("s2uq", [128, T], I16)
        g16 = A("s2g16", [128, T])
        gfp = A("s2gfp", [128, T])
        sig = A("s2sig", [128, T])
        sigq = A("s2sigq", [128, T], U16)
        pbuf = A("s2p", [128, T])
        junk = A("s2junk", [128, 1])

        def _sync(e):
            e.wait_ge(s_sc2.h, 2)
            e.wait_ge(s_sd[0].h, 16 * ((self.TCH + 1) // 2))
            e.wait_ge(s_sd[1].h, 16 * (self.TCH // 2))
            if self.debug:
                sdbg = self.sem("dbg")
                e.wait_ge(sdbg.h, sdbg.v)
            for ic in range(ICH):
                sl = ic % 2
                if ic >= 2:
                    e.wait_ge(s_ldf[sl].h, b_ldf[sl] + ic // 2)
                e.dma_start(out=ubuf[sl][:], in_=self.h_u_up[ic]).then_inc(s_ld[sl].h, 16)
                e.dma_start(out=gbuf[sl][:], in_=self.h_u_gate[ic]).then_inc(s_ld[sl].h, 16)
                s_ld[sl].v += 32
        blk.sync(_sync)

        def _v(e):
            for ic in range(ICH):
                sl = ic % 2
                e.wait_ge(s_ld[sl].h, b_ld[sl] + 32 * (ic // 2 + 1))
                e.tensor_scalar(t1b[:], ubuf[sl][:], self.scal["inv_su"][:], RC,
                                ALU.mult, ALU.add)
                e.drain()
                if ic >= 1:
                    e.wait_ge(s_st.h, 32 * ic)
                e.tensor_scalar(uq16[:], t1b[:], RC, None, ALU.subtract)
                e.drain()
                e.tensor_scalar(t1b[:], gbuf[sl][:], self.scal["inv_sg"][:], RC,
                                ALU.mult, ALU.add)
                e.drain()
                e.sem_inc(s_ldf[sl].h, 1)
                s_ldf[sl].v += 1
                e.tensor_scalar(g16[:], t1b[:], RC, None, ALU.subtract)
                e.drain()
                if ic >= 1:
                    e.wait_ge(s_sig.h, ic)
                e.tensor_scalar(gfp[:], g16[:], self.scal["sg"][:], None, ALU.mult)
                e.drain()
                e.sem_inc(s_v.h, 1)
                s_v.v += 1
                e.wait_ge(s_sig.h, ic + 1)
                e.tensor_scalar(t1b[:], sig[:], 65536.0, RC, ALU.mult, ALU.add)
                e.drain()
                e.tensor_scalar(sigq[:], t1b[:], RC, 65535.0, ALU.subtract, ALU.min)
                e.drain()
                e.tensor_tensor(pbuf[:], gfp[:], sigq[:], ALU.mult)
                e.drain()
                e.tensor_scalar(pbuf[:], pbuf[:], 1.0 / 65536.0, None, ALU.mult)
                e.drain()
                e.tensor_reduce(self.amax_b[:], pbuf[:], axis=AX.X, op=ALU.max, apply_absolute_value=True)
                e.drain()
                if ic == 0:
                    e.tensor_copy(self.scal["m_p"][:], self.amax_b[:])
                else:
                    e.tensor_tensor(self.scal["m_p"][:], self.scal["m_p"][:],
                                    self.amax_b[:], ALU.max)
                e.drain()
                e.sem_inc(s_done.h, 1)
                s_done.v += 1
        blk.vector(_v)

        def _a(e):
            for ic in range(ICH):
                e.wait_ge(s_v.h, ic + 1)
                e.activation(sig[:], gfp[:], AF.Sigmoid)
                e.drain()
                e.sem_inc(s_sig.h, 1)
                s_sig.v += 1
                e.wait_ge(s_done.h, ic + 1)
                e.dma_start(out=self.h_uq[ic], in_=uq16[:]).then_inc(s_st.h, 16)
                e.dma_start(out=self.h_p[ic], in_=pbuf[:]).then_inc(s_st.h, 16)
                s_st.v += 32
        blk.scalar(_a)

        s_ardone = self.ar_scalar("g2ar", [self.scal["m_p"][:]])
        s_sc3 = self.sem("scd")

        def _v2(e):
            e.wait_ge(s_ardone.h, s_ardone.v)
            e.tensor_scalar(self.scal["sg2"][:], self.ar_stage[:, 0:1], 1.0 / QMAX,
                            1e-12, ALU.mult, ALU.max)
            e.drain()
            e.reciprocal(self.scal["inv_sg2"][:], self.scal["sg2"][:])
            e.drain()
            e.sem_inc(s_sc3.h, 1)
            s_sc3.v += 1
        blk.vector(_v2)

        if self.debug:
            s_d = self.sem("dbg")

            def _dbg(e):
                e.wait_ge(s_st.h, s_st.v)
                for ic in range(ICH):
                    e.dma_start(out=self.d_uq[ic], in_=self.h_uq[ic]).then_inc(s_d.h, 16)
                    e.dma_start(out=self.d_p[ic], in_=self.h_p[ic]).then_inc(s_d.h, 16)
                    s_d.v += 32
            blk.scalar(_dbg)

    def ph_sweep3_and_wdprep(self):
        nc, blk, T = self.nc, self.blk, self.T
        s_sc3 = self.sem("scd")
        s_st2 = self.sem("s2st")
        s_ld = [self.sem(f"ppld{i}") for i in range(2)]
        s_ldf = [self.sem(f"ppldf{i}") for i in range(2)]
        b_ld = [s.v for s in s_ld]
        b_ldf = [s.v for s in s_ldf]
        s_v = self.sem("s3v")
        s_st = self.sem("s3st")
        A = lambda n, s, dt=F32: self.salloc("s23", n, s, dt)
        p3 = [A(f"s3p{i}", [128, T]) for i in range(2)]
        uq3 = [A(f"s3uq{i}", [128, T], I16) for i in range(2)]
        g2b = A("s3g2", [128, T])
        pob = A("s3po", [128, T])
        junk = A("s3junk", [128, 1])

        s_dld = [self.sem(f"wdld{i}") for i in range(2)]
        s_dldf = [self.sem(f"wdldf{i}") for i in range(2)]
        s_dg = self.sem("wdg")
        s_dst = [self.sem("wdst0"), self.sem("wdst1")]
        b_dld = [s.v for s in s_dld]
        b_dldf = [s.v for s in s_dldf]
        b_dg = s_dg.v
        b_dst = [s.v for s in s_dst]
        dwin = [A(f"wdin{i}", [128, D]) for i in range(2)]
        dbm = A("wdbm", [128, 32])
        dsb = [A(f"wds{i}", [128, 32]) for i in range(2)]
        drb = [A(f"wdr{i}", [128, 32]) for i in range(2)]
        dtq = A("wdt", [128, D])
        ddq = A("wddq", [128, D])
        dwh = [A(f"wdh{i}", [128, D], BF16) for i in range(2)]
        dwl = [A(f"wdl{i}", [128, D], BF16) for i in range(2)]
        djobs = [(c, s3) for c in range(DCH) for s3 in range(3)]
        self.n_djobs = len(djobs)

        def _sync(e):
            e.wait_ge(s_sc3.h, 3)
            for ic in range(ICH):
                sl = ic % 2
                e.wait_ge(s_st2.h, 32 * (ic + 1))
                if ic >= 2:
                    e.wait_ge(s_ldf[sl].h, b_ldf[sl] + ic // 2)
                e.dma_start(out=p3[sl][:], in_=self.h_p[ic]).then_inc(s_ld[sl].h, 16)
                e.dma_start(out=uq3[sl][:], in_=self.h_uq[ic]).then_inc(s_ld[sl].h, 16)
                s_ld[sl].v += 32
                c, s3 = djobs[ic]
                if ic >= 2:
                    e.wait_ge(s_dldf[sl].h, b_dldf[sl] + ic // 2)
                e.dma_start(out=dwin[sl][:],
                            in_=self.w_down[c * 128:(c + 1) * 128, s3 * D:(s3 + 1) * D]
                            ).then_inc(s_dld[sl].h, 16)
                s_dld[sl].v += 16
        blk.sync(_sync)

        def _v(e):
            for ic in range(ICH):
                sl = ic % 2
                e.wait_ge(s_ld[sl].h, b_ld[sl] + 32 * (ic // 2 + 1))
                e.tensor_scalar(g2b[:], p3[sl][:], self.scal["inv_sg2"][:], RC,
                                ALU.mult, ALU.add)
                e.drain()
                e.tensor_scalar(g2b[:], g2b[:], RC, None, ALU.subtract)
                e.drain()
                if ic >= 1:
                    e.wait_ge(s_st.h, 16 * ic)
                e.tensor_tensor(pob[:], g2b[:], uq3[sl][:], ALU.mult)
                e.drain()
                e.sem_inc(s_ldf[sl].h, 1)
                s_ldf[sl].v += 1
                e.tensor_reduce(self.amax_b[:], pob[:], axis=AX.X, op=ALU.max, apply_absolute_value=True)
                e.drain()
                if ic == 0:
                    e.tensor_copy(self.scal["m_po"][:], self.amax_b[:])
                else:
                    e.tensor_tensor(self.scal["m_po"][:], self.scal["m_po"][:],
                                    self.amax_b[:], ALU.max)
                e.drain()
                e.sem_inc(s_v.h, 1)
                s_v.v += 1
                e.wait_ge(s_dld[sl].h, b_dld[sl] + 16 * (ic // 2 + 1))
                e.tensor_reduce(dbm[:], dwin[sl][:].rearrange("p (a b) -> p a b", b=32),
                                axis=AX.X, op=ALU.max, apply_absolute_value=True)
                e.drain()
                e.tensor_scalar(dsb[sl][:], dbm[:], 1.0 / 7.0, 1e-12,
                                ALU.mult, ALU.max)
                e.drain()
                e.reciprocal(drb[sl][:], dsb[sl][:])
                e.drain()
                e.tensor_tensor(dtq[:], dwin[sl][:],
                                drb[sl][:].rearrange("p (a b) -> p a b", b=1)
                                .broadcast_to([128, 32, 32]), ALU.mult)
                e.drain()
                e.sem_inc(s_dldf[sl].h, 1)
                s_dldf[sl].v += 1
                e.tensor_scalar(dtq[:], dtq[:], RC, RC, ALU.add, ALU.subtract)
                e.drain()
                e.tensor_tensor(ddq[:], dtq[:],
                                dsb[sl][:].rearrange("p (a b) -> p a b", b=1)
                                .broadcast_to([128, 32, 32]), ALU.mult)
                e.drain()
                if ic >= 2:
                    e.wait_ge(s_dst[sl].h, b_dst[sl] + 32 * (ic // 2))
                e.tensor_copy(dwh[sl][:], ddq[:])
                e.drain()
                e.tensor_tensor(dwl[sl][:], ddq[:], dwh[sl][:], ALU.subtract)
                e.drain()
                e.sem_inc(s_dg.h, 1)
                s_dg.v += 1
        blk.vector(_v)

        def _a(e):
            for ic in range(ICH):
                e.wait_ge(s_v.h, ic + 1)
                e.dma_start(out=self.h_po[ic], in_=pob[:]).then_inc(s_st.h, 16)
                s_st.v += 16
                c, s3 = djobs[ic]
                sl = ic % 2
                e.wait_ge(s_dg.h, b_dg + ic + 1)
                e.dma_start(out=self.h_wdnh[c * 128:(c + 1) * 128, s3 * D:(s3 + 1) * D],
                            in_=dwh[sl][:]).then_inc(s_dst[sl].h, 16)
                e.dma_start(out=self.h_wdnl[c * 128:(c + 1) * 128, s3 * D:(s3 + 1) * D],
                            in_=dwl[sl][:]).then_inc(s_dst[sl].h, 16)
                s_dst[sl].v += 32
        blk.scalar(_a)

        s_ardone = self.ar_scalar("oar", [self.scal["m_po"][:]])
        s_sc4 = self.sem("scd")

        def _v2(e):
            e.wait_ge(s_ardone.h, s_ardone.v)
            e.tensor_scalar(self.scal["tmp0"][:], self.ar_stage[:, 0:1],
                            self.scal["sg2"][:], self.scal["su"][:], ALU.mult, ALU.mult)
            e.drain()
            e.tensor_scalar(self.scal["so"][:], self.scal["tmp0"][:], 1.0 / QMAX,
                            1e-12, ALU.mult, ALU.max)
            e.drain()
            e.reciprocal(self.scal["inv_so"][:], self.scal["so"][:])
            e.drain()
            e.tensor_scalar(self.scal["c3"][:], self.scal["inv_so"][:],
                            self.scal["sg2"][:], self.scal["su"][:], ALU.mult, ALU.mult)
            e.drain()
            e.sem_inc(s_sc4.h, 1)
            s_sc4.v += 1
        blk.vector(_v2)

        if self.debug:
            s_d = self.sem("dbg")

            def _dbg(e):
                e.wait_ge(s_st.h, s_st.v)
                for ic in range(ICH):
                    e.dma_start(out=self.d_po[ic], in_=self.h_po[ic]).then_inc(s_d.h, 16)
                    s_d.v += 16
            blk.scalar(_dbg)

    def ph_sweep4(self):
        nc, blk, T, TT, NT, TSUB = self.nc, self.blk, self.T, self.TT, self.NT, self.TSUB
        s_sc4 = self.sem("scd")
        s_dst = [self.sem("wdst0"), self.sem("wdst1")]
        s_st3 = self.sem("s3st")
        s_sd = [self.sem("a8sd0"), self.sem("a8sd1")]

        s_wd = self.sem("s4wd")
        s_po = [self.sem(f"l1ld{i}") for i in range(2)]
        s_pof = [self.sem(f"l1ldf{i}") for i in range(2)]
        b_po = [s.v for s in s_po]
        b_pof = [s.v for s in s_pof]
        s_oi = self.sem("l1xn")
        b_oi = s_oi.v
        s_mm = self.sem("s4mm")
        s_act = self.sem("l1sq")
        b_act = s_act.v
        s_sxld = [self.sem("s4sxA"), self.sem("s4sxB")]
        s_sgv = self.sem("s4sgv")
        s_tr = self.sem("s4tr")
        s_out = [self.sem("s4outA"), self.sem("s4outB")]

        A = lambda n, s, dt=F32: self.salloc("s4", n, s, dt)
        wdTH = A("s4wdh", [128, ICH * D], BF16)
        wdTL = A("s4wdl", [128, ICH * D], BF16)
        pobuf = [A(f"s4pob{i}", [128, TT]) for i in range(2)]
        oiH = A("s4oih", [128, ICH * TT], BF16)
        oiL = A("s4oil", [128, ICH * TT], BF16)
        oi_f = A("s4oif", [128, TT])
        gbuf = A("s4gb", [128, DCH * TT])
        s_tmb = [A(f"s4stm{i}", [128, D]) for i in range(2)]
        x_tmb = [A(f"s4xtm{i}", [128, D]) for i in range(2)]

        def _sync(e):
            e.wait_ge(s_dst[0].h, s_dst[0].v)
            e.wait_ge(s_dst[1].h, s_dst[1].v)
            e.wait_ge(s_st3.h, 16 * ICH)
            if self.debug:
                sdbg = self.sem("dbg")
                e.wait_ge(sdbg.h, sdbg.v)
            for ic in range(ICH):
                e.dma_start(out=wdTH[:, ic * D:(ic + 1) * D],
                            in_=self.h_wdnh[:, ic * 128:(ic + 1) * 128],
                            transpose=True).then_inc(s_wd.h, 16)
                e.dma_start(out=wdTL[:, ic * D:(ic + 1) * D],
                            in_=self.h_wdnl[:, ic * 128:(ic + 1) * 128],
                            transpose=True).then_inc(s_wd.h, 16)
                s_wd.v += 32
            e.wait_ge(s_sc4.h, 4)
            for tt in range(NT):
                for ic in range(ICH):
                    sl = ic % 2
                    j = tt * ICH + ic
                    if j >= 2:
                        e.wait_ge(s_pof[sl].h, b_pof[sl] + j // 2)
                    e.dma_start(out=pobuf[sl][:],
                                in_=self.h_po[ic, :, tt * TT:(tt + 1) * TT]
                                ).then_inc(s_po[sl].h, 16)
                    s_po[sl].v += 16
                for ts_ in range(TSUB):
                    i = tt * TSUB + ts_
                    par = i % 2
                    e.wait_ge(s_sd[par].h, 16 * (i // 2 + 1))
                    if i >= 2:
                        e.wait_ge(s_sgv.h, i - 1)
                        e.wait_ge(s_out[par].h, 16 * (i // 2))
                    e.dma_start(out=s_tmb[par][:], in_=self.h_s[i]
                                ).then_inc(s_sxld[par].h, 16)
                    e.dma_start(out=x_tmb[par][:],
                                in_=self.x_in[i * 128:(i + 1) * 128, :]
                                ).then_inc(s_sxld[par].h, 16)
                    s_sxld[par].v += 32
        blk.sync(_sync)

        def _v(e):
            for tt in range(NT):
                for ic in range(ICH):
                    sl = ic % 2
                    j = tt * ICH + ic
                    e.wait_ge(s_po[sl].h, b_po[sl] + 16 * (j // 2 + 1))
                    e.tensor_scalar(oi_f[:], pobuf[sl][:], self.scal["c3"][:], RC,
                                    ALU.mult, ALU.add)
                    e.drain()
                    e.sem_inc(s_pof[sl].h, 1)
                    s_pof[sl].v += 1
                    e.tensor_scalar(oi_f[:], oi_f[:], RC, None, ALU.subtract)
                    e.drain()
                    if tt >= 1 and ic == 0:
                        e.wait_ge(s_mm.h, tt * DCH)
                    e.tensor_copy(oiH[:, ic * TT:(ic + 1) * TT], oi_f[:])
                    e.drain()
                    e.tensor_tensor(oiL[:, ic * TT:(ic + 1) * TT], oi_f[:],
                                    oiH[:, ic * TT:(ic + 1) * TT], ALU.subtract)
                    e.drain()
                    e.sem_inc(s_oi.h, 1)
                    s_oi.v += 1
                for ts_ in range(TSUB):
                    i = tt * TSUB + ts_
                    par = i % 2
                    e.wait_ge(s_tr.h, i + 1)
                    e.wait_ge(s_sxld[par].h, 32 * (i // 2 + 1))
                    e.tensor_tensor(s_tmb[par][:], self.psP[3][:, 0:D], s_tmb[par][:],
                                    ALU.mult)
                    e.drain()
                    self.free_bank(e, 6)
                    self.free_bank(e, 7)
                    e.tensor_tensor(s_tmb[par][:], s_tmb[par][:], x_tmb[par][:], ALU.add)
                    e.drain()
                    e.sem_inc(s_sgv.h, 1)
                    s_sgv.v += 1
        blk.vector(_v)

        def _t(e):
            e.wait_ge(s_wd.h, 32 * ICH)
            for tt in range(NT):
                for fo in range(DCH):
                    bk = fo % 2
                    e.wait_ge(s_oi.h, b_oi + tt * ICH + ICH)
                    self.wait_bank(e, bk)
                    first = True
                    for ic in range(ICH):
                        lw_h = wdTH[:, ic * D + fo * 128: ic * D + (fo + 1) * 128]
                        lw_l = wdTL[:, ic * D + fo * 128: ic * D + (fo + 1) * 128]
                        rx_h = oiH[:, ic * TT:(ic + 1) * TT]
                        rx_l = oiL[:, ic * TT:(ic + 1) * TT]
                        e.matmul(self.bank(bk), lw_h, rx_h, start=first, stop=False)
                        first = False
                        e.matmul(self.bank(bk), lw_h, rx_l, start=False, stop=False)
                        mm = e.matmul(self.bank(bk), lw_l, rx_h, start=False,
                                      stop=(ic == ICH - 1))
                    mm.then_inc(s_mm.h, 1)
                    s_mm.v += 1
                for ts_ in range(TSUB):
                    e.wait_ge(s_act.h, b_act + tt * DCH + DCH)
                    self.wait_bank(e, 6)
                    self.wait_bank(e, 7)
                    for c in range(DCH):
                        instr = e.transpose(
                            self.psP[3][:, c * 128:(c + 1) * 128],
                            gbuf[:, c * TT + ts_ * 128: c * TT + (ts_ + 1) * 128],
                            self.ident[:])
                    instr.then_inc(s_tr.h, 1)
                    s_tr.v += 1
        blk.tensor(_t)

        def _a(e):
            for tt in range(NT):
                for fo in range(DCH):
                    bk = fo % 2
                    e.wait_ge(s_mm.h, tt * DCH + fo + 1)
                    if tt >= 1 and fo == 0:
                        e.wait_ge(s_tr.h, tt * TSUB)
                    e.activation(gbuf[:, fo * TT:(fo + 1) * TT], self.bank(bk), AF.Silu,
                                 scale=self.scal["so"][:])
                    e.drain()
                    self.free_bank(e, bk)
                    e.sem_inc(s_act.h, 1)
                    s_act.v += 1
                for ts_ in range(TSUB):
                    i = tt * TSUB + ts_
                    par = i % 2
                    e.wait_ge(s_sgv.h, i + 1)
                    e.dma_start(out=self.out[i * 128:(i + 1) * 128, :],
                                in_=s_tmb[par][:]).then_inc(s_out[par].h, 16)
                    s_out[par].v += 16
        blk.scalar(_a)

    def ph_final_wait(self):
        blk = self.blk
        s_out = [self.sem("s4outA"), self.sem("s4outB")]

        def _sync(e):
            e.wait_ge(s_out[0].h, s_out[0].v)
            e.wait_ge(s_out[1].h, s_out[1].v)
        blk.sync(_sync)

        if self.debug:
            s_fd = self.sem("dbg")

            def _a(e):
                e.wait_ge(s_out[0].h, s_out[0].v)
                e.wait_ge(s_out[1].h, s_out[1].v)
                for j, nm in enumerate(["sx", "su", "sg", "sg2", "so"]):
                    e.dma_start(out=self.d_scal[:, j:j + 1], in_=self.scal[nm][:],
                                allow_slow_non_contiguous=True).then_inc(s_fd.h, 16)
                    s_fd.v += 16
                e.wait_ge(s_fd.h, s_fd.v)
            blk.scalar(_a)


def _make_consts():
    ident = np.eye(128, dtype=np.float32)
    selE = np.zeros((DCH, H, 128), np.float32)
    for c in range(DCH):
        selE[c, 2 * c, 0:64] = 1.0
        selE[c, 2 * c + 1, 64:128] = 1.0
    return ident, selE


def _in_maps(inputs, T):
    ident, selE = _make_consts()
    x = np.ascontiguousarray(np.asarray(inputs["L0_attn_x"], np.float32))
    base = {
        "Wq": inputs["L0_attn_Wq"], "Wk": inputs["L0_attn_Wk"], "Wv": inputs["L0_attn_Wv"],
        "Wq2": inputs["L0_attn_Wq2"], "Wk2": inputs["L0_attn_Wk2"],
        "Wv2": inputs["L0_attn_Wv2"], "Wo": inputs["L0_attn_Wo"],
        "bq": inputs["L0_attn_bq"], "bk": inputs["L0_attn_bk"], "bv": inputs["L0_attn_bv"],
        "bq2": inputs["L0_attn_bq2"], "bk2": inputs["L0_attn_bk2"],
        "bv2": inputs["L0_attn_bv2"], "bo": inputs["L0_attn_bo"],
        "gamma": inputs["L0_attn_gamma"], "beta": inputs["L0_attn_beta"],
        "w_gate": inputs["L0_ffn_w_gate"], "w_up": inputs["L0_ffn_w_up"],
        "w_down": inputs["L0_ffn_w_down"],
        "ident": ident, "selE": selE,
    }
    base = {k: np.ascontiguousarray(np.asarray(v, np.float32)) for k, v in base.items()}
    maps = []
    for c in range(NCORE):
        m = dict(base)
        m["x"] = np.ascontiguousarray(x[c // 2, (c % 2) * T:(c % 2 + 1) * T, :])
        maps.append(m)
    return maps


def kernel(**inputs):
    T, TT = 2048, 512
    b = Builder(T=T, TT=TT, debug=False)
    nc = b.build()
    in_maps = _in_maps(inputs, T)
    res = run_bass_kernel_spmd(nc, in_maps, core_ids=list(range(NCORE)))
    out = np.empty((4, 4096, D), np.float32)
    for c in range(NCORE):
        out[c // 2, (c % 2) * T:(c % 2 + 1) * T, :] = np.asarray(res.results[c]["out"])
    return out


# version 13
# speedup vs baseline: 1.0596x; 1.0171x over previous
import contextlib
import numpy as np
from concourse import library_config
import concourse.bass as bass
import concourse.mybir as mybir
from bass_rust import ReduceOp
from concourse.bass_utils import run_bass_kernel_spmd

F32 = mybir.dt.float32
BF16 = mybir.dt.bfloat16
I16 = mybir.dt.int16
U16 = mybir.dt.uint16
ALU = mybir.AluOpType
AF = mybir.ActivationFunctionType
AX = mybir.AxisListType

RC = 12582912.0
QMAX = 32767.0
NCORE = 8
D = 1024
DCH = 8
H = 16
DH = 64
I3 = 3072
ICH = 24
EPS1 = 1e-5
EPS2 = 1e-3
EPSQ = 1e-6


class Sem:
    def __init__(self, nc, name):
        self.h = nc.alloc_semaphore(name)
        self.v = 0


class Builder:
    def __init__(self, T=2048, TT=512, debug=False):
        assert T % TT == 0 and TT % 128 == 0
        self.T = T
        self.TT = TT
        self.NT = T // TT
        self.TSUB = TT // 128
        self.TCH = T // 128
        self.debug = debug
        self.nc = bass.Bass("TRN2", target_bir_lowering=False)
        self.sems = {}
        self.es = {}

    def sem(self, name):
        if name not in self.sems:
            self.sems[name] = Sem(self.nc, f"s_{name}")
        return self.sems[name]

    def scope(self, name):
        self.es[name] = contextlib.ExitStack()

    def salloc(self, scope, name, shape, dt=F32):
        return self.es[scope].enter_context(self.nc.sbuf_tensor(name, shape, dt))

    def close_scope(self, name):
        self.es[name].close()

    def bank(self, i, w=None):
        w = w if w is not None else self.TT
        return self.psP[i // 2][:, (i % 2) * 512:(i % 2) * 512 + w]

    def wait_bank(self, e, i):
        e.wait_ge(self.s_bk[i].h, self.bk_uses[i])
        self.bk_uses[i] += 1

    def free_bank(self, e, i):
        e.sem_inc(self.s_bk[i].h, 1)
        self.s_bk[i].v += 1

    @staticmethod
    def fo_groups(gsz):
        gs, fo = [], 0
        while fo < DCH:
            gs.append(list(range(fo, min(fo + gsz, DCH))))
            fo += gsz
        return gs

    def declare_io(self):
        nc, T = self.nc, self.T
        g = lambda n, s, dt=F32, kind="ExternalInput": nc.dram_tensor(n, s, dt, kind=kind)
        self.x_in = g("x", [T, D])
        for w in ["Wq", "Wk", "Wv", "Wq2", "Wk2", "Wv2", "Wo"]:
            setattr(self, w, g(w, [D, D]))
        for b in ["bq", "bk", "bv", "bq2", "bk2", "bv2", "bo", "gamma", "beta"]:
            setattr(self, b, g(b, [D]))
        self.w_gate = g("w_gate", [I3, D])
        self.w_up = g("w_up", [I3, D])
        self.w_down = g("w_down", [D, I3])
        self.ident_in = g("ident", [128, 128])
        self.selE_in = g("selE", [DCH, H, 128])
        self.out = g("out", [T, D], F32, kind="ExternalOutput")

        dt_ = lambda n, s, dt=F32: nc.dram_tensor(n, s, dt)
        self.h_u_up = dt_("h_u_up", [ICH, 128, T])
        self.h_u_gate = dt_("h_u_gate", [ICH, 128, T])
        self.h_uq = dt_("h_uq", [ICH, 128, T], I16)
        self.h_p = dt_("h_p", [ICH, 128, T])
        self.h_po = dt_("h_po", [ICH, 128, T])
        self.h_s = dt_("h_s", [self.TCH, 128, D])
        self.h_qeh = dt_("h_qeh", [self.NT, 128, self.TSUB * D], BF16)
        self.h_qel = dt_("h_qel", [self.NT, 128, self.TSUB * D], BF16)
        self.h_xnh = dt_("h_xnh", [DCH, 128, T], BF16)
        self.h_xnl = dt_("h_xnl", [DCH, 128, T], BF16)
        self.h_vh = dt_("h_vh", [DCH, 128, T], BF16)
        self.h_vl = dt_("h_vl", [DCH, 128, T], BF16)
        for w in ["wq", "wk", "wv", "wq2", "wk2", "wv2", "wo"]:
            setattr(self, f"h_{w}h", dt_(f"h_{w}h", [DCH, 128, D], BF16))
            setattr(self, f"h_{w}l", dt_(f"h_{w}l", [DCH, 128, D], BF16))
        self.h_wuph = dt_("h_wuph", [I3, D], BF16)
        self.h_wupl = dt_("h_wupl", [I3, D], BF16)
        self.h_wgh = dt_("h_wgh", [I3, D], BF16)
        self.h_wgl = dt_("h_wgl", [I3, D], BF16)
        self.h_wdnh = dt_("h_wdnh", [D, I3], BF16)
        self.h_wdnl = dt_("h_wdnl", [D, I3], BF16)
        self.cc_in = dt_("cc_in", [1, 8])
        self.cc_out = dt_("cc_out", [1, 8])
        self.cc_Sin = dt_("cc_Sin", [DH, H * DH])
        self.cc_Sout = dt_("cc_Sout", [DH, H * DH])

        if self.debug:
            dbg = lambda n, s, dt=F32: nc.dram_tensor(n, s, dt, kind="ExternalOutput")
            self.d_xnh = dbg("d_xnh", [128, DCH * T], BF16)
            self.d_xnl = dbg("d_xnl", [128, DCH * T], BF16)
            self.d_scal = dbg("d_scal", [128, 8])
            self.d_S = dbg("d_S", [DH, H * DH])
            self.d_rq = dbg("d_rq", [H, T])
            self.d_s = dbg("d_s", [self.TCH, 128, D])
            self.d_u = dbg("d_u", [ICH, 128, T])
            self.d_uq = dbg("d_uq", [ICH, 128, T], I16)
            self.d_p = dbg("d_p", [ICH, 128, T])
            self.d_po = dbg("d_po", [ICH, 128, T])
            self.d_vh = dbg("d_vh", [128, DCH * T], BF16)
            self.d_vl = dbg("d_vl", [128, DCH * T], BF16)

    def alloc_static(self):
        nc, T = self.nc, self.T
        S = lambda n, s, dt=F32: nc.alloc_sbuf_tensor(n, s, dt)
        self.ident = S("c_ident", [128, 128])
        self.selE = S("c_selE", [H, DCH * 128])
        self.biases = {}
        for b in ["bq", "bk", "bv", "bq2", "bk2", "bv2", "bo"]:
            self.biases[b] = S(f"c_{b}", [128, DCH])
        self.scal = {n: S(f"sc_{n}", [128, 1]) for n in
                     ["m_up", "m_gate", "m_p", "m_po",
                      "sx", "inv_sx", "su", "inv_su", "sg", "inv_sg",
                      "sg2", "inv_sg2", "so", "inv_so", "c3", "tmp0"]}
        self.sevens = S("c_sevens", [128, 1])
        self.c_eps1 = S("c_eps1", [128, 1])
        self.c_eps2 = S("c_eps2", [128, 1])
        self.ones_row = S("c_ones_row", [1, 128])
        self.rq = S("rq", [H, T])
        self.qks_tm = S("qks_tm", [128, self.TCH * H])
        self.S_sb = S("S_sb", [DH, H * DH])
        self.Sp_f = S("Sp_f", [128, DCH * DH])
        self.SpH = S("SpH", [128, DCH * DH], BF16)
        self.SpL = S("SpL", [128, DCH * DH], BF16)
        self.amax_a = S("amax_a", [128, 1])
        self.amax_b = S("amax_b", [128, 1])
        self.ar_stage = S("ar_stage", [128, 8])
        self.psP = [nc.alloc_psum_tensor(f"psP{i}", [128, 1024], F32) for i in range(4)]
        self.s_bk = [self.sem(f"bk{i}") for i in range(8)]
        self.bk_uses = [0] * 8

    def build(self):
        self.declare_io()
        self.alloc_static()
        nc = self.nc
        with nc.Block() as blk:
            self.blk = blk
            self.ph_consts()
            self.scope("e0")
            self.scope("ln1")
            self.ph_ln1()
            self.ph_sx_scales()
            self.close_scope("ln1")
            self.ph_xi()
            self.ph_prep_and_sweep1()
            self.close_scope("e0")
            self.scope("att")
            self.ph_attn()
            self.close_scope("att")
            self.scope("a68")
            self.ph_A6A8()
            self.close_scope("a68")
            self.scope("s23")
            self.ph_sweep2()
            self.ph_sweep3_and_wdprep()
            self.close_scope("s23")
            self.scope("s4")
            self.ph_sweep4()
            self.close_scope("s4")
            self.ph_final_wait()
        return nc

    def ph_consts(self):
        blk = self.blk
        s_c = self.sem("consts")
        s_g = self.sem("ar_g")

        def _sync(e):
            e.dma_start(out=self.ident[:], in_=self.ident_in[:, :]).then_inc(s_c.h, 16)
            e.dma_start(out=self.selE[:].rearrange("h (c p) -> h c p", c=DCH),
                        in_=self.selE_in[:, :, :].transpose([1, 0, 2])
                        ).then_inc(s_c.h, 16)
            for b in ["bq", "bk", "bv", "bq2", "bk2", "bv2", "bo"]:
                src = getattr(self, b)[:].rearrange("(c p) -> p c", p=128)
                e.dma_start(out=self.biases[b][:], in_=src,
                            allow_slow_non_contiguous=True).then_inc(s_c.h, 16)
            s_c.v = 16 * 9
        blk.sync(_sync)

        def _g(e):
            e.memset(self.sevens[:], 7.0)
            e.memset(self.amax_a[:], 0.0)
            e.memset(self.c_eps1[:], EPS1)
            e.memset(self.c_eps2[:], EPS2)
            e.memset(self.ones_row[:], 1.0)
            e.drain()
            e.sem_inc(s_g.h, 1)
            s_g.v += 1
        blk.gpsimd(_g)

    def ph_ln1(self):
        nc, blk = self.nc, self.blk
        TCH = self.TCH
        s_c = self.sem("consts")
        s_g0 = self.sem("ar_g")
        s_ld = [self.sem(f"l1ld{i}") for i in range(2)]
        s_ldf = [self.sem(f"l1ldf{i}") for i in range(2)]
        s_xm = self.sem("l1xm")
        s_sq = self.sem("l1sq")
        s_xn = self.sem("l1xn")
        s_tr = self.sem("l1tr")
        s_hi = self.sem("l1hi")
        s_lo = self.sem("l1lo")
        s_xst = [self.sem("xstA"), self.sem("xstB")]

        A = lambda n, s, dt=F32: self.salloc("ln1", n, s, dt)
        xt = [A(f"l1xt{i}", [128, D]) for i in range(2)]
        xm = [A(f"l1xm{i}", [128, D]) for i in range(2)]
        xn_tm = [A(f"l1xn{i}", [128, D]) for i in range(2)]
        xnhS = [A(f"l1xnh{i}", [128, D], BF16) for i in range(2)]
        xnlS = [A(f"l1xnl{i}", [128, D], BF16) for i in range(2)]
        sq = A("l1sqb", [128, D])
        st1 = A("l1st1", [128, 1])
        st2 = [A(f"l1st2{i}", [128, 1]) for i in range(2)]
        red = A("l1red", [128, 1])

        def _sync(e):
            for i in range(TCH):
                sl = i % 2
                if i >= 2:
                    e.wait_ge(s_ldf[sl].h, i // 2)
                e.dma_start(out=xt[sl][:], in_=self.x_in[i * 128:(i + 1) * 128, :]
                            ).then_inc(s_ld[sl].h, 16)
                s_ld[sl].v += 16
        blk.sync(_sync)

        def _v(e):
            e.wait_ge(s_g0.h, 1)
            e.wait_ge(s_c.h, s_c.v)
            for i in range(TCH):
                sl = i % 2
                e.wait_ge(s_ld[sl].h, 16 * (i // 2 + 1))
                e.tensor_reduce(st1[:], xt[sl][:], axis=AX.X, op=ALU.add)
                e.drain()
                e.tensor_scalar(st1[:], st1[:], 1.0 / D, None, ALU.mult)
                e.drain()
                if i >= 2:
                    e.wait_ge(s_sq.h, i - 1)
                e.tensor_scalar(xm[sl][:], xt[sl][:], st1[:], None, ALU.subtract)
                e.drain()
                e.sem_inc(s_ldf[sl].h, 1)
                s_ldf[sl].v += 1
                e.sem_inc(s_xm.h, 1)
                s_xm.v += 1
                e.wait_ge(s_sq.h, i + 1)
                e.reciprocal(red[:], st2[sl][:])
                e.drain()
                if i >= 2:
                    e.wait_ge(s_tr.h, i - 1)
                e.tensor_scalar(xn_tm[sl][:], xm[sl][:], red[:], None, ALU.mult)
                e.drain()
                e.tensor_reduce(st1[:], xn_tm[sl][:], axis=AX.X, op=ALU.max, apply_absolute_value=True)
                e.drain()
                e.tensor_tensor(self.amax_a[:], self.amax_a[:], st1[:], ALU.max)
                e.drain()
                e.sem_inc(s_xn.h, 1)
                s_xn.v += 1
                e.wait_ge(s_hi.h, i + 1)
                bj = i % 2
                e.tensor_tensor(xnlS[bj][:], self.psP[bj][:, 0:D], xnhS[bj][:],
                                ALU.subtract)
                e.drain()
                self.free_bank(e, 2 * bj)
                self.free_bank(e, 2 * bj + 1)
                e.sem_inc(s_lo.h, 1)
                s_lo.v += 1
        blk.vector(_v)

        def _a(e):
            e.wait_ge(s_g0.h, 1)
            for i in range(TCH):
                sl = i % 2
                e.wait_ge(s_xm.h, i + 1)
                if i >= 2:
                    e.wait_ge(s_xn.h, i - 1)
                e.activation(sq[:], xm[sl][:], AF.Square, accum_out=st2[sl][:])
                e.drain()
                e.activation(st2[sl][:], st2[sl][:], AF.Sqrt, bias=self.c_eps1[:], scale=1.0 / D)
                e.drain()
                e.sem_inc(s_sq.h, 1)
                s_sq.v += 1
                e.wait_ge(s_tr.h, i + 1)
                bj = i % 2
                if i >= 2:
                    e.wait_ge(s_xst[bj].h, 32 * (i // 2))
                e.activation(xnhS[bj][:], self.psP[bj][:, 0:D], AF.Copy)
                e.drain()
                e.sem_inc(s_hi.h, 1)
                s_hi.v += 1
                e.wait_ge(s_lo.h, i + 1)
                dsth = self.h_xnh[:, :, i * 128:(i + 1) * 128].transpose([1, 0, 2])
                dstl = self.h_xnl[:, :, i * 128:(i + 1) * 128].transpose([1, 0, 2])
                e.dma_start(out=dsth, in_=xnhS[bj][:].rearrange("p (c t) -> p c t", c=DCH)
                            ).then_inc(s_xst[bj].h, 16)
                e.dma_start(out=dstl, in_=xnlS[bj][:].rearrange("p (c t) -> p c t", c=DCH)
                            ).then_inc(s_xst[bj].h, 16)
                s_xst[bj].v += 32
        blk.scalar(_a)

        def _t(e):
            e.wait_ge(s_c.h, s_c.v)
            for i in range(TCH):
                sl = i % 2
                bj = i % 2
                e.wait_ge(s_xn.h, i + 1)
                self.wait_bank(e, 2 * bj)
                self.wait_bank(e, 2 * bj + 1)
                for c in range(DCH):
                    instr = e.transpose(self.psP[bj][:, c * 128:(c + 1) * 128],
                                        xn_tm[sl][:, c * 128:(c + 1) * 128], self.ident[:])
                instr.then_inc(s_tr.h, 1)
                s_tr.v += 1
        blk.tensor(_t)

    def ar_scalar(self, name, srcs):
        blk = self.blk
        n_src = len(srcs)
        s_d = self.sem("ar_d")
        s_g = self.sem("ar_g")
        s_v = self.sem("l1xm")
        s_t = self.sem("l1tr")
        s_done = self.sem("ar_done")

        def _v0(e):
            e.drain()
            e.sem_inc(s_v.h, 1)
            s_v.v += 1
        blk.vector(_v0)

        def _t0(e):
            e.wait_ge(s_v.h, s_v.v)
            self.wait_bank(e, 4)
            for j, srcap in enumerate(srcs):
                instr = e.transpose(self.psP[2][0:1, j * 128:(j + 1) * 128],
                                    srcap, self.ident[:])
            instr.then_inc(s_t.h, 1)
            s_t.v += 1
        blk.tensor(_t0)

        def _v1(e):
            e.wait_ge(s_t.h, s_t.v)
            for j in range(n_src):
                e.tensor_reduce(self.ar_stage[0:1, j:j + 1],
                                self.psP[2][0:1, j * 128:(j + 1) * 128],
                                axis=AX.X, op=ALU.max)
            e.drain()
            self.free_bank(e, 4)
            e.sem_inc(s_v.h, 1)
            s_v.v += 1
        blk.vector(_v1)

        def _sy(e):
            e.wait_ge(s_v.h, s_v.v)
            e.dma_start(out=self.cc_in[0:1, 0:n_src],
                        in_=self.ar_stage[0:1, 0:n_src]).then_inc(s_d.h, 16)
            s_d.v += 16
        blk.sync(_sy)

        def _g2(e):
            e.wait_ge(s_d.h, s_d.v)
            cc = e.collective_compute(
                "AllReduce", ALU.max,
                replica_groups=[list(range(NCORE))],
                ins=[self.cc_in[0:1, 0:n_src].opt()],
                outs=[self.cc_out[0:1, 0:n_src].opt()])
            cc.then_inc(s_g.h, 1)
            s_g.v += 1
        blk.gpsimd(_g2)

        def _sy2(e):
            e.wait_ge(s_g.h, s_g.v)
            e.dma_start(out=self.ar_stage[0:1, 0:n_src],
                        in_=self.cc_out[0:1, 0:n_src]).then_inc(s_d.h, 16)
            s_d.v += 16
        blk.sync(_sy2)

        def _t1(e):
            e.wait_ge(s_d.h, s_d.v)
            self.wait_bank(e, 4)
            mm = e.matmul(self.psP[2][:, 512:512 + n_src], self.ones_row[:],
                          self.ar_stage[0:1, 0:n_src], start=True, stop=True)
            mm.then_inc(s_t.h, 1)
            s_t.v += 1
        blk.tensor(_t1)

        def _v2(e):
            e.wait_ge(s_t.h, s_t.v)
            e.tensor_copy(self.ar_stage[:, 0:n_src], self.psP[2][:, 512:512 + n_src])
            e.drain()
            self.free_bank(e, 4)
            e.sem_inc(s_done.h, 1)
            s_done.v += 1
        blk.vector(_v2)
        return s_done

    def ph_sx_scales(self):
        blk = self.blk
        s_xn = self.sem("l1xn")

        s_done = self.ar_scalar("sxar", [self.amax_a[:]])
        s_sc = self.sem("scd")

        def _v(e):
            e.wait_ge(s_done.h, s_done.v)
            e.tensor_scalar(self.scal["sx"][:], self.ar_stage[:, 0:1], 1.0 / QMAX,
                            1e-12, ALU.mult, ALU.max)
            e.drain()
            e.reciprocal(self.scal["inv_sx"][:], self.scal["sx"][:])
            e.drain()
            e.sem_inc(s_sc.h, 1)
            s_sc.v += 1
        blk.vector(_v)

    def ph_xi(self):
        T, blk = self.T, self.blk
        s_sc = self.sem("scd")
        s_lo = self.sem("l1lo")
        s_xst = [self.sem("xstA"), self.sem("xstB")]
        s_xi = self.sem("l1xm")
        s_ld = [self.sem(f"l1ld{i}") for i in range(2)]
        s_ldf = [self.sem(f"l1ldf{i}") for i in range(2)]
        b_ld = [s.v for s in s_ld]
        b_ldf = [s.v for s in s_ldf]
        self.xiH = self.salloc("e0", "xiH", [128, DCH * T], BF16)
        self.xiL = self.salloc("e0", "xiL", [128, DCH * T], BF16)
        xnc_h = [self.salloc("e0", f"xinh{i}", [128, T], BF16) for i in range(2)]
        xnc_l = [self.salloc("e0", f"xinl{i}", [128, T], BF16) for i in range(2)]
        xi_f = self.salloc("e0", "xi_f", [128, T])

        def _sync(e):
            e.wait_ge(s_xst[0].h, s_xst[0].v)
            e.wait_ge(s_xst[1].h, s_xst[1].v)
            for c in range(DCH):
                sl = c % 2
                if c >= 2:
                    e.wait_ge(s_ldf[sl].h, b_ldf[sl] + c // 2)
                e.dma_start(out=xnc_h[sl][:], in_=self.h_xnh[c]).then_inc(s_ld[sl].h, 16)
                e.dma_start(out=xnc_l[sl][:], in_=self.h_xnl[c]).then_inc(s_ld[sl].h, 16)
                s_ld[sl].v += 32
        blk.sync(_sync)

        def _v(e):
            e.wait_ge(s_sc.h, 1)
            for c in range(DCH):
                sl = c % 2
                cs = slice(c * T, (c + 1) * T)
                e.wait_ge(s_ld[sl].h, b_ld[sl] + 32 * (c // 2 + 1))
                e.tensor_tensor(xi_f[:], xnc_h[sl][:], xnc_l[sl][:], ALU.add)
                e.drain()
                e.sem_inc(s_ldf[sl].h, 1)
                s_ldf[sl].v += 1
                e.tensor_scalar(xi_f[:], xi_f[:], self.scal["inv_sx"][:], RC,
                                ALU.mult, ALU.add)
                e.drain()
                e.tensor_scalar(xi_f[:], xi_f[:], RC, None, ALU.subtract)
                e.drain()
                e.tensor_copy(self.xiH[:, cs], xi_f[:])
                e.drain()
                e.tensor_tensor(self.xiL[:, cs], xi_f[:], self.xiH[:, cs], ALU.subtract)
                e.drain()
                e.sem_inc(s_xi.h, 1)
                s_xi.v += 1
        blk.vector(_v)

    def ph_prep_and_sweep1(self):
        nc, blk, T, TT, NT = self.nc, self.blk, self.T, self.TT, self.NT
        s_xi = self.sem("l1xm")
        s_lo = self.sem("l1lo")

        s_pld = [self.sem(f"ppld{i}") for i in range(2)]
        s_pldf = [self.sem(f"ppldf{i}") for i in range(2)]
        s_pg = self.sem("ppg")
        s_pst = [self.sem("ppst0"), self.sem("ppst1")]
        A = lambda n, s, dt=F32: self.salloc("e0", n, s, dt)
        win = [A(f"ppin{i}", [128, D]) for i in range(2)]
        bmax = A("ppbm", [128, 32])
        sblk = [A(f"pps{i}", [128, 32]) for i in range(2)]
        rblk = [A(f"ppr{i}", [128, 32]) for i in range(2)]
        tq = A("ppt", [128, D])
        wh = [A(f"pph{i}", [128, D], BF16) for i in range(2)]
        wl = [A(f"ppl{i}", [128, D], BF16) for i in range(2)]

        up_jobs = [(self.w_up[ic * 128:(ic + 1) * 128, :],
                    self.h_wuph[ic * 128:(ic + 1) * 128, :],
                    self.h_wupl[ic * 128:(ic + 1) * 128, :], True) for ic in range(ICH)]
        gate_jobs = [(self.w_gate[ic * 128:(ic + 1) * 128, :],
                      self.h_wgh[ic * 128:(ic + 1) * 128, :],
                      self.h_wgl[ic * 128:(ic + 1) * 128, :], True) for ic in range(ICH)]
        aw_jobs = []
        nbase = len(up_jobs) + len(gate_jobs)
        for wname, W in [("wq", self.Wq), ("wq2", self.Wq2), ("wk", self.Wk),
                         ("wk2", self.Wk2), ("wv", self.Wv), ("wv2", self.Wv2),
                         ("wo", self.Wo)]:
            hh = getattr(self, f"h_{wname}h")
            hl = getattr(self, f"h_{wname}l")
            for c in range(DCH):
                aw_jobs.append((W[c * 128:(c + 1) * 128, :], hh[c], hl[c], False))
        self.prep_counts = {"up": len(up_jobs), "gate": nbase,
                            "all": nbase + len(aw_jobs)}

        NQ = 8
        NOC = ICH // NQ
        seq = []
        q_need = {}
        aw_pos = {}

        def _push(jobs):
            p0 = len(seq)
            seq.extend(jobs)
            return list(range(p0, len(seq)))

        hjs = list(range(6, 2 * NQ))
        per = [len(aw_jobs) // len(hjs)] * len(hjs)
        for i in range(len(aw_jobs) - sum(per)):
            per[-1 - i] += 1
        q_need[0] = _push(up_jobs[0:NOC])
        q_need[1] = _push(up_jobs[NOC:2 * NOC])
        awi = 0
        for hj in range(2 * NQ):
            q = hj + 2
            if q < 2 * NQ:
                jl = up_jobs if q < NQ else gate_jobs
                qf = q if q < NQ else q - NQ
                q_need[q] = _push(jl[qf * NOC:(qf + 1) * NOC])
            if hj in hjs:
                n = per[hjs.index(hj)]
                aw_pos[hj] = list(range(awi, awi + n))
                awi += n
            else:
                aw_pos[hj] = []

        s_ald = [self.sem("wdld0"), self.sem("wdld1")]
        s_aldf = self.sem("wdldf0")
        s_ag = self.sem("wdg")
        s_ast = [self.sem("wdst0"), self.sem("wdst1")]
        awin = [A(f"awin{i}", [128, D]) for i in range(2)]
        awh = [A(f"awh{i}", [128, D], BF16) for i in range(2)]
        awl = [A(f"awl{i}", [128, D], BF16) for i in range(2)]

        def emit_aw_load(e, a):
            sl = a % 2
            if a >= 2:
                e.wait_ge(s_aldf.h, a - 1)
            e.dma_start(out=awin[sl][:], in_=aw_jobs[a][0]).then_inc(s_ald[sl].h, 16)
            s_ald[sl].v += 16

        def emit_aw_gp(e, a):
            sl = a % 2
            e.wait_ge(s_ald[sl].h, 16 * (a // 2 + 1))
            if a >= 2:
                e.wait_ge(s_ast[sl].h, 32 * (a // 2))
            e.tensor_copy(awh[sl][:], awin[sl][:])
            e.drain()
            e.tensor_tensor(awl[sl][:], awin[sl][:], awh[sl][:], ALU.subtract)
            e.drain()
            e.sem_inc(s_aldf.h, 1)
            s_aldf.v += 1
            e.sem_inc(s_ag.h, 1)
            s_ag.v += 1

        def emit_aw_store(e, a):
            sl = a % 2
            e.wait_ge(s_ag.h, a + 1)
            e.dma_start(out=aw_jobs[a][1], in_=awh[sl][:]).then_inc(s_ast[sl].h, 16)
            e.dma_start(out=aw_jobs[a][2], in_=awl[sl][:]).then_inc(s_ast[sl].h, 16)
            s_ast[sl].v += 32

        def emit_prep_sync_job(e, j):
            src, dh, dl, lp = seq[j]
            sl = j % 2
            if j >= 2:
                e.wait_ge(s_pldf[sl].h, j // 2)
            e.dma_start(out=win[sl][:], in_=src).then_inc(s_pld[sl].h, 16)
            s_pld[sl].v += 16
            e.wait_ge(s_pg.h, j + 1)
            e.dma_start(out=dh, in_=wh[sl][:]).then_inc(s_pst[sl].h, 16)
            e.dma_start(out=dl, in_=wl[sl][:]).then_inc(s_pst[sl].h, 16)
            s_pst[sl].v += 32

        def emit_prep_dve(e, j):
            src, dh, dl, lp = seq[j]
            sl = j % 2
            e.wait_ge(s_pld[sl].h, 16 * (j // 2 + 1))
            if j >= 2:
                e.wait_ge(s_pst[sl].h, 32 * (j // 2))
            if lp:
                e.tensor_reduce(bmax[:], win[sl][:].rearrange("p (a b) -> p a b", b=32),
                                axis=AX.X, op=ALU.max, apply_absolute_value=True)
                e.drain()
                e.tensor_scalar(sblk[sl][:], bmax[:], 1.0 / 7.0, 1e-12,
                                ALU.mult, ALU.max)
                e.drain()
                e.reciprocal(rblk[sl][:], sblk[sl][:])
                e.drain()
                e.tensor_tensor(tq[:], win[sl][:],
                                rblk[sl][:].rearrange("p (a b) -> p a b", b=1)
                                .broadcast_to([128, 32, 32]), ALU.mult)
                e.drain()
                e.sem_inc(s_pldf[sl].h, 1)
                s_pldf[sl].v += 1
                e.tensor_scalar(tq[:], tq[:], RC, RC, ALU.add, ALU.subtract)
                e.drain()
                e.tensor_tensor(tq[:], tq[:],
                                sblk[sl][:].rearrange("p (a b) -> p a b", b=1)
                                .broadcast_to([128, 32, 32]), ALU.mult)
                e.drain()
                src_dq = tq
            else:
                src_dq = win[sl]
            e.tensor_copy(wh[sl][:], src_dq[:])
            e.drain()
            e.tensor_tensor(wl[sl][:], src_dq[:], wh[sl][:], ALU.subtract)
            e.drain()
            if not lp:
                e.sem_inc(s_pldf[sl].h, 1)
                s_pldf[sl].v += 1
            e.sem_inc(s_pg.h, 1)
            s_pg.v += 1

        NWS = 3
        s_w1 = [self.sem(f"s1w{i}") for i in range(NWS)]
        s_w1f = [self.sem(f"s1wf{i}") for i in range(NWS)]
        s_mm = self.sem("s1mm")
        s_ev = self.sem("s1ev")
        s_st = [self.sem("s1st0"), self.sem("s1st1")]
        HOW = I3 // NQ
        wT_h = [A(f"s1th{i}", [128, DCH * HOW], BF16) for i in range(NWS)]
        wT_l = [A(f"s1tl{i}", [128, DCH * HOW], BF16) for i in range(NWS)]
        ub = [A(f"s1u{i}", [128, TT]) for i in range(2)]
        quarters = [(nm, qf) for nm in ["up", "gate"] for qf in range(NQ)]
        self.s1_total = len(quarters) * NOC * NT

        def emit_s1_wload(e, hj, kcs):
            nm, qf = quarters[hj]
            sl = hj % NWS
            m = max(q_need[hj])
            e.wait_ge(s_pst[0].h, 32 * (m // 2 + 1))
            e.wait_ge(s_pst[1].h, 32 * ((m + 1) // 2))
            if hj >= NWS:
                e.wait_ge(s_w1f[sl].h, hj // NWS)
            hsrc = self.h_wuph if nm == "up" else self.h_wgh
            lsrc = self.h_wupl if nm == "up" else self.h_wgl
            if hj == 0:
                e.wait_ge(s_lo.h, self.TCH)
            for kc in kcs:
                e.dma_start(
                    out=wT_h[sl][:, kc * HOW:(kc + 1) * HOW],
                    in_=hsrc[qf * HOW:(qf + 1) * HOW, kc * 128:(kc + 1) * 128],
                    transpose=True).then_inc(s_w1[sl].h, 16)
                e.dma_start(
                    out=wT_l[sl][:, kc * HOW:(kc + 1) * HOW],
                    in_=lsrc[qf * HOW:(qf + 1) * HOW, kc * 128:(kc + 1) * 128],
                    transpose=True).then_inc(s_w1[sl].h, 16)
                s_w1[sl].v += 32

        KC_SY = list(range(DCH))

        def _sync(e):
            for q in (0, 1):
                for j in q_need[q]:
                    emit_prep_sync_job(e, j)
            emit_s1_wload(e, 0, KC_SY)
            emit_s1_wload(e, 1, KC_SY)
            for hj in range(2 * NQ):
                if hj + 2 < 2 * NQ:
                    for j in q_need[hj + 2]:
                        emit_prep_sync_job(e, j)
                    emit_s1_wload(e, hj + 2, KC_SY)
        blk.sync(_sync)

        def _g(e):
            for hj in range(2 * NQ):
                for a in aw_pos[hj]:
                    emit_aw_gp(e, a)
        blk.gpsimd(_g)

        def emit_s1_evacs(e, hj0, hj1, nmm):
            for hj in range(hj0, hj1):
                nm, qf = quarters[hj]
                acc = self.scal["m_up"] if nm == "up" else self.scal["m_gate"]
                for oc in range(NOC):
                    for tt in range(NT):
                        ubuf = ub[nmm % 2]
                        bk = nmm % 2
                        par = nmm % 2
                        nmm += 1
                        e.wait_ge(s_mm.h, nmm)
                        if nmm > 2:
                            e.wait_ge(s_st[par].h, 16 * ((nmm - 1) // 2))
                        e.tensor_scalar(ubuf[:], self.bank(bk), self.scal["sx"][:],
                                        None, ALU.mult)
                        e.drain()
                        self.free_bank(e, bk)
                        e.tensor_reduce(self.amax_b[:], ubuf[:], axis=AX.X,
                                        op=ALU.max, apply_absolute_value=True)
                        e.drain()
                        if qf == 0 and oc == 0 and tt == 0:
                            e.tensor_copy(acc[:], self.amax_b[:])
                        else:
                            e.tensor_tensor(acc[:], acc[:], self.amax_b[:], ALU.max)
                        e.drain()
                        e.sem_inc(s_ev.h, 1)
                        s_ev.v += 1
                e.sem_inc(s_w1f[hj % NWS].h, 1)
                s_w1f[hj % NWS].v += 1
            return nmm

        def _v(e):
            for q in (0, 1):
                for j in q_need[q]:
                    emit_prep_dve(e, j)
            nmm = 0
            for hj in range(2 * NQ):
                if hj + 2 < 2 * NQ:
                    for j in q_need[hj + 2]:
                        emit_prep_dve(e, j)
                nmm = emit_s1_evacs(e, hj, hj + 1, nmm)
        blk.vector(_v)

        def _t(e):
            e.wait_ge(s_xi.h, s_xi.v)
            nmm = 0
            for hj, (nm, qf) in enumerate(quarters):
                sl = hj % NWS
                e.wait_ge(s_w1[sl].h, 32 * DCH * (hj // NWS + 1))
                for oc in range(NOC):
                    for tt in range(NT):
                        bk = nmm % 2
                        nmm += 1
                        self.wait_bank(e, bk)
                        first = True
                        for kc in range(DCH):
                            lw_h = wT_h[sl][:, kc * HOW + oc * 128: kc * HOW + (oc + 1) * 128]
                            lw_l = wT_l[sl][:, kc * HOW + oc * 128: kc * HOW + (oc + 1) * 128]
                            rx_h = self.xiH[:, kc * T + tt * TT: kc * T + (tt + 1) * TT]
                            rx_l = self.xiL[:, kc * T + tt * TT: kc * T + (tt + 1) * TT]
                            e.matmul(self.bank(bk), lw_h, rx_h, start=first, stop=False)
                            first = False
                            e.matmul(self.bank(bk), lw_h, rx_l, start=False, stop=False)
                            mm = e.matmul(self.bank(bk), lw_l, rx_h, start=False,
                                          stop=(kc == DCH - 1))
                        mm.then_inc(s_mm.h, 1)
                        s_mm.v += 1
        blk.tensor(_t)

        def _a(e):
            nmm = 0
            for hj, (nm, qf) in enumerate(quarters):
                dst = self.h_u_up if nm == "up" else self.h_u_gate
                for oc in range(NOC):
                    ic = qf * NOC + oc
                    for tt in range(NT):
                        nmm += 1
                        e.wait_ge(s_ev.h, nmm)
                        par = (nmm - 1) % 2
                        e.dma_start(out=dst[ic, :, tt * TT:(tt + 1) * TT],
                                    in_=ub[par][:]).then_inc(s_st[par].h, 16)
                        s_st[par].v += 16
                for a in aw_pos[hj]:
                    if a - 2 >= 0:
                        emit_aw_store(e, a - 2)
                    emit_aw_load(e, a)
            for a in (len(aw_jobs) - 2, len(aw_jobs) - 1):
                emit_aw_store(e, a)
        blk.scalar(_a)

        s_done = self.ar_scalar("ugar", [self.scal["m_up"][:], self.scal["m_gate"][:]])
        s_sc2 = self.sem("scd")

        def _v2(e):
            e.wait_ge(s_done.h, s_done.v)
            for j, nm in enumerate(["su", "sg"]):
                e.tensor_scalar(self.scal[nm][:], self.ar_stage[:, j:j + 1], 1.0 / QMAX,
                                1e-12, ALU.mult, ALU.max)
                e.drain()
                e.reciprocal(self.scal["inv_" + nm][:], self.scal[nm][:])
                e.drain()
            e.sem_inc(s_sc2.h, 1)
            s_sc2.v += 1
        blk.vector(_v2)

        if self.debug:
            s_dbg = self.sem("dbg")

            def _dbg(e):
                e.wait_ge(s_st[0].h, s_st[0].v)
                e.wait_ge(s_st[1].h, s_st[1].v)
                for ic in range(ICH):
                    e.dma_start(out=self.d_u[ic], in_=self.h_u_up[ic]
                                ).then_inc(s_dbg.h, 16)
                    s_dbg.v += 16
            blk.scalar(_dbg)

    def attn_iter(self):
        for pi, ph in enumerate(self.attnP):
            pname, gsz = ph[0], ph[7]
            for tt in range(self.NT):
                for st in (0, 1):
                    for grp in self.fo_groups(gsz):
                        yield (pi, pname, tt, st, grp)

    def ph_attn(self):
        nc, blk, T, TT, NT, TSUB = self.nc, self.blk, self.T, self.TT, self.NT, self.TSUB
        s_lo = self.sem("l1lo")
        s_pst = [self.sem("ppst0"), self.sem("ppst1")]
        s_s1st = [self.sem("s1st0"), self.sem("s1st1")]

        s_stg = [self.sem(f"ws{i}") for i in range(2)]
        s_stgf = [self.sem(f"wsf{i}") for i in range(2)]
        s_xs = [self.sem(f"atxs{i}") for i in range(2)]
        s_xsf = self.sem("atxsf")
        A = lambda n, s, dt=F32: self.salloc("att", n, s, dt)
        wSH = [A(f"wsrh{i}", [128, DCH * D], BF16) for i in range(2)]
        wSL = [A(f"wsrl{i}", [128, DCH * D], BF16) for i in range(2)]
        xsTH = [A(f"xsth{i}", [128, DCH * TT], BF16) for i in range(2)]
        xsTL = [A(f"xstl{i}", [128, DCH * TT], BF16) for i in range(2)]

        t1H = A("att1h", [128, DCH * TT], BF16)
        t1L = A("att1l", [128, DCH * TT], BF16)
        qe_fm = A("atqefm", [128, DCH * TT])
        elu_a = A("atelua", [128, TT])
        elu_b = A("atelub", [128, TT])
        tmH = A("attmh", [128, TSUB * D], BF16)
        tmL = A("attml", [128, TSUB * D], BF16)
        qeH = A("atqeh2", [128, TSUB * D], BF16)
        qeL = A("atqel2", [128, TSUB * D], BF16)
        pk1 = A("atpk1", [128, D])
        pk2 = A("atpk2", [128, D])
        qksr = A("atqksr", [128, H])
        vstH = [A(f"atvsth{i}", [128, TT], BF16) for i in range(2)]
        vstL = [A(f"atvstl{i}", [128, TT], BF16) for i in range(2)]
        self.b_qest = [32 * ((self.NT + 1) // 2), 32 * (self.NT // 2)]

        s_mmg = self.sem("atmmg")
        s_ev1 = self.sem("atev1")
        s_qe = self.sem("atqe")
        s_exp = self.sem("atexp")
        s_elu_z = self.sem("atz")
        s_tr = self.sem("attr")
        s_hi = self.sem("athi")
        s_lo2 = self.sem("atlo")
        s_qest = [self.sem("atqestA"), self.sem("atqestB")]
        s_qeld = self.sem("atqeld")
        s_qefree = self.sem("atqefree")
        s_qks = self.sem("atqks")

        self.attnP = [
            ("q", self.h_wqh, self.h_wql, "bq", self.h_wq2h, self.h_wq2l, "bq2", 8),
            ("k", self.h_wkh, self.h_wkl, "bk", self.h_wk2h, self.h_wk2l, "bk2", 6),
            ("v", self.h_wvh, self.h_wvl, "bv", self.h_wv2h, self.h_wv2l, "bv2", 8)]
        psS = self.psP[3][0:DH, :]

        s_ast = [self.sem("wdst0"), self.sem("wdst1")]

        def emit_wstage(e, s):
            ph = self.attnP[s // 2]
            hsrc = ph[1] if s % 2 == 0 else ph[4]
            lsrc = ph[2] if s % 2 == 0 else ph[5]
            sl = s % 2
            if s >= 2:
                e.wait_ge(s_stgf[sl].h, s // 2)
            for kc in range(DCH):
                e.dma_start(out=wSH[sl][:, kc * D:(kc + 1) * D], in_=hsrc[kc]
                            ).then_inc(s_stg[sl].h, 16)
                e.dma_start(out=wSL[sl][:, kc * D:(kc + 1) * D], in_=lsrc[kc]
                            ).then_inc(s_stg[sl].h, 16)
                s_stg[sl].v += 32

        def _sync(e):
            e.wait_ge(s_pst[0].h, s_pst[0].v)
            e.wait_ge(s_pst[1].h, s_pst[1].v)
            e.wait_ge(s_ast[0].h, s_ast[0].v)
            e.wait_ge(s_ast[1].h, s_ast[1].v)
            e.wait_ge(s_s1st[0].h, 16 * ((self.s1_total + 1) // 2))
            e.wait_ge(s_s1st[1].h, 16 * (self.s1_total // 2))
            if self.debug:
                sdbg = self.sem("dbg")
                e.wait_ge(sdbg.h, sdbg.v)
            emit_wstage(e, 0)
            emit_wstage(e, 1)
            for pi, ph in enumerate(self.attnP):
                pname = ph[0]
                for tt in range(NT):
                    n = pi * NT + tt
                    sl = n % 2
                    if n >= 2:
                        e.wait_ge(s_xsf.h, n - 1)
                    cs = slice(tt * self.TT, (tt + 1) * self.TT)
                    for kc in range(DCH):
                        e.dma_start(out=xsTH[sl][:, kc * TT:(kc + 1) * TT],
                                    in_=self.h_xnh[kc][:, cs]).then_inc(s_xs[sl].h, 16)
                        e.dma_start(out=xsTL[sl][:, kc * TT:(kc + 1) * TT],
                                    in_=self.h_xnl[kc][:, cs]).then_inc(s_xs[sl].h, 16)
                        s_xs[sl].v += 32
                    if pname == "k":
                        e.wait_ge(s_qest[tt % 2].h, 32 * (tt // 2 + 1))
                        if tt >= 1:
                            e.wait_ge(s_qefree.h, 2 * tt)
                        e.dma_start(out=qeH[:], in_=self.h_qeh[tt]).then_inc(s_qeld.h, 16)
                        e.dma_start(out=qeL[:], in_=self.h_qel[tt]).then_inc(s_qeld.h, 16)
                        s_qeld.v += 32
                    if tt == 0 and pi > 0:
                        emit_wstage(e, 2 * pi + 1)
                if 2 * pi + 2 < 6:
                    emit_wstage(e, 2 * pi + 2)
        blk.sync(_sync)

        def _t(e):
            e.wait_ge(s_lo.h, self.TCH)

            def mm_group(grp, s, rhsH, rhsL, col_of, xn=None):
                wsl = s % 2
                for fi, fo in enumerate(grp):
                    self.wait_bank(e, fi)
                for kc in range(DCH):
                    for fi, fo in enumerate(grp):
                        lw_h = wSH[wsl][:, kc * D + fo * 128:kc * D + (fo + 1) * 128]
                        lw_l = wSL[wsl][:, kc * D + fo * 128:kc * D + (fo + 1) * 128]
                        if rhsH is None:
                            rx_h = xsTH[xn][:, kc * TT:(kc + 1) * TT]
                            rx_l = xsTL[xn][:, kc * TT:(kc + 1) * TT]
                        else:
                            rx_h = rhsH[:, col_of(kc)]
                            rx_l = rhsL[:, col_of(kc)]
                        e.matmul(self.bank(fi), lw_h, rx_h, start=(kc == 0), stop=False)
                        e.matmul(self.bank(fi), lw_h, rx_l, start=False, stop=False)
                        mm = e.matmul(self.bank(fi), lw_l, rx_h, start=False,
                                      stop=(kc == DCH - 1))
                mm.then_inc(s_mmg.h, 1)
                s_mmg.v += 1
                return mm

            for pi, ph in enumerate(self.attnP):
                pname, gsz = ph[0], ph[7]
                for tt in range(NT):
                    n = pi * NT + tt
                    xsl = n % 2
                    e.wait_ge(s_xs[xsl].h, 32 * DCH * (n // 2 + 1))
                    if tt == 0:
                        e.wait_ge(s_stg[0].h, 32 * DCH * (pi + 1))
                    grps = self.fo_groups(gsz)
                    for gi, grp in enumerate(grps):
                        mm = mm_group(grp, 2 * pi, None, None, None, xn=xsl)
                    e.drain()
                    e.sem_inc(s_xsf.h, 1)
                    s_xsf.v += 1
                    if tt == NT - 1:
                        e.sem_inc(s_stgf[0].h, 1)
                        s_stgf[0].v += 1
                    if tt == 0:
                        e.wait_ge(s_stg[1].h, 32 * DCH * (pi + 1))
                    e.wait_ge(s_ev1.h, pi * NT + tt + 1)
                    for gi, grp in enumerate(grps):
                        mm = mm_group(grp, 2 * pi + 1, t1H, t1L,
                                      lambda kc: slice(kc * TT, (kc + 1) * TT))
                    if tt == NT - 1:
                        e.drain()
                        e.sem_inc(s_stgf[1].h, 1)
                        s_stgf[1].v += 1
                    if pname in ("q", "k"):
                        if pname == "k":
                            e.wait_ge(s_qks.h, tt)
                        e.wait_ge(s_qe.h, pi * NT + tt + 1)
                        for ts_ in range(TSUB):
                            bj = ts_ % 2
                            self.wait_bank(e, 2 * bj)
                            self.wait_bank(e, 2 * bj + 1)
                            for c in range(DCH):
                                instr = e.transpose(
                                    self.psP[bj][:, c * 128:(c + 1) * 128],
                                    qe_fm[:, c * TT + ts_ * 128: c * TT + (ts_ + 1) * 128],
                                    self.ident[:])
                            instr.then_inc(s_tr.h, 1)
                            s_tr.v += 1
                    if pname == "k":
                        if tt == 0:
                            self.wait_bank(e, 6)
                            self.wait_bank(e, 7)
                        e.wait_ge(s_lo2.h, (NT + tt) * TSUB + TSUB)
                        e.wait_ge(s_qeld.h, 32 * (tt + 1))
                        for ts_ in range(TSUB):
                            for h in range(H):
                                qh_ = qeH[:, ts_ * D + h * DH: ts_ * D + (h + 1) * DH]
                                ql_ = qeL[:, ts_ * D + h * DH: ts_ * D + (h + 1) * DH]
                                kh_ = tmH[:, ts_ * D + h * DH: ts_ * D + (h + 1) * DH]
                                kl_ = tmL[:, ts_ * D + h * DH: ts_ * D + (h + 1) * DH]
                                po = psS[:, h * DH:(h + 1) * DH]
                                firstS = (tt == 0 and ts_ == 0 and h % 8 == 0)
                                e.matmul(po, kh_, qh_, start=firstS, stop=False,
                                         skip_group_check=True)
                                e.matmul(po, kh_, ql_, start=False, stop=False,
                                         skip_group_check=True)
                                mm = e.matmul(po, kl_, qh_, start=False,
                                              stop=(tt == NT - 1 and ts_ == TSUB - 1),
                                              skip_group_check=True)
                        mm.then_inc(s_qefree.h, 1)
                        s_qefree.v += 1
            e.wait_ge(s_qks.h, NT + 1)
            per = 4
            self.n_rq = 0
            for base in range(0, self.TCH, per):
                self.wait_bank(e, 0)
                for j in range(per):
                    i = base + j
                    if i >= self.TCH:
                        break
                    instr = e.transpose(self.psP[0][0:H, j * 128:(j + 1) * 128],
                                        self.qks_tm[:, i * H:(i + 1) * H], self.ident[:])
                instr.then_inc(s_tr.h, 1)
                s_tr.v += 1
                self.n_rq += 1
        blk.tensor(_t)

        def _v(e):
            ngrp = 0
            for pi, ph in enumerate(self.attnP):
                pname, gsz = ph[0], ph[7]
                bias1 = self.biases[ph[3]]
                bias2 = self.biases[ph[6]]
                for tt in range(NT):
                    for grp in self.fo_groups(gsz):
                        ngrp += 1
                        e.wait_ge(s_mmg.h, ngrp)
                        for fi, fo in enumerate(grp):
                            dst_h = t1H[:, fo * TT:(fo + 1) * TT]
                            dst_l = t1L[:, fo * TT:(fo + 1) * TT]
                            e.tensor_scalar(dst_h, self.bank(fi), bias1[:, fo:fo + 1],
                                            None, ALU.add)
                            e.drain()
                            e.scalar_tensor_tensor(dst_l, self.bank(fi),
                                                   bias1[:, fo:fo + 1], dst_h,
                                                   ALU.add, ALU.subtract)
                            e.drain()
                            self.free_bank(e, fi)
                    e.sem_inc(s_ev1.h, 1)
                    s_ev1.v += 1
                    for grp in self.fo_groups(gsz):
                        ngrp += 1
                        e.wait_ge(s_mmg.h, ngrp)
                        for fi, fo in enumerate(grp):
                            if pname in ("q", "k"):
                                e.tensor_scalar(elu_a[:], self.bank(fi),
                                                bias2[:, fo:fo + 1], 0.0, ALU.add, ALU.min)
                                e.drain()
                                e.sem_inc(s_elu_z.h, 1)
                                s_elu_z.v += 1
                                e.tensor_scalar(elu_b[:], self.bank(fi),
                                                bias2[:, fo:fo + 1], 0.0, ALU.add, ALU.max)
                                e.drain()
                                self.free_bank(e, fi)
                                e.wait_ge(s_exp.h, s_elu_z.v)
                                e.tensor_tensor(qe_fm[:, fo * TT:(fo + 1) * TT],
                                                elu_b[:], elu_a[:], ALU.add)
                                e.drain()
                            else:
                                idx = tt * DCH + fo
                                par = idx % 2
                                if idx >= 2:
                                    e.wait_ge(s_qest[par].h,
                                              self.b_qest[par] + 32 * (idx // 2))
                                dst_h = vstH[par][:]
                                dst_l = vstL[par][:]
                                e.tensor_scalar(dst_h, self.bank(fi), bias2[:, fo:fo + 1],
                                                None, ALU.add)
                                e.drain()
                                e.scalar_tensor_tensor(dst_l, self.bank(fi),
                                                       bias2[:, fo:fo + 1], dst_h,
                                                       ALU.add, ALU.subtract)
                                e.drain()
                                self.free_bank(e, fi)
                                e.sem_inc(s_elu_z.h, 1)
                                s_elu_z.v += 1
                    e.sem_inc(s_qe.h, 1)
                    s_qe.v += 1
                    if pname in ("q", "k"):
                        for ts_ in range(TSUB):
                            bj = ts_ % 2
                            idx = (pi * NT + tt) * TSUB + ts_ + 1
                            e.wait_ge(s_hi.h, idx)
                            e.tensor_tensor(tmL[:, ts_ * D:(ts_ + 1) * D],
                                            self.psP[bj][:, 0:D],
                                            tmH[:, ts_ * D:(ts_ + 1) * D], ALU.subtract)
                            e.drain()
                            self.free_bank(e, 2 * bj)
                            self.free_bank(e, 2 * bj + 1)
                            e.sem_inc(s_lo2.h, 1)
                            s_lo2.v += 1
                    if pname == "k":
                        e.wait_ge(s_qeld.h, 32 * (tt + 1))
                        for ts_ in range(TSUB):
                            qh_ = qeH[:, ts_ * D:(ts_ + 1) * D]
                            ql_ = qeL[:, ts_ * D:(ts_ + 1) * D]
                            kh_ = tmH[:, ts_ * D:(ts_ + 1) * D]
                            kl_ = tmL[:, ts_ * D:(ts_ + 1) * D]
                            e.tensor_tensor(pk1[:], qh_, kh_, ALU.mult)
                            e.tensor_tensor(pk2[:], qh_, kl_, ALU.mult)
                            e.drain()
                            e.tensor_tensor(pk1[:], pk1[:], pk2[:], ALU.add)
                            e.drain()
                            e.tensor_tensor(pk2[:], ql_, kh_, ALU.mult)
                            e.drain()
                            e.tensor_tensor(pk1[:], pk1[:], pk2[:], ALU.add)
                            e.drain()
                            e.tensor_reduce(qksr[:],
                                            pk1[:].rearrange("p (h d) -> p h d", h=H),
                                            axis=AX.X, op=ALU.add)
                            e.drain()
                            i_sub = tt * TSUB + ts_
                            e.tensor_scalar(self.qks_tm[:, i_sub * H:(i_sub + 1) * H],
                                            qksr[:], EPSQ, None, ALU.add)
                            e.drain()
                        e.sem_inc(s_qefree.h, 1)
                        s_qefree.v += 1
                        e.sem_inc(s_qks.h, 1)
                        s_qks.v += 1
                if pname == "k":
                    e.wait_ge(s_qefree.h, 2 * NT)
                    e.tensor_copy(self.S_sb[:], psS[:])
                    e.drain()
                    self.free_bank(e, 6)
                    self.free_bank(e, 7)
                    e.reciprocal(self.qks_tm[:], self.qks_tm[:])
                    e.drain()
                    e.sem_inc(s_qks.h, 1)
                    s_qks.v += 1
            for r in range(self.TCH // 4 + (1 if self.TCH % 4 else 0)):
                e.wait_ge(s_tr.h, 2 * NT * TSUB + r + 1)
                base = r * 4
                n_here = min(4, self.TCH - base)
                e.tensor_copy(self.rq[:, base * 128:(base + n_here) * 128],
                              self.psP[0][0:H, 0:n_here * 128])
                e.drain()
                self.free_bank(e, 0)
        blk.vector(_v)

        def _a(e):
            nz = 0
            for pi, ph in enumerate(self.attnP):
                pname, gsz = ph[0], ph[7]
                if pname == "v":
                    for tt in range(NT):
                        for fo in range(DCH):
                            idx = tt * DCH + fo
                            par = idx % 2
                            nz += 1
                            e.wait_ge(s_elu_z.h, nz)
                            cs = slice(tt * TT, (tt + 1) * TT)
                            e.dma_start(out=self.h_vh[fo][:, cs], in_=vstH[par][:]
                                        ).then_inc(s_qest[par].h, 16)
                            e.dma_start(out=self.h_vl[fo][:, cs], in_=vstL[par][:]
                                        ).then_inc(s_qest[par].h, 16)
                            s_qest[par].v += 32
                    continue
                for tt in range(NT):
                    for grp in self.fo_groups(gsz):
                        for fi, fo in enumerate(grp):
                            nz += 1
                            e.wait_ge(s_elu_z.h, nz)
                            e.activation(elu_a[:], elu_a[:], AF.Exp)
                            e.drain()
                            e.sem_inc(s_exp.h, 1)
                            s_exp.v += 1
                    for ts_ in range(TSUB):
                        bj = ts_ % 2
                        idx = (pi * NT + tt) * TSUB + ts_ + 1
                        if ts_ == 0:
                            if pname == "q" and tt >= 1:
                                pt = (tt - 1) % 2
                                e.wait_ge(s_qest[pt].h, 32 * ((tt - 1) // 2 + 1))
                            if pname == "k" and tt == 0:
                                e.wait_ge(s_qest[0].h, s_qest[0].v)
                                e.wait_ge(s_qest[1].h, s_qest[1].v)
                        e.wait_ge(s_tr.h, idx)
                        e.activation(tmH[:, ts_ * D:(ts_ + 1) * D],
                                     self.psP[bj][:, 0:D], AF.Copy)
                        e.drain()
                        e.sem_inc(s_hi.h, 1)
                        s_hi.v += 1
                    if pname == "q":
                        e.wait_ge(s_lo2.h, tt * TSUB + TSUB)
                        e.dma_start(out=self.h_qeh[tt], in_=tmH[:]
                                    ).then_inc(s_qest[tt % 2].h, 16)
                        e.dma_start(out=self.h_qel[tt], in_=tmL[:]
                                    ).then_inc(s_qest[tt % 2].h, 16)
                        s_qest[tt % 2].v += 32
        blk.scalar(_a)

        s_sar = self.sem("atsar")
        s_sld = self.sem("atsld")
        s_spl = self.sem("atspl")

        def _sy2(e):
            e.wait_ge(s_qks.h, NT + 1)
            e.dma_start(out=self.cc_Sin[:, :], in_=self.S_sb[:]).then_inc(s_sld.h, 16)
            s_sld.v += 16
        blk.sync(_sy2)

        def _g2(e):
            e.wait_ge(s_sld.h, 16)
            cc = e.collective_compute(
                "AllReduce", ALU.add,
                replica_groups=[[0, 1], [2, 3], [4, 5], [6, 7]],
                ins=[self.cc_Sin[:, :].opt()], outs=[self.cc_Sout[:, :].opt()])
            cc.then_inc(s_sar.h, 1)
            s_sar.v += 1
        blk.gpsimd(_g2)

        def _sy3(e):
            e.wait_ge(s_sar.h, 1)
            for c in range(DCH):
                e.dma_start(out=self.Sp_f[0:DH, c * DH:(c + 1) * DH],
                            in_=self.cc_Sout[:, (2 * c) * DH:(2 * c + 1) * DH]
                            ).then_inc(s_sld.h, 16)
                e.dma_start(out=self.Sp_f[DH:128, c * DH:(c + 1) * DH],
                            in_=self.cc_Sout[:, (2 * c + 1) * DH:(2 * c + 2) * DH]
                            ).then_inc(s_sld.h, 16)
                s_sld.v += 32
        blk.sync(_sy3)

        def _v3(e):
            e.wait_ge(s_sld.h, s_sld.v)
            e.tensor_copy(self.SpH[:], self.Sp_f[:])
            e.drain()
            e.tensor_tensor(self.SpL[:], self.Sp_f[:], self.SpH[:], ALU.subtract)
            e.drain()
            e.sem_inc(s_spl.h, 1)
            s_spl.v += 1
        blk.vector(_v3)

        if self.debug:
            s_d = self.sem("dbg")

            def _dbg(e):
                e.wait_ge(s_spl.h, 1)
                for sq_ in (self.sem("atqestA"), self.sem("atqestB")):
                    e.wait_ge(sq_.h, sq_.v)
                e.dma_start(out=self.d_S[:], in_=self.S_sb[:]).then_inc(s_d.h, 16)
                e.dma_start(out=self.d_rq[:], in_=self.rq[:]).then_inc(s_d.h, 16)
                e.dma_start(out=self.d_xnh[:].rearrange("p (c t) -> c p t", c=DCH),
                            in_=self.h_xnh[:, :, :]).then_inc(s_d.h, 16)
                e.dma_start(out=self.d_xnl[:].rearrange("p (c t) -> c p t", c=DCH),
                            in_=self.h_xnl[:, :, :]).then_inc(s_d.h, 16)
                e.dma_start(out=self.d_vh[:].rearrange("p (c t) -> c p t", c=DCH),
                            in_=self.h_vh[:, :, :]).then_inc(s_d.h, 16)
                e.dma_start(out=self.d_vl[:].rearrange("p (c t) -> c p t", c=DCH),
                            in_=self.h_vl[:, :, :]).then_inc(s_d.h, 16)
                s_d.v += 96
            blk.scalar(_dbg)

    def ph_A6A8(self):
        nc, blk, T, TT, NT, TSUB = self.nc, self.blk, self.T, self.TT, self.NT, self.TSUB
        s_spl = self.sem("atspl")

        s_mm6 = self.sem("a8mm6")
        s_rbc = self.sem("a8rbc")
        s_ev6 = self.sem("a8ev6")
        s_mmg8 = self.sem("a8mmg")
        s_oatt = self.sem("a8oatt")
        s_y = self.sem("a8y")
        s_ytr = self.sem("a8ytr")
        s_yev = self.sem("a8yev")
        s_ym = self.sem("a8ym")
        s_sq2 = self.sem("a8sq2")
        s_sv = self.sem("a8sv")
        s_sd = [self.sem("a8sd0"), self.sem("a8sd1")]

        A = lambda n, s, dt=F32: self.salloc("a68", n, s, dt)
        self.gamma_bc = A("gamma_bc", [128, D])
        self.beta_bc = A("beta_bc", [128, D])
        woRH = A("a8worh", [128, DCH * D], BF16)
        woRL = A("a8worl", [128, DCH * D], BF16)
        s_wo = self.sem("a8wo")
        vqH = [A(f"a8vqh{i}", [128, TT], BF16) for i in range(2)]
        vqL = [A(f"a8vql{i}", [128, TT], BF16) for i in range(2)]
        xnsbH = [A(f"a8xnh{i}", [128, DCH * 128], BF16) for i in range(2)]
        xnsbL = [A(f"a8xnl{i}", [128, DCH * 128], BF16) for i in range(2)]
        s_vld = [self.sem("ppld0"), self.sem("ppld1")]
        b_vld = [s.v for s in s_vld]
        s_qefree = self.sem("atqefree")
        b_qefree = s_qefree.v
        s_xl = [self.sem(f"l1ld{i}") for i in range(2)]
        s_xlf = [self.sem(f"l1ldf{i}") for i in range(2)]
        b_xl = [s.v for s in s_xl]
        b_xlf = [s.v for s in s_xlf]
        s_qest = [self.sem("atqestA"), self.sem("atqestB")]
        s_c = self.sem("consts")
        outH = A("a8outh", [128, DCH * TT], BF16)
        outL = A("a8outl", [128, DCH * TT], BF16)
        out_f = A("a8outf", [128, TT])
        rqbc = A("a8rqbc", [128, TT])
        oatt = A("a8oatt", [128, DCH * TT])
        y_fm = A("a8yfm", [128, DCH * 128])
        y_tm = [A(f"a8ytm{i}", [128, D]) for i in range(2)]
        ym = [A(f"a8ym{i}", [128, D]) for i in range(2)]
        sqb = A("a8sqb", [128, D])
        s_tmv = [A(f"a8stm{i}", [128, D]) for i in range(2)]
        st1 = A("a8st1", [128, 1])
        st2 = [A(f"a8st2{i}", [128, 1]) for i in range(2)]

        def _sync(e):
            e.wait_ge(s_qest[0].h, s_qest[0].v)
            e.wait_ge(s_qest[1].h, s_qest[1].v)
            e.wait_ge(s_spl.h, 1)
            for kc in range(DCH):
                e.dma_start(out=woRH[:, kc * D:(kc + 1) * D], in_=self.h_woh[kc]
                            ).then_inc(s_wo.h, 16)
                e.dma_start(out=woRL[:, kc * D:(kc + 1) * D], in_=self.h_wol[kc]
                            ).then_inc(s_wo.h, 16)
                s_wo.v += 32
            e.dma_start(out=self.gamma_bc[:],
                        in_=self.gamma[:].partition_broadcast(128)).then_inc(s_c.h, 16)
            e.dma_start(out=self.beta_bc[:],
                        in_=self.beta[:].partition_broadcast(128)).then_inc(s_c.h, 16)
            s_c.v += 32
            nvq = 0
            for tt in range(NT):
                for c in range(DCH):
                    idxs = tt * DCH + c
                    e.wait_ge(s_qest[idxs % 2].h,
                              self.b_qest[idxs % 2] + 32 * (idxs // 2 + 1))
                    sl = nvq % 2
                    if nvq >= 2:
                        e.wait_ge(s_qefree.h, b_qefree + nvq - 1)
                    nvq += 1
                    cs = slice(tt * TT, (tt + 1) * TT)
                    e.dma_start(out=vqH[sl][:], in_=self.h_vh[c][:, cs]
                                ).then_inc(s_vld[sl].h, 16)
                    e.dma_start(out=vqL[sl][:], in_=self.h_vl[c][:, cs]
                                ).then_inc(s_vld[sl].h, 16)
                    s_vld[sl].v += 32
                for ts_ in range(TSUB):
                    i = tt * TSUB + ts_
                    sl = i % 2
                    if i >= 2:
                        e.wait_ge(s_xlf[sl].h, b_xlf[sl] + i // 2)
                    colv = slice(i * 128, (i + 1) * 128)
                    e.dma_start(out=xnsbH[sl][:].rearrange("p (c t) -> p c t", c=DCH),
                                in_=self.h_xnh[:, :, colv].transpose([1, 0, 2])
                                ).then_inc(s_xl[sl].h, 16)
                    e.dma_start(out=xnsbL[sl][:].rearrange("p (c t) -> p c t", c=DCH),
                                in_=self.h_xnl[:, :, colv].transpose([1, 0, 2])
                                ).then_inc(s_xl[sl].h, 16)
                    s_xl[sl].v += 32
        blk.sync(_sync)

        def _t(e):
            e.wait_ge(s_spl.h, 1)
            for tt in range(NT):
                for c in range(DCH):
                    bk = c % 2
                    nvq = tt * DCH + c
                    vsl = nvq % 2
                    e.wait_ge(s_vld[vsl].h, b_vld[vsl] + 32 * (nvq // 2 + 1))
                    self.wait_bank(e, bk)
                    for half in range(2):
                        p0, p1 = half * DH, half * DH + DH
                        sh = self.SpH[p0:p1, c * DH:(c + 1) * DH]
                        sl_ = self.SpL[p0:p1, c * DH:(c + 1) * DH]
                        vh = vqH[vsl][p0:p1, :]
                        vl = vqL[vsl][p0:p1, :]
                        po = self.bank(bk)[p0:p1, :]
                        e.matmul(po, sh, vh, start=True, stop=False)
                        e.matmul(po, sh, vl, start=False, stop=False)
                        mm = e.matmul(po, sl_, vh, start=False, stop=True)
                    mm.then_inc(s_qefree.h, 1)
                    s_qefree.v += 1
                    mm2_note = None
                    e.drain()
                    e.sem_inc(s_mm6.h, 1)
                    s_mm6.v += 1
                    self.wait_bank(e, 5)
                    mm = e.matmul(self.bank(5), self.selE[:, c * 128:(c + 1) * 128],
                                  self.rq[:, tt * TT:(tt + 1) * TT], start=True, stop=True)
                    mm.then_inc(s_rbc.h, 1)
                    s_rbc.v += 1
                e.wait_ge(s_ev6.h, tt * DCH + DCH)
                if tt == 0:
                    e.wait_ge(s_wo.h, s_wo.v)
                for fo_g in self.fo_groups(2):
                    for fi, fo in enumerate(fo_g):
                        self.wait_bank(e, 2 + fi)
                    for kc in range(DCH):
                        for fi, fo in enumerate(fo_g):
                            lw_h = woRH[:, kc * D + fo * 128:kc * D + (fo + 1) * 128]
                            lw_l = woRL[:, kc * D + fo * 128:kc * D + (fo + 1) * 128]
                            rx_h = outH[:, kc * TT:(kc + 1) * TT]
                            rx_l = outL[:, kc * TT:(kc + 1) * TT]
                            e.matmul(self.bank(2 + fi), lw_h, rx_h, start=(kc == 0),
                                     stop=False)
                            e.matmul(self.bank(2 + fi), lw_h, rx_l, start=False, stop=False)
                            mm = e.matmul(self.bank(2 + fi), lw_l, rx_h, start=False,
                                          stop=(kc == DCH - 1))
                    mm.then_inc(s_mmg8.h, 1)
                    s_mmg8.v += 1
                for ts_ in range(TSUB):
                    i = tt * TSUB + ts_
                    e.wait_ge(s_y.h, i + 1)
                    self.wait_bank(e, 6)
                    self.wait_bank(e, 7)
                    for c in range(DCH):
                        instr = e.transpose(self.psP[3][:, c * 128:(c + 1) * 128],
                                            y_fm[:, c * 128:(c + 1) * 128], self.ident[:])
                    instr.then_inc(s_ytr.h, 1)
                    s_ytr.v += 1
        blk.tensor(_t)

        def _v(e):
            e.wait_ge(s_c.h, s_c.v)
            for tt in range(NT):
                for c in range(DCH):
                    bk = c % 2
                    e.wait_ge(s_mm6.h, tt * DCH + c + 1)
                    e.wait_ge(s_rbc.h, tt * DCH + c + 1)
                    e.tensor_copy(rqbc[:], self.bank(5))
                    e.drain()
                    self.free_bank(e, 5)
                    e.tensor_tensor(out_f[:], self.bank(bk), rqbc[:], ALU.mult)
                    e.drain()
                    self.free_bank(e, bk)
                    e.tensor_copy(outH[:, c * TT:(c + 1) * TT], out_f[:])
                    e.drain()
                    e.tensor_tensor(outL[:, c * TT:(c + 1) * TT], out_f[:],
                                    outH[:, c * TT:(c + 1) * TT], ALU.subtract)
                    e.drain()
                    e.sem_inc(s_ev6.h, 1)
                    s_ev6.v += 1
                for gi, fo_g in enumerate(self.fo_groups(2)):
                    e.wait_ge(s_mmg8.h, tt * 4 + gi + 1)
                    for fi, fo in enumerate(fo_g):
                        e.tensor_scalar(oatt[:, fo * TT:(fo + 1) * TT], self.bank(2 + fi),
                                        self.biases["bo"][:, fo:fo + 1], None, ALU.add)
                        e.drain()
                        self.free_bank(e, 2 + fi)
                e.sem_inc(s_oatt.h, 1)
                s_oatt.v += 1
                for ts_ in range(TSUB):
                    i = tt * TSUB + ts_
                    par = i % 2
                    if i >= 1:
                        e.wait_ge(s_ytr.h, i)
                    e.wait_ge(s_xl[par].h, b_xl[par] + 32 * (i // 2 + 1))
                    oat_v = oatt[:].rearrange("p (c t) -> p c t", c=DCH)[
                        :, :, ts_ * 128:(ts_ + 1) * 128]
                    yv = y_fm[:].rearrange("p (c t) -> p c t", c=DCH)
                    e.tensor_tensor(yv, oat_v, xnsbH[par][:]
                                    .rearrange("p (c t) -> p c t", c=DCH), ALU.add)
                    e.drain()
                    e.tensor_tensor(yv, yv, xnsbL[par][:]
                                    .rearrange("p (c t) -> p c t", c=DCH), ALU.add)
                    e.drain()
                    e.sem_inc(s_xlf[par].h, 1)
                    s_xlf[par].v += 1
                    e.sem_inc(s_y.h, 1)
                    s_y.v += 1
                    e.wait_ge(s_yev.h, i + 1)
                    e.tensor_reduce(st1[:], y_tm[par][:], axis=AX.X, op=ALU.add)
                    e.drain()
                    e.tensor_scalar(st1[:], st1[:], 1.0 / D, None, ALU.mult)
                    e.drain()
                    if i >= 2:
                        e.wait_ge(s_sq2.h, i - 1)
                    e.tensor_scalar(ym[par][:], y_tm[par][:], st1[:], None, ALU.subtract)
                    e.drain()
                    e.sem_inc(s_ym.h, 1)
                    s_ym.v += 1
                    e.wait_ge(s_sq2.h, i + 1)
                    e.reciprocal(st1[:], st2[par][:])
                    e.drain()
                    if i >= 2:
                        e.wait_ge(s_sd[par].h, 16 * (i // 2))
                    e.scalar_tensor_tensor(s_tmv[par][:], ym[par][:], st1[:],
                                           self.gamma_bc[:], ALU.mult, ALU.mult)
                    e.drain()
                    e.tensor_tensor(s_tmv[par][:], s_tmv[par][:], self.beta_bc[:], ALU.add)
                    e.drain()
                    e.sem_inc(s_sv.h, 1)
                    s_sv.v += 1
        blk.vector(_v)

        def _a(e):
            for tt in range(NT):
                for ts_ in range(TSUB):
                    i = tt * TSUB + ts_
                    par = i % 2
                    e.wait_ge(s_ytr.h, i + 1)
                    if i >= 2:
                        e.wait_ge(s_ym.h, i - 1)
                    e.activation(y_tm[par][:], self.psP[3][:, 0:D], AF.Copy)
                    e.drain()
                    self.free_bank(e, 6)
                    self.free_bank(e, 7)
                    e.sem_inc(s_yev.h, 1)
                    s_yev.v += 1
                    e.wait_ge(s_ym.h, i + 1)
                    if i >= 2:
                        e.wait_ge(s_sv.h, i - 1)
                    e.activation(sqb[:], ym[par][:], AF.Square, accum_out=st2[par][:])
                    e.drain()
                    e.activation(st2[par][:], st2[par][:], AF.Sqrt,
                                 bias=self.c_eps2[:], scale=1.0 / D)
                    e.drain()
                    e.sem_inc(s_sq2.h, 1)
                    s_sq2.v += 1
                    e.wait_ge(s_sv.h, i + 1)
                    e.dma_start(out=self.h_s[i], in_=s_tmv[par][:]
                                ).then_inc(s_sd[par].h, 16)
                    s_sd[par].v += 16
        blk.scalar(_a)

        if self.debug:
            s_d = self.sem("dbg")

            def _dbg(e):
                e.wait_ge(s_sd[0].h, s_sd[0].v)
                e.wait_ge(s_sd[1].h, s_sd[1].v)
                for i in range(self.TCH):
                    e.dma_start(out=self.d_s[i], in_=self.h_s[i]).then_inc(s_d.h, 16)
                    s_d.v += 16
            blk.sync(_dbg)

    def ph_sweep2(self):
        nc, blk, T = self.nc, self.blk, self.T
        s_sc2 = self.sem("scd")
        s_sd = [self.sem("a8sd0"), self.sem("a8sd1")]
        s_ld = [self.sem(f"l1ld{i}") for i in range(2)]
        s_ldf = [self.sem(f"l1ldf{i}") for i in range(2)]
        b_ld = [s.v for s in s_ld]
        b_ldf = [s.v for s in s_ldf]
        s_v = self.sem("s2v")
        s_sig = self.sem("s2sig")
        s_done = self.sem("s2done")
        s_st = self.sem("s2st")
        A = lambda n, s, dt=F32: self.salloc("s23", n, s, dt)
        ubuf = [A(f"s2u{i}", [128, T]) for i in range(2)]
        gbuf = [A(f"s2g{i}", [128, T]) for i in range(2)]
        t1b = A("s2t1", [128, T])
        uq16 = A("s2uq", [128, T], I16)
        g16 = A("s2g16", [128, T])
        gfp = A("s2gfp", [128, T])
        sig = A("s2sig", [128, T])
        sigq = A("s2sigq", [128, T], U16)
        pbuf = A("s2p", [128, T])
        junk = A("s2junk", [128, 1])

        def _sync(e):
            e.wait_ge(s_sc2.h, 2)
            e.wait_ge(s_sd[0].h, 16 * ((self.TCH + 1) // 2))
            e.wait_ge(s_sd[1].h, 16 * (self.TCH // 2))
            if self.debug:
                sdbg = self.sem("dbg")
                e.wait_ge(sdbg.h, sdbg.v)
            for ic in range(ICH):
                sl = ic % 2
                if ic >= 2:
                    e.wait_ge(s_ldf[sl].h, b_ldf[sl] + ic // 2)
                e.dma_start(out=ubuf[sl][:], in_=self.h_u_up[ic]).then_inc(s_ld[sl].h, 16)
                e.dma_start(out=gbuf[sl][:], in_=self.h_u_gate[ic]).then_inc(s_ld[sl].h, 16)
                s_ld[sl].v += 32
        blk.sync(_sync)

        def _v(e):
            for ic in range(ICH):
                sl = ic % 2
                e.wait_ge(s_ld[sl].h, b_ld[sl] + 32 * (ic // 2 + 1))
                e.tensor_scalar(t1b[:], ubuf[sl][:], self.scal["inv_su"][:], RC,
                                ALU.mult, ALU.add)
                e.drain()
                if ic >= 1:
                    e.wait_ge(s_st.h, 32 * ic)
                e.tensor_scalar(uq16[:], t1b[:], RC, None, ALU.subtract)
                e.drain()
                e.tensor_scalar(t1b[:], gbuf[sl][:], self.scal["inv_sg"][:], RC,
                                ALU.mult, ALU.add)
                e.drain()
                e.sem_inc(s_ldf[sl].h, 1)
                s_ldf[sl].v += 1
                e.tensor_scalar(g16[:], t1b[:], RC, None, ALU.subtract)
                e.drain()
                if ic >= 1:
                    e.wait_ge(s_sig.h, ic)
                e.tensor_scalar(gfp[:], g16[:], self.scal["sg"][:], None, ALU.mult)
                e.drain()
                e.sem_inc(s_v.h, 1)
                s_v.v += 1
                e.wait_ge(s_sig.h, ic + 1)
                e.tensor_scalar(t1b[:], sig[:], 65536.0, RC, ALU.mult, ALU.add)
                e.drain()
                e.tensor_scalar(sigq[:], t1b[:], RC, 65535.0, ALU.subtract, ALU.min)
                e.drain()
                e.tensor_tensor(pbuf[:], gfp[:], sigq[:], ALU.mult)
                e.drain()
                e.tensor_scalar(pbuf[:], pbuf[:], 1.0 / 65536.0, None, ALU.mult)
                e.drain()
                e.tensor_reduce(self.amax_b[:], pbuf[:], axis=AX.X, op=ALU.max, apply_absolute_value=True)
                e.drain()
                if ic == 0:
                    e.tensor_copy(self.scal["m_p"][:], self.amax_b[:])
                else:
                    e.tensor_tensor(self.scal["m_p"][:], self.scal["m_p"][:],
                                    self.amax_b[:], ALU.max)
                e.drain()
                e.sem_inc(s_done.h, 1)
                s_done.v += 1
        blk.vector(_v)

        def _a(e):
            for ic in range(ICH):
                e.wait_ge(s_v.h, ic + 1)
                e.activation(sig[:], gfp[:], AF.Sigmoid)
                e.drain()
                e.sem_inc(s_sig.h, 1)
                s_sig.v += 1
                e.wait_ge(s_done.h, ic + 1)
                e.dma_start(out=self.h_uq[ic], in_=uq16[:]).then_inc(s_st.h, 16)
                e.dma_start(out=self.h_p[ic], in_=pbuf[:]).then_inc(s_st.h, 16)
                s_st.v += 32
        blk.scalar(_a)

        s_ardone = self.ar_scalar("g2ar", [self.scal["m_p"][:]])
        s_sc3 = self.sem("scd")

        def _v2(e):
            e.wait_ge(s_ardone.h, s_ardone.v)
            e.tensor_scalar(self.scal["sg2"][:], self.ar_stage[:, 0:1], 1.0 / QMAX,
                            1e-12, ALU.mult, ALU.max)
            e.drain()
            e.reciprocal(self.scal["inv_sg2"][:], self.scal["sg2"][:])
            e.drain()
            e.sem_inc(s_sc3.h, 1)
            s_sc3.v += 1
        blk.vector(_v2)

        if self.debug:
            s_d = self.sem("dbg")

            def _dbg(e):
                e.wait_ge(s_st.h, s_st.v)
                for ic in range(ICH):
                    e.dma_start(out=self.d_uq[ic], in_=self.h_uq[ic]).then_inc(s_d.h, 16)
                    e.dma_start(out=self.d_p[ic], in_=self.h_p[ic]).then_inc(s_d.h, 16)
                    s_d.v += 32
            blk.scalar(_dbg)

    def ph_sweep3_and_wdprep(self):
        nc, blk, T = self.nc, self.blk, self.T
        s_sc3 = self.sem("scd")
        s_st2 = self.sem("s2st")
        s_ld = [self.sem(f"ppld{i}") for i in range(2)]
        s_ldf = [self.sem(f"ppldf{i}") for i in range(2)]
        b_ld = [s.v for s in s_ld]
        b_ldf = [s.v for s in s_ldf]
        s_v = self.sem("s3v")
        s_st = self.sem("s3st")
        A = lambda n, s, dt=F32: self.salloc("s23", n, s, dt)
        p3 = [A(f"s3p{i}", [128, T]) for i in range(2)]
        uq3 = [A(f"s3uq{i}", [128, T], I16) for i in range(2)]
        g2b = A("s3g2", [128, T])
        pob = A("s3po", [128, T])
        junk = A("s3junk", [128, 1])

        s_dld = [self.sem(f"wdld{i}") for i in range(2)]
        s_dldf = [self.sem(f"wdldf{i}") for i in range(2)]
        s_dg = self.sem("wdg")
        s_dst = [self.sem("wdst0"), self.sem("wdst1")]
        b_dld = [s.v for s in s_dld]
        b_dldf = [s.v for s in s_dldf]
        b_dg = s_dg.v
        b_dst = [s.v for s in s_dst]
        dwin = [A(f"wdin{i}", [128, D]) for i in range(2)]
        dbm = A("wdbm", [128, 32])
        dsb = [A(f"wds{i}", [128, 32]) for i in range(2)]
        drb = [A(f"wdr{i}", [128, 32]) for i in range(2)]
        dtq = A("wdt", [128, D])
        ddq = A("wddq", [128, D])
        dwh = [A(f"wdh{i}", [128, D], BF16) for i in range(2)]
        dwl = [A(f"wdl{i}", [128, D], BF16) for i in range(2)]
        djobs = [(c, s3) for c in range(DCH) for s3 in range(3)]
        self.n_djobs = len(djobs)

        def _sync(e):
            e.wait_ge(s_sc3.h, 3)
            for ic in range(ICH):
                sl = ic % 2
                e.wait_ge(s_st2.h, 32 * (ic + 1))
                if ic >= 2:
                    e.wait_ge(s_ldf[sl].h, b_ldf[sl] + ic // 2)
                e.dma_start(out=p3[sl][:], in_=self.h_p[ic]).then_inc(s_ld[sl].h, 16)
                e.dma_start(out=uq3[sl][:], in_=self.h_uq[ic]).then_inc(s_ld[sl].h, 16)
                s_ld[sl].v += 32
                c, s3 = djobs[ic]
                if ic >= 2:
                    e.wait_ge(s_dldf[sl].h, b_dldf[sl] + ic // 2)
                e.dma_start(out=dwin[sl][:],
                            in_=self.w_down[c * 128:(c + 1) * 128, s3 * D:(s3 + 1) * D]
                            ).then_inc(s_dld[sl].h, 16)
                s_dld[sl].v += 16
        blk.sync(_sync)

        def _v(e):
            for ic in range(ICH):
                sl = ic % 2
                e.wait_ge(s_ld[sl].h, b_ld[sl] + 32 * (ic // 2 + 1))
                e.tensor_scalar(g2b[:], p3[sl][:], self.scal["inv_sg2"][:], RC,
                                ALU.mult, ALU.add)
                e.drain()
                e.tensor_scalar(g2b[:], g2b[:], RC, None, ALU.subtract)
                e.drain()
                if ic >= 1:
                    e.wait_ge(s_st.h, 16 * ic)
                e.tensor_tensor(pob[:], g2b[:], uq3[sl][:], ALU.mult)
                e.drain()
                e.sem_inc(s_ldf[sl].h, 1)
                s_ldf[sl].v += 1
                e.tensor_reduce(self.amax_b[:], pob[:], axis=AX.X, op=ALU.max, apply_absolute_value=True)
                e.drain()
                if ic == 0:
                    e.tensor_copy(self.scal["m_po"][:], self.amax_b[:])
                else:
                    e.tensor_tensor(self.scal["m_po"][:], self.scal["m_po"][:],
                                    self.amax_b[:], ALU.max)
                e.drain()
                e.sem_inc(s_v.h, 1)
                s_v.v += 1
                e.wait_ge(s_dld[sl].h, b_dld[sl] + 16 * (ic // 2 + 1))
                e.tensor_reduce(dbm[:], dwin[sl][:].rearrange("p (a b) -> p a b", b=32),
                                axis=AX.X, op=ALU.max, apply_absolute_value=True)
                e.drain()
                e.tensor_scalar(dsb[sl][:], dbm[:], 1.0 / 7.0, 1e-12,
                                ALU.mult, ALU.max)
                e.drain()
                e.reciprocal(drb[sl][:], dsb[sl][:])
                e.drain()
                e.tensor_tensor(dtq[:], dwin[sl][:],
                                drb[sl][:].rearrange("p (a b) -> p a b", b=1)
                                .broadcast_to([128, 32, 32]), ALU.mult)
                e.drain()
                e.sem_inc(s_dldf[sl].h, 1)
                s_dldf[sl].v += 1
                e.tensor_scalar(dtq[:], dtq[:], RC, RC, ALU.add, ALU.subtract)
                e.drain()
                e.tensor_tensor(ddq[:], dtq[:],
                                dsb[sl][:].rearrange("p (a b) -> p a b", b=1)
                                .broadcast_to([128, 32, 32]), ALU.mult)
                e.drain()
                if ic >= 2:
                    e.wait_ge(s_dst[sl].h, b_dst[sl] + 32 * (ic // 2))
                e.tensor_copy(dwh[sl][:], ddq[:])
                e.drain()
                e.tensor_tensor(dwl[sl][:], ddq[:], dwh[sl][:], ALU.subtract)
                e.drain()
                e.sem_inc(s_dg.h, 1)
                s_dg.v += 1
        blk.vector(_v)

        def _a(e):
            for ic in range(ICH):
                e.wait_ge(s_v.h, ic + 1)
                e.dma_start(out=self.h_po[ic], in_=pob[:]).then_inc(s_st.h, 16)
                s_st.v += 16
                c, s3 = djobs[ic]
                sl = ic % 2
                e.wait_ge(s_dg.h, b_dg + ic + 1)
                e.dma_start(out=self.h_wdnh[c * 128:(c + 1) * 128, s3 * D:(s3 + 1) * D],
                            in_=dwh[sl][:]).then_inc(s_dst[sl].h, 16)
                e.dma_start(out=self.h_wdnl[c * 128:(c + 1) * 128, s3 * D:(s3 + 1) * D],
                            in_=dwl[sl][:]).then_inc(s_dst[sl].h, 16)
                s_dst[sl].v += 32
        blk.scalar(_a)

        s_ardone = self.ar_scalar("oar", [self.scal["m_po"][:]])
        s_sc4 = self.sem("scd")

        def _v2(e):
            e.wait_ge(s_ardone.h, s_ardone.v)
            e.tensor_scalar(self.scal["tmp0"][:], self.ar_stage[:, 0:1],
                            self.scal["sg2"][:], self.scal["su"][:], ALU.mult, ALU.mult)
            e.drain()
            e.tensor_scalar(self.scal["so"][:], self.scal["tmp0"][:], 1.0 / QMAX,
                            1e-12, ALU.mult, ALU.max)
            e.drain()
            e.reciprocal(self.scal["inv_so"][:], self.scal["so"][:])
            e.drain()
            e.tensor_scalar(self.scal["c3"][:], self.scal["inv_so"][:],
                            self.scal["sg2"][:], self.scal["su"][:], ALU.mult, ALU.mult)
            e.drain()
            e.sem_inc(s_sc4.h, 1)
            s_sc4.v += 1
        blk.vector(_v2)

        if self.debug:
            s_d = self.sem("dbg")

            def _dbg(e):
                e.wait_ge(s_st.h, s_st.v)
                for ic in range(ICH):
                    e.dma_start(out=self.d_po[ic], in_=self.h_po[ic]).then_inc(s_d.h, 16)
                    s_d.v += 16
            blk.scalar(_dbg)

    def ph_sweep4(self):
        nc, blk, T, TT, NT, TSUB = self.nc, self.blk, self.T, self.TT, self.NT, self.TSUB
        s_sc4 = self.sem("scd")
        s_dst = [self.sem("wdst0"), self.sem("wdst1")]
        s_st3 = self.sem("s3st")
        s_sd = [self.sem("a8sd0"), self.sem("a8sd1")]

        s_wd = self.sem("s4wd")
        s_po = [self.sem(f"l1ld{i}") for i in range(2)]
        s_pof = [self.sem(f"l1ldf{i}") for i in range(2)]
        b_po = [s.v for s in s_po]
        b_pof = [s.v for s in s_pof]
        s_oi = self.sem("l1xn")
        b_oi = s_oi.v
        s_mm = self.sem("s4mm")
        s_act = self.sem("l1sq")
        b_act = s_act.v
        s_sxld = [self.sem("s4sxA"), self.sem("s4sxB")]
        s_sgv = self.sem("s4sgv")
        s_tr = self.sem("s4tr")
        s_out = [self.sem("s4outA"), self.sem("s4outB")]

        A = lambda n, s, dt=F32: self.salloc("s4", n, s, dt)
        wdTH = A("s4wdh", [128, ICH * D], BF16)
        wdTL = A("s4wdl", [128, ICH * D], BF16)
        pobuf = [A(f"s4pob{i}", [128, TT]) for i in range(2)]
        oiH = A("s4oih", [128, ICH * TT], BF16)
        oiL = A("s4oil", [128, ICH * TT], BF16)
        oi_f = A("s4oif", [128, TT])
        gbuf = A("s4gb", [128, DCH * TT])
        s_tmb = [A(f"s4stm{i}", [128, D]) for i in range(2)]
        x_tmb = [A(f"s4xtm{i}", [128, D]) for i in range(2)]

        def _sync(e):
            e.wait_ge(s_dst[0].h, s_dst[0].v)
            e.wait_ge(s_dst[1].h, s_dst[1].v)
            e.wait_ge(s_st3.h, 16 * ICH)
            if self.debug:
                sdbg = self.sem("dbg")
                e.wait_ge(sdbg.h, sdbg.v)
            for ic in range(ICH):
                e.dma_start(out=wdTH[:, ic * D:(ic + 1) * D],
                            in_=self.h_wdnh[:, ic * 128:(ic + 1) * 128],
                            transpose=True).then_inc(s_wd.h, 16)
                e.dma_start(out=wdTL[:, ic * D:(ic + 1) * D],
                            in_=self.h_wdnl[:, ic * 128:(ic + 1) * 128],
                            transpose=True).then_inc(s_wd.h, 16)
                s_wd.v += 32
            e.wait_ge(s_sc4.h, 4)
            for tt in range(NT):
                for ic in range(ICH):
                    sl = ic % 2
                    j = tt * ICH + ic
                    if j >= 2:
                        e.wait_ge(s_pof[sl].h, b_pof[sl] + j // 2)
                    e.dma_start(out=pobuf[sl][:],
                                in_=self.h_po[ic, :, tt * TT:(tt + 1) * TT]
                                ).then_inc(s_po[sl].h, 16)
                    s_po[sl].v += 16
                for ts_ in range(TSUB):
                    i = tt * TSUB + ts_
                    par = i % 2
                    e.wait_ge(s_sd[par].h, 16 * (i // 2 + 1))
                    if i >= 2:
                        e.wait_ge(s_sgv.h, i - 1)
                        e.wait_ge(s_out[par].h, 16 * (i // 2))
                    e.dma_start(out=s_tmb[par][:], in_=self.h_s[i]
                                ).then_inc(s_sxld[par].h, 16)
                    e.dma_start(out=x_tmb[par][:],
                                in_=self.x_in[i * 128:(i + 1) * 128, :]
                                ).then_inc(s_sxld[par].h, 16)
                    s_sxld[par].v += 32
        blk.sync(_sync)

        def _v(e):
            for tt in range(NT):
                for ic in range(ICH):
                    sl = ic % 2
                    j = tt * ICH + ic
                    e.wait_ge(s_po[sl].h, b_po[sl] + 16 * (j // 2 + 1))
                    e.tensor_scalar(oi_f[:], pobuf[sl][:], self.scal["c3"][:], RC,
                                    ALU.mult, ALU.add)
                    e.drain()
                    e.sem_inc(s_pof[sl].h, 1)
                    s_pof[sl].v += 1
                    e.tensor_scalar(oi_f[:], oi_f[:], RC, None, ALU.subtract)
                    e.drain()
                    if tt >= 1 and ic == 0:
                        e.wait_ge(s_mm.h, tt * DCH)
                    e.tensor_copy(oiH[:, ic * TT:(ic + 1) * TT], oi_f[:])
                    e.drain()
                    e.tensor_tensor(oiL[:, ic * TT:(ic + 1) * TT], oi_f[:],
                                    oiH[:, ic * TT:(ic + 1) * TT], ALU.subtract)
                    e.drain()
                    e.sem_inc(s_oi.h, 1)
                    s_oi.v += 1
                for ts_ in range(TSUB):
                    i = tt * TSUB + ts_
                    par = i % 2
                    e.wait_ge(s_tr.h, i + 1)
                    e.wait_ge(s_sxld[par].h, 32 * (i // 2 + 1))
                    e.tensor_tensor(s_tmb[par][:], self.psP[3][:, 0:D], s_tmb[par][:],
                                    ALU.mult)
                    e.drain()
                    self.free_bank(e, 6)
                    self.free_bank(e, 7)
                    e.tensor_tensor(s_tmb[par][:], s_tmb[par][:], x_tmb[par][:], ALU.add)
                    e.drain()
                    e.sem_inc(s_sgv.h, 1)
                    s_sgv.v += 1
        blk.vector(_v)

        def _t(e):
            e.wait_ge(s_wd.h, 32 * ICH)
            for tt in range(NT):
                for fo in range(DCH):
                    bk = fo % 2
                    e.wait_ge(s_oi.h, b_oi + tt * ICH + ICH)
                    self.wait_bank(e, bk)
                    first = True
                    for ic in range(ICH):
                        lw_h = wdTH[:, ic * D + fo * 128: ic * D + (fo + 1) * 128]
                        lw_l = wdTL[:, ic * D + fo * 128: ic * D + (fo + 1) * 128]
                        rx_h = oiH[:, ic * TT:(ic + 1) * TT]
                        rx_l = oiL[:, ic * TT:(ic + 1) * TT]
                        e.matmul(self.bank(bk), lw_h, rx_h, start=first, stop=False)
                        first = False
                        e.matmul(self.bank(bk), lw_h, rx_l, start=False, stop=False)
                        mm = e.matmul(self.bank(bk), lw_l, rx_h, start=False,
                                      stop=(ic == ICH - 1))
                    mm.then_inc(s_mm.h, 1)
                    s_mm.v += 1
                for ts_ in range(TSUB):
                    e.wait_ge(s_act.h, b_act + tt * DCH + DCH)
                    self.wait_bank(e, 6)
                    self.wait_bank(e, 7)
                    for c in range(DCH):
                        instr = e.transpose(
                            self.psP[3][:, c * 128:(c + 1) * 128],
                            gbuf[:, c * TT + ts_ * 128: c * TT + (ts_ + 1) * 128],
                            self.ident[:])
                    instr.then_inc(s_tr.h, 1)
                    s_tr.v += 1
        blk.tensor(_t)

        def _a(e):
            for tt in range(NT):
                for fo in range(DCH):
                    bk = fo % 2
                    e.wait_ge(s_mm.h, tt * DCH + fo + 1)
                    if tt >= 1 and fo == 0:
                        e.wait_ge(s_tr.h, tt * TSUB)
                    e.activation(gbuf[:, fo * TT:(fo + 1) * TT], self.bank(bk), AF.Silu,
                                 scale=self.scal["so"][:])
                    e.drain()
                    self.free_bank(e, bk)
                    e.sem_inc(s_act.h, 1)
                    s_act.v += 1
                for ts_ in range(TSUB):
                    i = tt * TSUB + ts_
                    par = i % 2
                    e.wait_ge(s_sgv.h, i + 1)
                    e.dma_start(out=self.out[i * 128:(i + 1) * 128, :],
                                in_=s_tmb[par][:]).then_inc(s_out[par].h, 16)
                    s_out[par].v += 16
        blk.scalar(_a)

    def ph_final_wait(self):
        blk = self.blk
        s_out = [self.sem("s4outA"), self.sem("s4outB")]

        def _sync(e):
            e.wait_ge(s_out[0].h, s_out[0].v)
            e.wait_ge(s_out[1].h, s_out[1].v)
        blk.sync(_sync)

        if self.debug:
            s_fd = self.sem("dbg")

            def _a(e):
                e.wait_ge(s_out[0].h, s_out[0].v)
                e.wait_ge(s_out[1].h, s_out[1].v)
                for j, nm in enumerate(["sx", "su", "sg", "sg2", "so"]):
                    e.dma_start(out=self.d_scal[:, j:j + 1], in_=self.scal[nm][:],
                                allow_slow_non_contiguous=True).then_inc(s_fd.h, 16)
                    s_fd.v += 16
                e.wait_ge(s_fd.h, s_fd.v)
            blk.scalar(_a)


def _make_consts():
    ident = np.eye(128, dtype=np.float32)
    selE = np.zeros((DCH, H, 128), np.float32)
    for c in range(DCH):
        selE[c, 2 * c, 0:64] = 1.0
        selE[c, 2 * c + 1, 64:128] = 1.0
    return ident, selE


def _in_maps(inputs, T):
    ident, selE = _make_consts()
    x = np.ascontiguousarray(np.asarray(inputs["L0_attn_x"], np.float32))
    base = {
        "Wq": inputs["L0_attn_Wq"], "Wk": inputs["L0_attn_Wk"], "Wv": inputs["L0_attn_Wv"],
        "Wq2": inputs["L0_attn_Wq2"], "Wk2": inputs["L0_attn_Wk2"],
        "Wv2": inputs["L0_attn_Wv2"], "Wo": inputs["L0_attn_Wo"],
        "bq": inputs["L0_attn_bq"], "bk": inputs["L0_attn_bk"], "bv": inputs["L0_attn_bv"],
        "bq2": inputs["L0_attn_bq2"], "bk2": inputs["L0_attn_bk2"],
        "bv2": inputs["L0_attn_bv2"], "bo": inputs["L0_attn_bo"],
        "gamma": inputs["L0_attn_gamma"], "beta": inputs["L0_attn_beta"],
        "w_gate": inputs["L0_ffn_w_gate"], "w_up": inputs["L0_ffn_w_up"],
        "w_down": inputs["L0_ffn_w_down"],
        "ident": ident, "selE": selE,
    }
    base = {k: np.ascontiguousarray(np.asarray(v, np.float32)) for k, v in base.items()}
    maps = []
    for c in range(NCORE):
        m = dict(base)
        m["x"] = np.ascontiguousarray(x[c // 2, (c % 2) * T:(c % 2 + 1) * T, :])
        maps.append(m)
    return maps


def kernel(**inputs):
    T, TT = 2048, 512
    b = Builder(T=T, TT=TT, debug=False)
    nc = b.build()
    in_maps = _in_maps(inputs, T)
    res = run_bass_kernel_spmd(nc, in_maps, core_ids=list(range(NCORE)))
    out = np.empty((4, 4096, D), np.float32)
    for c in range(NCORE):
        out[c // 2, (c % 2) * T:(c % 2 + 1) * T, :] = np.asarray(res.results[c]["out"])
    return out
